# version 02
# baseline (speedup 1.0000x reference)
import math
import numpy as np
import ml_dtypes
from contextlib import ExitStack

import concourse.bass as bass
import concourse.mybir as mybir
from concourse.bass_utils import run_bass_kernel_spmd

F32 = mybir.dt.float32
BF16 = mybir.dt.bfloat16
AF = mybir.ActivationFunctionType
ALU = mybir.AluOpType
AX = mybir.AxisListType.X
BF = ml_dtypes.bfloat16

NCORES = 8
NSEQ = 4
T = 2048
D = 1024
TOK = NSEQ * T
NE = D // 128
NG = TOK // 512
NT = TOK // 128
HEADS = 16
HD = 64
NUM_BUCKETS = 32
MAX_DISTANCE = 128
SCALE = (D // HEADS) ** (-0.5)
NEG = -1e30
B_DBG = False
B_NOBIAS = False


def _bucket(d):
    max_exact = NUM_BUCKETS // 2
    if d < max_exact:
        return d
    v = max_exact + int(
        math.log(d / max_exact) / math.log(MAX_DISTANCE / max_exact)
        * (NUM_BUCKETS - max_exact))
    return min(v, NUM_BUCKETS - 1)


def host_consts():
    c = {}
    c["ident_bf"] = np.eye(128, dtype=BF)
    c["ident_f32"] = np.eye(128, dtype=np.float32)
    c["ones_f32"] = np.ones((1, 128), dtype=np.float32)
    A = np.zeros((NUM_BUCKETS, 128 * 256), dtype=np.float32)
    for p in range(128):
        for ip in range(256):
            d = ip - p
            if d < 0:
                continue
            pos = p * 256 + ip
            A[_bucket(d), pos] += 1.0
            A[NUM_BUCKETS - 1, pos] -= 1.0
    c["selmat"] = A.astype(BF)
    mb = np.zeros((128, 256), dtype=np.float32)
    for p in range(128):
        mb[p, :p] = NEG
    c["maskband"] = mb
    return c


class Sems:
    def __init__(self, nc, prefix):
        self.nc = nc
        self.prefix = prefix
        self.h = {}

    def __call__(self, name):
        if name not in self.h:
            self.h[name] = self.nc.alloc_semaphore(self.prefix + "_" + name)
        return self.h[name]

    def release(self):
        if self.h:
            self.nc.clear_and_free_semaphores(list(self.h.values()))
            self.nc.all_engine_barrier()
            self.h = {}

class DmaQ:

    def __init__(self, S, name, nsem=2):
        self.sems = [S(f"{name}{i}") for i in range(nsem)]
        self.P = nsem
        self.n = 0

    def inc(self, instr):
        instr.then_inc(self.sems[self.n % self.P], 16)
        self.n += 1

    def wait(self, eng, k):
        eng.wait_ge(self.sems[k % self.P], 16 * (k // self.P + 1))



def emit_load_wT(nc, es, w_drams, tag, ident_bf):
    outs = [es.enter_context(nc.sbuf_tensor(f"{tag}_wt{i}", [128, NE * D], BF16))
            for i in range(len(w_drams))]
    les = ExitStack()
    stage = [les.enter_context(nc.sbuf_tensor(f"{tag}_wstg{i}", [128, D], BF16))
             for i in range(2)]
    ps = [les.enter_context(nc.psum_tensor(f"{tag}_wps{i}", [128, 1024], BF16))
          for i in range(2)]
    S = Sems(nc, tag + "_lw")
    NW = len(w_drams)
    qW = DmaQ(S, "dma", 2)
    with les, nc.Block() as blk:
        @blk.gpsimd
        def _(g):
            n = 0
            for wi in range(NW):
                for fi in range(NE):
                    n += 1
                    if n > 2:
                        g.wait_ge(S("pe"), 8 * (n - 2))
                    qW.inc(g.dma_start(stage[n % 2][:, :],
                                       w_drams[wi][fi * 128:(fi + 1) * 128, :]))

        @blk.tensor
        def _(t):
            n = 0
            for wi in range(NW):
                for fi in range(NE):
                    n += 1
                    qW.wait(t, n - 1)
                    for ei in range(NE):
                        k = (n - 1) * 8 + ei
                        if k >= 2:
                            t.wait_ge(S("se"), k - 1)
                        t.transpose(ps[k % 2][:, 0:128],
                                    stage[n % 2][:, ei * 128:(ei + 1) * 128],
                                    ident_bf[:, :]).then_inc(S("pe"), 1)

        @blk.scalar
        def _(s):
            n = 0
            for wi in range(NW):
                for fi in range(NE):
                    n += 1
                    for ei in range(NE):
                        k = (n - 1) * 8 + ei
                        s.wait_ge(S("pe"), k + 1)
                        s.copy(outs[wi][:, ei * D + fi * 128: ei * D + (fi + 1) * 128],
                               ps[k % 2][:, 0:128]).then_inc(S("se"), 1)
    S.release()
    return outs


def emit_load_bias_cols(nc, es, b_drams, tag, ident_f32):
    outs = [es.enter_context(nc.sbuf_tensor(f"{tag}_bc{i}", [128, NE], F32))
            for i in range(len(b_drams))]
    les = ExitStack()
    stage = [les.enter_context(nc.sbuf_tensor(f"{tag}_bstg{i}", [NE, 128], F32))
             for i in range(len(b_drams))]
    ps = les.enter_context(nc.psum_tensor(f"{tag}_bps", [128, 512], F32))
    S = Sems(nc, tag + "_lb")
    with les, nc.Block() as blk:
        @blk.sync
        def _(sync):
            for i, b in enumerate(b_drams):
                sync.dma_start(stage[i][:, :], b[:, :]).then_inc(S("dma"), 16)

        @blk.tensor
        def _(t):
            t.wait_ge(S("dma"), 16 * len(b_drams))
            for i in range(len(b_drams)):
                if i >= 1:
                    t.wait_ge(S("se"), i)
                t.transpose(ps[:, 0:NE], stage[i][:, :],
                            ident_f32[0:NE, 0:NE]).then_inc(S("pe"), 1)

        @blk.scalar
        def _(s):
            for i in range(len(b_drams)):
                s.wait_ge(S("pe"), i + 1)
                s.copy(outs[i][:, :], ps[:, 0:NE]).then_inc(S("se"), 1)
    S.release()
    return outs


def emit_phase_a2(nc, es, Dr, Pc):
    with ExitStack() as wes:
        wts = emit_load_wT(nc, wes, [Dr["wq0"], Dr["wk0"], Dr["wv0"]], "aw",
                           Pc["ident_bf"])
        bcols = emit_load_bias_cols(nc, wes, [Dr["bq0"], Dr["bk0"], Dr["bv0"]],
                                    "ab", Pc["ident_f32"])
        xin = [wes.enter_context(nc.sbuf_tensor(f"a_xin{i}", [128, D], F32))
               for i in range(2)]
        xsq = wes.enter_context(nc.sbuf_tensor("a_xsq", [128, D], F32))
        cols = wes.enter_context(nc.sbuf_tensor("a_cols", [128, 8], F32))
        xlnb = wes.enter_context(nc.sbuf_tensor("a_xlnb", [128, D], BF16))
        xfm = wes.enter_context(nc.sbuf_tensor("a_xfm", [128, NE * 512], BF16))
        qt = [wes.enter_context(nc.sbuf_tensor(f"a_qt{i}", [128, 512], BF16))
              for i in range(2)]
        vfm = wes.enter_context(nc.sbuf_tensor("a_vfm", [128, NE * 512], BF16))
        vtm = [wes.enter_context(nc.sbuf_tensor(f"a_vtm{i}", [128, D], BF16))
               for i in range(2)]
        pst = [wes.enter_context(nc.psum_tensor(f"a_pst{i}", [128, 1024], BF16))
               for i in range(2)]
        pstv = [wes.enter_context(nc.psum_tensor(f"a_pstv{i}", [128, 1024], BF16))
                for i in range(2)]
        psq = [wes.enter_context(nc.psum_tensor(f"a_psq{i}", [128, 512], F32))
               for i in range(2)]
        S = Sems(nc, "a")
        qA = DmaQ(S, "dA", 2)
        qQ = DmaQ(S, "dQ", 2)
        qV = DmaQ(S, "dV", 2)

        with nc.Block() as blk:
            @blk.sync
            def _(sync):
                for g in range(NG):
                    sq = g // 4
                    tg0 = (g % 4) * 512
                    for k in range(4):
                        tt = g * 4 + k
                        if tt >= 2:
                            sync.wait_ge(S("vD"), tt - 1)
                        t0 = tt * 128
                        qA.inc(sync.dma_start(xin[tt % 2][:, :],
                                              Dr["x"][t0:t0 + 128, :]))
                    for wi in range(2):
                        for fi in range(NE):
                            sync.wait_ge(S("sEv"), g * 24 + wi * 8 + fi + 1)
                            m = g * 16 + wi * 8 + fi + 1
                            dst = Dr["qf0"] if wi == 0 else Dr["kf0"]
                            qQ.inc(sync.dma_start(
                                dst[sq, fi * 128:(fi + 1) * 128, tg0:tg0 + 512],
                                qt[m % 2][:, :]))
                    for k2 in range(4):
                        sync.wait_ge(S("sV2"), g * 32 + k2 * 8 + 8)
                        n = g * 4 + k2 + 1
                        qV.inc(sync.dma_start(
                            Dr["vt0"][sq, tg0 + k2 * 128: tg0 + (k2 + 1) * 128, :],
                            vtm[n % 2][:, :]))
                qQ.wait(sync, NG * 16 - 1)
                qQ.wait(sync, NG * 16 - 2)
                qV.wait(sync, NG * 4 - 1)
                qV.wait(sync, NG * 4 - 2)

            @blk.scalar
            def _(s):
                for g in range(NG):
                    for k in range(4):
                        tt = g * 4 + k
                        qA.wait(s, tt)
                        if tt >= 1:
                            s.wait_ge(S("lnvA"), 2 * tt)
                        s.activation(xsq[:, :], xin[tt % 2][:, :],
                                     AF.Square).then_inc(S("lnsA"), 1)
                        s.wait_ge(S("lnvB"), tt + 1)
                        s.activation(cols[:, 6:7], cols[:, 5:6], AF.Sqrt, scale=-1.0,
                                     bias=Pc["c1e5"]).then_inc(S("lnsB"), 1)
                        for ei in range(NE):
                            kk = tt * 8 + ei
                            s.wait_ge(S("pA"), kk + 1)
                            if g >= 1 and k == 0 and ei == 0:
                                s.wait_ge(S("pB"), 24 * g)
                            s.copy(xfm[:, ei * 512 + k * 128: ei * 512 + (k + 1) * 128],
                                   pst[kk % 2][:, 0:128]).then_inc(S("sC"), 1)
                    for wi in range(3):
                        for fi in range(NE):
                            n = g * 24 + wi * 8 + fi + 1
                            s.wait_ge(S("pB"), n)
                            if wi < 2:
                                m = g * 16 + wi * 8 + fi + 1
                                if m > 2:
                                    qQ.wait(s, m - 3)
                                s.activation(qt[m % 2][:, :], psq[n % 2][:, :],
                                             AF.Identity, bias=bcols[wi][:, fi:fi + 1]
                                             ).then_inc(S("sEv"), 1)
                            else:
                                if g >= 1 and fi == 0:
                                    s.wait_ge(S("pV"), 32 * g)
                                s.activation(vfm[:, fi * 512:(fi + 1) * 512],
                                             psq[n % 2][:, :], AF.Identity,
                                             bias=bcols[2][:, fi:fi + 1]
                                             ).then_inc(S("sEv"), 1)
                    for k2 in range(4):
                        for fi in range(NE):
                            kk = g * 32 + k2 * 8 + fi
                            s.wait_ge(S("pV"), kk + 1)
                            n = g * 4 + k2 + 1
                            if n > 2 and fi == 0:
                                qV.wait(s, n - 3)
                            s.copy(vtm[n % 2][:, fi * 128:(fi + 1) * 128],
                                   pstv[kk % 2][:, 0:128]).then_inc(S("sV2"), 1)

            @blk.vector
            def _(v):
                for g in range(NG):
                    for k in range(4):
                        tt = g * 4 + k
                        qA.wait(v, tt)
                        v.reduce_sum(cols[:, 0:1], xin[tt % 2][:, :],
                                     axis=AX).then_inc(S("lnvA"), 1)
                        v.wait_ge(S("lnsA"), tt + 1)
                        v.reduce_sum(cols[:, 1:2], xsq[:, :],
                                     axis=AX).then_inc(S("lnvA"), 1)
                        v.wait_ge(S("lnvA"), 2 * (tt + 1))
                        v.tensor_scalar(cols[:, 2:3], cols[:, 0:1], 1.0 / D, None, ALU.mult)
                        v.tensor_scalar(cols[:, 3:4], cols[:, 1:2], 1.0 / D, None,
                                        ALU.mult).then_inc(S("lnvM"), 1)
                        v.wait_ge(S("lnvM"), tt + 1)
                        v.scalar_tensor_tensor(cols[:, 5:6], cols[:, 2:3], cols[:, 2:3],
                                               cols[:, 3:4], ALU.mult,
                                               ALU.subtract).then_inc(S("lnvB"), 1)
                        v.wait_ge(S("lnsB"), tt + 1)
                        v.reciprocal(cols[:, 7:8], cols[:, 6:7]).then_inc(S("lnvC"), 1)
                        v.wait_ge(S("lnvC"), tt + 1)
                        if tt >= 1:
                            v.wait_ge(S("pA"), 8 * tt)
                        v.tensor_scalar(xlnb[:, :], xin[tt % 2][:, :], cols[:, 2:3],
                                        cols[:, 7:8], ALU.subtract,
                                        ALU.mult).then_inc(S("vD"), 1)

            @blk.tensor
            def _(t):
                for g in range(NG):
                    for k in range(4):
                        tt = g * 4 + k
                        t.wait_ge(S("vD"), tt + 1)
                        for ei in range(NE):
                            kk = tt * 8 + ei
                            if kk >= 2:
                                t.wait_ge(S("sC"), kk - 1)
                            t.transpose(pst[kk % 2][:, 0:128],
                                        xlnb[:, ei * 128:(ei + 1) * 128],
                                        Pc["ident_bf"][:, :]).then_inc(S("pA"), 1)
                    t.wait_ge(S("sC"), 32 * (g + 1))
                    for wi in range(3):
                        for fi in range(NE):
                            n = g * 24 + wi * 8 + fi + 1
                            for ei in range(NE):
                                if ei == 0 and n > 2:
                                    t.wait_ge(S("sEv"), n - 2)
                                ins = t.matmul(
                                    psq[n % 2][:, :],
                                    wts[wi][:, ei * D + fi * 128: ei * D + fi * 128 + 128],
                                    xfm[:, ei * 512:(ei + 1) * 512],
                                    start=(ei == 0), stop=(ei == 7))
                                if ei == 7:
                                    ins.then_inc(S("pB"), 1)
                    t.wait_ge(S("sEv"), g * 24 + 24)
                    for k2 in range(4):
                        for fi in range(NE):
                            kk = g * 32 + k2 * 8 + fi
                            if kk >= 2:
                                t.wait_ge(S("sV2"), kk - 1)
                            t.transpose(pstv[kk % 2][:, 0:128],
                                        vfm[:, fi * 512 + k2 * 128: fi * 512 + (k2 + 1) * 128],
                                        Pc["ident_bf"][:, :]).then_inc(S("pV"), 1)
        S.release()



def _b_chunks(jt):
    j0 = jt * 128
    out = []
    for ac in range(j0 // 512, 4):
        lo = max(j0, ac * 512)
        hi = (ac + 1) * 512
        out.append((ac, lo, hi))
    return out


def _b_bias_segs(jt):
    j0 = jt * 128
    segs = []
    end = min(j0 + 256, 2048)
    c0 = j0 // 512
    s1hi = min(end, (c0 + 1) * 512)
    segs.append((c0, j0, s1hi))
    if end > s1hi and c0 + 1 <= 3:
        segs.append((c0 + 1, s1hi, end))
    return segs


_B_NSC = sum(len(_b_chunks(jt)) for jt in range(16))
_B_NBI = sum(len(_b_bias_segs(jt)) for jt in range(16))
_B_NMEM = sum(1 for jt in range(16) if (jt * 128) % 512)


def _b_sc_idx():
    m = {}
    i = 0
    for jt in range(16):
        for (ac, lo, hi) in _b_chunks(jt):
            m[(jt, ac)] = i
            i += 1
    return m


_B_IDX = _b_sc_idx()
_B_CUMAV = [0] * 17
for _jt in range(16):
    _B_CUMAV[_jt + 1] = _B_CUMAV[_jt] + len(_b_chunks(_jt))


def emit_phase_b(nc, es, Dr, Pc):
    with ExitStack() as wes:
        wo_t = emit_load_wT(nc, wes, [Dr["wo0"]], "bw", Pc["ident_bf"])[0]
        kh = [wes.enter_context(nc.sbuf_tensor(f"b_kh{i}", [64, T], BF16))
              for i in range(2)]
        qh = [wes.enter_context(nc.sbuf_tensor(f"b_qh{i}", [64, T], BF16))
              for i in range(2)]
        v_sb = wes.enter_context(nc.sbuf_tensor("b_vsb", [128, 16 * D], BF16))
        v65 = wes.enter_context(nc.sbuf_tensor("b_v65", [128, 16 * 65], BF16))
        A = [wes.enter_context(nc.sbuf_tensor(f"b_A{i}", [128, T], BF16))
             for i in range(2)]
        ao = wes.enter_context(nc.sbuf_tensor("b_ao", [128, 8 * T], BF16))
        ao_tmp = [wes.enter_context(nc.sbuf_tensor(f"b_aot{i}", [64, T], BF16))
                  for i in range(2)]
        tcorr = wes.enter_context(nc.sbuf_tensor("b_tcorr", [128, 16 * 256], F32))
        relf = wes.enter_context(nc.sbuf_tensor("b_relf", [32, 16], F32))
        relb = wes.enter_context(nc.sbuf_tensor("b_relb", [32, 16], BF16))
        sel = [wes.enter_context(nc.sbuf_tensor(f"b_sel{i}", [32, 4096], BF16))
               for i in range(2)]
        tcsb = [wes.enter_context(nc.sbuf_tensor(f"b_tcsb{i}", [16, 512], F32))
                for i in range(2)]
        lrow = wes.enter_context(nc.sbuf_tensor("b_lrow", [1, T], F32))
        lrow2 = wes.enter_context(nc.sbuf_tensor("b_lrow2", [1, T], F32))
        linv = wes.enter_context(nc.sbuf_tensor("b_linv", [1, T], F32))
        bcb = [wes.enter_context(nc.sbuf_tensor(f"b_bcb{i}", [64, 512], F32))
               for i in range(2)]
        bo_row = wes.enter_context(nc.sbuf_tensor("b_bor", [1, D], F32))
        bo_bc = wes.enter_context(nc.sbuf_tensor("b_bobc", [128, D], F32))
        xr = [wes.enter_context(nc.sbuf_tensor(f"b_xr{i}", [128, D], F32))
              for i in range(2)]
        x1t = [wes.enter_context(nc.sbuf_tensor(f"b_x1t{i}", [128, D], F32))
               for i in range(2)]
        ps_sc = [wes.enter_context(nc.psum_tensor(f"b_psc{i}", [128, 512], F32))
                 for i in range(2)]
        av_ps = wes.enter_context(nc.psum_tensor("b_av", [65, T], F32))
        wo_ps = wes.enter_context(nc.psum_tensor("b_wo", [128, D], F32))

        dbg = (wes.enter_context(nc.sbuf_tensor("b_dbg", [128, 2064], F32))
               if B_DBG else None)
        S = Sems(nc, "b")
        qK = DmaQ(S, "dK", 4)
        qVS = DmaQ(S, "dVS", 2)
        qSel = DmaQ(S, "dSel", 2)
        qTc = DmaQ(S, "dTc", 2)
        qTr = DmaQ(S, "dTr", 2)
        qX = DmaQ(S, "dX", 2)
        qR = DmaQ(S, "dR", 2)
        qAo = DmaQ(S, "dAo", 2)
        qMisc = DmaQ(S, "dM", 2)

        NSC, NBI, NMEM = _B_NSC, _B_NBI, _B_NMEM
        IDX, CUMAV = _B_IDX, _B_CUMAV

        with nc.Block() as blk:
            @blk.sync
            def _(sync):
                qMisc.inc(sync.dma_start(relf[:, :], Dr["rel"][:, :]))
                qMisc.inc(sync.dma_start(bo_row[:, :], Dr["bo0"][:, :]))
                for s in range(NSEQ):
                    for h in range(HEADS):
                        hh = s * 16 + h
                        if hh >= 2:
                            sync.wait_ge(S("peS"), NSC * (hh - 1))
                        qK.inc(sync.dma_start(kh[hh % 2][:, :],
                                              Dr["kf0"][s, h * 64:h * 64 + 64, :]))
                        qK.inc(sync.dma_start(qh[hh % 2][:, :],
                                              Dr["qf0"][s, h * 64:h * 64 + 64, :]))
                    for it in range(16):
                        wt = s * 16 + it
                        if wt >= 2:
                            sync.wait_ge(S("gR"), 2 * (wt - 1))
                        t0 = (s * 16 + it) * 128
                        qX.inc(sync.dma_start(xr[wt % 2][:, :],
                                              Dr["x"][t0:t0 + 128, :]))
                        sync.wait_ge(S("gR"), 2 * wt + 2)
                        qR.inc(sync.dma_start(Dr["r1"][t0:t0 + 128, :],
                                              x1t[wt % 2][:, :]))
                if B_DBG:
                    sync.wait_ge(S("vDbg"), 1)
                    sync.dma_start(Dr["dbg"][:, :], dbg[:, :]).then_inc(S("dDbg"), 16)
                    sync.wait_ge(S("dDbg"), 16)
                qR.wait(sync, NSEQ * 16 - 1)
                qR.wait(sync, NSEQ * 16 - 2)

            @blk.gpsimd
            def _(g):
                for c in range(8):
                    if c >= 2:
                        g.wait_ge(S("peTc"), 8 * (c - 1))
                    qSel.inc(g.dma_start(
                        sel[c % 2][:, :],
                        Pc["selmat_dram"][:, c * 4096:(c + 1) * 4096]))
                for s in range(NSEQ):
                    g.wait_ge(S("peA"), NSC * 16 * s)
                    qVS.inc(g.dma_start(
                        v_sb[:, :].rearrange("p (jt f) -> p jt f", f=D),
                        Dr["vt0"][s].rearrange("(jt p) f -> p jt f", p=128)))
                    for h in range(1, HEADS, 2):
                        hh = s * 16 + h
                        oh = hh // 2
                        g.wait_ge(S("vAo"), 4 * (hh + 1))
                        qAo.inc(g.dma_start(
                            ao[64:128, (h // 2) * T:(h // 2 + 1) * T],
                            ao_tmp[oh % 2][:, :]))
                    for it in range(16):
                        wt = s * 16 + it
                        qX.wait(g, wt)
                        for half in range(2):
                            g.wait_ge(S("vW"), 2 * wt + half + 1)
                            g.tensor_tensor(
                                x1t[wt % 2][:, half * 512:(half + 1) * 512],
                                x1t[wt % 2][:, half * 512:(half + 1) * 512],
                                xr[wt % 2][:, half * 512:(half + 1) * 512],
                                ALU.add).then_inc(S("gR"), 1)

            @blk.tensor
            def _(t):
                qMisc.wait(t, 0)
                t.wait_ge(S("vRel"), 1)
                for c in range(64):
                    qSel.wait(t, c // 8)
                    if c >= 1:
                        t.wait_ge(S("sTc"), c)
                    t.matmul(ps_sc[0][0:16, 0:512], relb[:, :],
                             sel[(c // 8) % 2][:, (c % 8) * 512:(c % 8 + 1) * 512],
                             start=True, stop=True).then_inc(S("peTc"), 1)
                qMisc.wait(t, 1)
                t.wait_ge(S("sTc"), 64)
                for half in range(2):
                    t.matmul(ps_sc[1][:, 0:512], Pc["ones_f32"][0:1, :],
                             bo_row[0:1, half * 512:(half + 1) * 512],
                             start=True, stop=True).then_inc(S("peBo"), 1)
                    t.wait_ge(S("sBo"), half + 1)
                t.wait_ge(S("vOne"), 16)
                for s in range(NSEQ):
                    for h in range(HEADS):
                        hh = s * 16 + h
                        qK.wait(t, 2 * hh)
                        qK.wait(t, 2 * hh + 1)
                        if hh >= 1:
                            t.wait_ge(S("sBc"), 4 * hh)
                            t.wait_ge(S("vAo"), 4 * hh)
                        for jt in range(16):
                            j0 = jt * 128
                            for (ac, lo, hi) in _b_chunks(jt):
                                idx = NSC * hh + IDX[(jt, ac)]
                                if idx >= 2:
                                    t.wait_ge(S("sE"), idx - 1)
                                t.matmul(ps_sc[idx % 2][:, 0:hi - lo],
                                         kh[hh % 2][:, j0:j0 + 128],
                                         qh[hh % 2][:, lo:hi],
                                         start=True, stop=True
                                         ).then_inc(S("peS"), 1)
                            for (ac, lo2, hi2) in _b_chunks(jt):
                                eidx = NSC * hh + IDX[(jt, ac)]
                                t.wait_ge(S("sE"), eidx + 1)
                                if (jt * 128) % 512 and ac == jt // 4:
                                    t.wait_ge(S("vM"), NMEM * hh
                                              + sum(1 for j in range(jt + 1)
                                                    if (j * 128) % 512))
                                t.wait_ge(S("vC"), 16 * hh + jt + 1)
                                t.matmul(av_ps[:, ac * 512:(ac + 1) * 512],
                                         v65[:, jt * 65:jt * 65 + 65],
                                         A[jt % 2][:, ac * 512:(ac + 1) * 512],
                                         start=(jt == 0),
                                         stop=(jt == min(15, 4 * ac + 3))
                                         ).then_inc(S("peA"), 1)
                        t.wait_ge(S("sL2"), hh + 1)
                        for c in range(4):
                            t.wait_ge(S("sBc"), 4 * hh + c)
                            t.matmul(ps_sc[0][0:64, 0:512],
                                     Pc["ones_f32"][0:1, 0:64],
                                     linv[0:1, c * 512:(c + 1) * 512],
                                     start=True, stop=True).then_inc(S("peBc"), 1)
                    t.wait_ge(S("vAo"), 64 * (s + 1))
                    qAo.wait(t, 8 * (s + 1) - 1)
                    qAo.wait(t, 8 * (s + 1) - 2)
                    for it in range(16):
                        wt = s * 16 + it
                        if wt >= 1:
                            t.wait_ge(S("vW"), 2 * wt)
                        for half in range(2):
                            for ec in range(NE):
                                ins = t.matmul(
                                    wo_ps[:, half * 512:(half + 1) * 512],
                                    ao[:, ec * T + it * 128: ec * T + (it + 1) * 128],
                                    wo_t[:, ec * D + half * 512: ec * D + (half + 1) * 512],
                                    start=(ec == 0), stop=(ec == 7))
                            ins.then_inc(S("peW"), 1)

            @blk.scalar
            def _(s_):
                for c in range(64):
                    s_.wait_ge(S("peTc"), c + 1)
                    if c >= 2:
                        qTc.wait(s_, c - 2)
                    s_.copy(tcsb[c % 2][:, :], ps_sc[0][0:16, 0:512]
                            ).then_inc(S("sTc"), 1)
                    s_.wait_ge(S("sTc"), c + 1)
                    qTc.inc(s_.dma_start(Dr["tcorr_d"][:, c * 512:(c + 1) * 512],
                                         tcsb[c % 2][:, :]))
                for h in range(16):
                    qTc.wait(s_, 63)
                    qTc.wait(s_, 62)
                    qTr.inc(s_.dma_start(
                        tcorr[:, h * 256:(h + 1) * 256],
                        Dr["tcorr_d"][h].rearrange("(p i) -> p i", p=128)))
                for half in range(2):
                    s_.wait_ge(S("peBo"), half + 1)
                    s_.copy(bo_bc[:, half * 512:(half + 1) * 512],
                            ps_sc[1][:, 0:512]).then_inc(S("sBo"), 1)
                for s in range(NSEQ):
                    for h in range(HEADS):
                        hh = s * 16 + h
                        nbi = 0
                        for jt in range(16):
                            segs = _b_bias_segs(jt)
                            for (ac, lo, hi) in _b_chunks(jt):
                                idx = NSC * hh + IDX[(jt, ac)]
                                s_.wait_ge(S("peS"), idx + 1)
                                has_bias = any(a == ac for (a, _, _) in segs)
                                if has_bias:
                                    nbi += sum(1 for (a, _, _) in segs if a == ac)
                                    s_.wait_ge(S("vB"), NBI * hh + nbi)
                                if jt >= 2 and ac == jt // 4:
                                    s_.wait_ge(S("peA"),
                                               NSC * hh + CUMAV[jt - 1])
                                s_.activation(A[jt % 2][:, lo:hi],
                                              ps_sc[idx % 2][:, 0:hi - lo],
                                              AF.Exp).then_inc(S("sE"), 1)
                        s_.wait_ge(S("peA"), NSC * (hh + 1))
                        s_.activation(lrow[0:1, :], av_ps[64:65, :],
                                      AF.Ln, scale=2.0 ** -44).then_inc(S("sL"), 1)
                        s_.wait_ge(S("sL"), 2 * hh + 1)
                        s_.activation(lrow2[0:1, :], lrow[0:1, :], AF.Relu,
                                      bias=Pc["c44"][0:1, :]).then_inc(S("sL"), 1)
                        s_.wait_ge(S("sL"), 2 * hh + 2)
                        s_.activation(linv[0:1, :], lrow2[0:1, :], AF.Exp,
                                      scale=-1.0,
                                      bias=Pc["cLE"][0:1, :]).then_inc(S("sL2"), 1)
                        for c in range(4):
                            s_.wait_ge(S("peBc"), 4 * hh + c + 1)
                            if 4 * hh + c >= 2:
                                s_.wait_ge(S("vAo"), 4 * hh + c - 1)
                            s_.copy(bcb[c % 2][:, :], ps_sc[0][0:64, 0:512]
                                    ).then_inc(S("sBc"), 1)

            @blk.vector
            def _(v):
                for jt in range(16):
                    v.memset(v65[:, jt * 65 + 64: jt * 65 + 65],
                             1.0).then_inc(S("vOne"), 1)
                qMisc.wait(v, 0)
                v.tensor_scalar(relb[:, :], relf[:, :], SCALE, None,
                                ALU.mult).then_inc(S("vRel"), 1)
                qTr.wait(v, 14)
                qTr.wait(v, 15)
                for h in range(16):
                    if B_NOBIAS:
                        v.tensor_copy(tcorr[:, h * 256:(h + 1) * 256],
                                      Pc["maskband"][:, :]).then_inc(S("vTc"), 1)
                    else:
                        v.tensor_tensor(tcorr[:, h * 256:(h + 1) * 256],
                                        tcorr[:, h * 256:(h + 1) * 256],
                                        Pc["maskband"][:, :], ALU.add
                                        ).then_inc(S("vTc"), 1)
                for s in range(NSEQ):
                    for h in range(HEADS):
                        hh = s * 16 + h
                        if h == 0:
                            qVS.wait(v, s)
                        if hh >= 1:
                            v.wait_ge(S("peA"), NSC * hh)
                        for jt in range(16):
                            v.tensor_copy(v65[:, jt * 65:jt * 65 + 64],
                                          v_sb[:, jt * D + h * 64: jt * D + h * 64 + 64]
                                          ).then_inc(S("vC"), 1)
                        if hh == 0:
                            v.wait_ge(S("vTc"), 16)
                        nbi = 0
                        nmem = 0
                        for jt in range(16):
                            j0 = jt * 128
                            if jt >= 2:
                                v.wait_ge(S("peA"), NSC * hh + CUMAV[jt - 1])
                            if j0 % 512:
                                nmem += 1
                                c0 = j0 // 512
                                v.memset(A[jt % 2][:, c0 * 512:j0], 0.0
                                         ).then_inc(S("vM"), 1)
                            for (ac, blo, bhi) in _b_bias_segs(jt):
                                idx = NSC * hh + IDX[(jt, ac)]
                                clo = max(j0, ac * 512)
                                v.wait_ge(S("peS"), idx + 1)
                                nbi += 1
                                v.tensor_tensor(
                                    ps_sc[idx % 2][:, blo - clo:bhi - clo],
                                    ps_sc[idx % 2][:, blo - clo:bhi - clo],
                                    tcorr[:, h * 256 + blo - j0: h * 256 + bhi - j0],
                                    ALU.add).then_inc(S("vB"), 1)
                        oh = hh // 2
                        for c in range(4):
                            v.wait_ge(S("sBc"), 4 * hh + c + 1)
                            if hh >= 1 and c == 0:
                                v.wait_ge(S("peW"), 32 * s)
                            if h % 2 and c == 0 and oh >= 2:
                                qAo.wait(v, oh - 2)
                            dst = (ao[0:64, (h // 2) * T + c * 512:
                                       (h // 2) * T + (c + 1) * 512]
                                   if h % 2 == 0 else
                                   ao_tmp[oh % 2][:, c * 512:(c + 1) * 512])
                            v.tensor_tensor(
                                dst, av_ps[0:64, c * 512:(c + 1) * 512],
                                bcb[c % 2][:, :], ALU.mult).then_inc(S("vAo"), 1)
                        if hh == 1 and B_DBG:
                            v.tensor_copy(dbg[0:64, 2048:2056], av_ps[0:64, 118:126])
                            v.tensor_copy(dbg[64:65, 2056:2060], av_ps[64:65, 119:123])
                            v.tensor_copy(dbg[0:1, 2060:2064], linv[0:1, 119:123])
                        if hh == 0 and B_DBG:
                            v.tensor_copy(dbg[:, 0:256], tcorr[:, 0:256])
                            v.wait_ge(S("sE"), 4)
                            v.tensor_copy(dbg[:, 256:768], A[0][:, 0:512])
                            v.tensor_copy(dbg[0:64, 768:1280], av_ps[0:64, 0:512])
                            v.tensor_copy(dbg[0:1, 1280:1792], linv[0:1, 0:512])
                            v.tensor_copy(dbg[0:1, 1792:2048], lrow[0:1, 0:256])
                            v.tensor_copy(dbg[64:65, 1280:1281], av_ps[64:65, 0:1]
                                          ).then_inc(S("vDbg"), 1)
                    for it in range(16):
                        wt = s * 16 + it
                        if wt >= 2:
                            qR.wait(v, wt - 2)
                        for half in range(2):
                            v.wait_ge(S("peW"), 2 * wt + half + 1)
                            v.scalar_tensor_tensor(
                                x1t[wt % 2][:, half * 512:(half + 1) * 512],
                                wo_ps[:, half * 512:(half + 1) * 512],
                                1.0, bo_bc[:, half * 512:(half + 1) * 512],
                                ALU.mult, ALU.add).then_inc(S("vW"), 1)
                    if s == 0 and B_DBG:
                        v.tensor_copy(dbg[:, 0:2048], ao[:, 0:2048]
                                      ).then_inc(S("vDbg"), 1)
        S.release()



def emit_phase_c(nc, es, Dr, Pc):
    with ExitStack() as wes:
        rin = [wes.enter_context(nc.sbuf_tensor(f"c_rin{i}", [128, D], F32))
               for i in range(2)]
        xsq = wes.enter_context(nc.sbuf_tensor("c_xsq", [128, D], F32))
        xi = wes.enter_context(nc.sbuf_tensor("c_xi", [128, D], F32))
        t1 = wes.enter_context(nc.sbuf_tensor("c_t1", [128, D], F32))
        t2 = wes.enter_context(nc.sbuf_tensor("c_t2", [128, D], F32))
        xnb = wes.enter_context(nc.sbuf_tensor("c_xnb", [128, D], BF16))
        cols = wes.enter_context(nc.sbuf_tensor("c_cols", [128, 16], F32))
        mx = wes.enter_context(nc.sbuf_tensor("c_mx", [128, NT], F32))
        xfm = wes.enter_context(nc.sbuf_tensor("c_xfm", [128, NE * 512], BF16))
        gb = wes.enter_context(nc.sbuf_tensor("c_gb", [128, D], F32))
        bb = wes.enter_context(nc.sbuf_tensor("c_bb", [128, D], F32))
        gsc = wes.enter_context(nc.sbuf_tensor("c_gsc", [1, 8], F32))
        lrow = wes.enter_context(nc.sbuf_tensor("c_lrow", [1, D], F32))
        wstg = [wes.enter_context(nc.sbuf_tensor(f"c_wstg{i}", [128, D], F32))
                for i in range(2)]
        wqt = [wes.enter_context(nc.sbuf_tensor(f"c_wqt{i}", [128, D], BF16))
               for i in range(2)]
        wcols = wes.enter_context(nc.sbuf_tensor("c_wcols", [128, 8], F32))
        ones128 = wes.enter_context(nc.sbuf_tensor("c_ones", [128, 1], F32))
        pst = [wes.enter_context(nc.psum_tensor(f"c_pst{i}", [128, 1024], BF16))
               for i in range(2)]
        psb = wes.enter_context(nc.psum_tensor("c_psb", [128, 512], F32))
        S = Sems(nc, "c")
        qC = DmaQ(S, "dC", 2)
        qW = DmaQ(S, "dW", 2)
        qN = DmaQ(S, "dN", 2)
        qQn = DmaQ(S, "dQn", 2)
        qRow = DmaQ(S, "dRow", 2)
        qG = DmaQ(S, "dG", 2)

        with nc.Block() as blk:
            @blk.sync
            def _(sync):
                qRow.inc(sync.dma_start(lrow[0:1, :], Dr["lng"][:, :]))
                for g in range(NG):
                    for k in range(4):
                        tt = g * 4 + k
                        if tt >= 2:
                            sync.wait_ge(S("lnvXi"), tt - 1)
                        t0 = tt * 128
                        qC.inc(sync.dma_start(rin[tt % 2][:, :],
                                              Dr["r1"][t0:t0 + 128, :]))
                    sync.wait_ge(S("sC"), 32 * (g + 1))
                    qN.inc(sync.dma_start(
                        Dr["xn1g"][g].rearrange("e p t -> p e t"),
                        xfm[:, :].rearrange("p (e t) -> p e t", t=512)))
                qN.wait(sync, NG - 1)
                qN.wait(sync, NG - 2)

            @blk.gpsimd
            def _(g_):
                g_.memset(ones128[:, :], 1.0)
                for tt in range(NT):
                    g_.wait_ge(S("vT2"), tt + 1)
                    if tt >= 1:
                        g_.wait_ge(S("pA"), 8 * tt)
                        g_.wait_ge(S("vD"), tt)
                    g_.tensor_tensor(xnb[:, :], t2[:, :], bb[:, :],
                                     ALU.add).then_inc(S("gXn"), 1)
                for n in range(16):
                    dc = n % 8
                    if 2 <= n < 10:
                        g_.wait_ge(S("vWc"), n - 1)
                    elif n >= 10:
                        g_.wait_ge(S("vWq"), n - 9)
                    qW.inc(g_.dma_start(wstg[n % 2][:, :],
                                        Dr["wffn"][dc * 128:(dc + 1) * 128, :]))
                g_.wait_ge(S("vBeta"), 1)
                qG.inc(g_.dma_start(Dr["beta_d"][:, :], gsc[0:1, 0:1]))
                g_.wait_ge(S("vG"), 1)
                qG.inc(g_.dma_start(Dr["cc_in"][:, :], gsc[0:1, 2:3]))
                qG.wait(g_, 1)
                g_.collective_compute(
                    "AllReduce", ALU.max,
                    replica_groups=[list(range(NCORES))],
                    ins=[Dr["cc_in"].ap().opt()],
                    outs=[Dr["cc_out"].ap().opt()]).then_inc(S("cc"), 1)
                g_.wait_ge(S("cc"), 1)
                qG.inc(g_.dma_start(Dr["gmax_d"][:, :], Dr["cc_out"][:, :]))
                qG.wait(g_, 2)
                qG.wait(g_, 0)

            @blk.scalar
            def _(s_):
                for half in range(2):
                    s_.wait_ge(S("peGB"), half + 1)
                    s_.copy(gb[:, half * 512:(half + 1) * 512],
                            psb[:, 0:512]).then_inc(S("sGBe"), 1)
                qRow.inc(s_.dma_start(lrow[0:1, :], Dr["lnb"][:, :]))
                for half in range(2):
                    s_.wait_ge(S("peGB"), half + 3)
                    s_.copy(bb[:, half * 512:(half + 1) * 512],
                            psb[:, 0:512]).then_inc(S("sGBe"), 1)
                for tt in range(NT):
                    qC.wait(s_, tt)
                    if tt >= 1:
                        s_.wait_ge(S("lnvA"), 4 * tt)
                    s_.activation(xsq[:, :], rin[tt % 2][:, :],
                                  AF.Square).then_inc(S("lnsA"), 1)
                    s_.wait_ge(S("lnvB"), 2 * tt + 1)
                    s_.activation(cols[:, 6:7], cols[:, 5:6], AF.Sqrt, scale=-1.0,
                                  bias=Pc["c1e5"]).then_inc(S("lnsB"), 1)
                    s_.wait_ge(S("lnvXi"), tt + 1)
                    s_.activation(xsq[:, :], xi[:, :],
                                  AF.Square).then_inc(S("lnsA"), 1)
                    s_.wait_ge(S("lnvB"), 2 * tt + 2)
                    s_.activation(cols[:, 14:15], cols[:, 13:14], AF.Sqrt, scale=-1.0,
                                  bias=Pc["c1e3"]).then_inc(S("lnsB"), 1)
                    for ei in range(NE):
                        kk = tt * 8 + ei
                        s_.wait_ge(S("pA"), kk + 1)
                        if tt % 4 == 0 and ei == 0 and tt >= 4:
                            qN.wait(s_, tt // 4 - 1)
                        k = tt % 4
                        s_.copy(xfm[:, ei * 512 + k * 128: ei * 512 + (k + 1) * 128],
                                pst[kk % 2][:, 0:128]).then_inc(S("sC"), 1)
                s_.wait_ge(S("peWbM"), 1)
                s_.copy(cols[:, 12:13], psb[:, 4:5]).then_inc(S("sWb"), 1)
                for dc in range(8):
                    s_.wait_ge(S("vWq"), dc + 1)
                    qQn.inc(s_.dma_start(Dr["wqn_d"][dc * 128:(dc + 1) * 128, :],
                                         wqt[dc % 2][:, :]))
                qQn.wait(s_, 7)
                qQn.wait(s_, 6)

            @blk.vector
            def _(v):
                for tt in range(NT):
                    qC.wait(v, tt)
                    v.reduce_sum(cols[:, 0:1], rin[tt % 2][:, :],
                                 axis=AX).then_inc(S("lnvA"), 1)
                    v.wait_ge(S("lnsA"), 2 * tt + 1)
                    v.reduce_sum(cols[:, 1:2], xsq[:, :],
                                 axis=AX).then_inc(S("lnvA"), 1)
                    v.wait_ge(S("lnvA"), 4 * tt + 2)
                    v.tensor_scalar(cols[:, 2:3], cols[:, 0:1], 1.0 / D, None, ALU.mult)
                    v.tensor_scalar(cols[:, 3:4], cols[:, 1:2], 1.0 / D, None,
                                    ALU.mult).then_inc(S("lnvM"), 1)
                    v.wait_ge(S("lnvM"), 2 * tt + 1)
                    v.scalar_tensor_tensor(cols[:, 5:6], cols[:, 2:3], cols[:, 2:3],
                                           cols[:, 3:4], ALU.mult,
                                           ALU.subtract).then_inc(S("lnvB"), 1)
                    v.wait_ge(S("lnsB"), 2 * tt + 1)
                    v.reciprocal(cols[:, 7:8], cols[:, 6:7]).then_inc(S("lnvC"), 1)
                    v.wait_ge(S("lnvC"), 2 * tt + 1)
                    v.tensor_scalar(xi[:, :], rin[tt % 2][:, :], cols[:, 2:3],
                                    cols[:, 7:8], ALU.subtract,
                                    ALU.mult).then_inc(S("lnvXi"), 1)
                    v.wait_ge(S("lnvXi"), tt + 1)
                    v.reduce_sum(cols[:, 8:9], xi[:, :], axis=AX).then_inc(S("lnvA"), 1)
                    v.wait_ge(S("lnsA"), 2 * tt + 2)
                    v.reduce_sum(cols[:, 9:10], xsq[:, :],
                                 axis=AX).then_inc(S("lnvA"), 1)
                    v.wait_ge(S("lnvA"), 4 * tt + 4)
                    v.tensor_scalar(cols[:, 10:11], cols[:, 8:9], 1.0 / D, None, ALU.mult)
                    v.tensor_scalar(cols[:, 11:12], cols[:, 9:10], 1.0 / D, None,
                                    ALU.mult).then_inc(S("lnvM"), 1)
                    v.wait_ge(S("lnvM"), 2 * tt + 2)
                    v.scalar_tensor_tensor(cols[:, 13:14], cols[:, 10:11], cols[:, 10:11],
                                           cols[:, 11:12], ALU.mult,
                                           ALU.subtract).then_inc(S("lnvB"), 1)
                    v.wait_ge(S("lnsB"), 2 * tt + 2)
                    v.reciprocal(cols[:, 15:16], cols[:, 14:15]).then_inc(S("lnvC"), 1)
                    v.wait_ge(S("lnvC"), 2 * tt + 2)
                    if tt == 0:
                        v.wait_ge(S("sGBe"), 4)
                    if tt >= 1:
                        v.wait_ge(S("gXn"), tt)
                    v.tensor_scalar(t1[:, :], xi[:, :], cols[:, 10:11],
                                    cols[:, 15:16], ALU.subtract,
                                    ALU.mult).then_inc(S("vT1"), 1)
                    v.wait_ge(S("vT1"), tt + 1)
                    v.tensor_tensor(t2[:, :], t1[:, :], gb[:, :],
                                    ALU.mult).then_inc(S("vT2"), 1)
                    v.wait_ge(S("gXn"), tt + 1)
                    v.reduce_max(mx[:, tt:tt + 1], xnb[:, :], axis=AX,
                                 apply_absolute_value=True).then_inc(S("vD"), 1)
                for dc in range(8):
                    qW.wait(v, dc)
                    v.reduce_sum(wcols[:, dc:dc + 1], wstg[dc % 2][:, :], axis=AX,
                                 apply_absolute_value=True).then_inc(S("vWc"), 1)
                v.wait_ge(S("vWc"), 8)
                v.reduce_sum(cols[:, 0:1], wcols[:, 0:8], axis=AX).then_inc(S("vWs"), 1)
                v.wait_ge(S("vWs"), 1)
                v.wait_ge(S("peB"), 1)
                v.tensor_scalar(gsc[0:1, 0:1], psb[0:1, 0:1], 1.0 / (D * D), None,
                                ALU.mult).then_inc(S("vBeta"), 1)
                v.wait_ge(S("vBeta"), 1)
                v.tensor_scalar(gsc[0:1, 1:2], gsc[0:1, 0:1], 1e-5, None, ALU.add
                                ).then_inc(S("vBeta2"), 1)
                v.wait_ge(S("vBeta2"), 1)
                v.reciprocal(gsc[0:1, 4:5], gsc[0:1, 1:2]).then_inc(S("vBinv"), 1)
                v.wait_ge(S("vBinv"), 1)
                for dc in range(8):
                    n = 8 + dc
                    qW.wait(v, n)
                    v.wait_ge(S("sWb"), 1)
                    if dc >= 2:
                        qQn.wait(v, dc - 2)
                    v.tensor_scalar(t1[:, :], wstg[n % 2][:, :], cols[:, 12:13],
                                    12582912.0, ALU.mult,
                                    ALU.add).then_inc(S("vWq1"), 1)
                    v.wait_ge(S("vWq1"), dc + 1)
                    v.tensor_scalar(t2[:, :], t1[:, :], 12582912.0, 1.0,
                                    ALU.subtract, ALU.min).then_inc(S("vWq2"), 1)
                    v.wait_ge(S("vWq2"), dc + 1)
                    v.tensor_scalar(wqt[dc % 2][:, :], t2[:, :], -1.0, None,
                                    ALU.max).then_inc(S("vWq"), 1)
                v.wait_ge(S("vD"), NT)
                v.reduce_max(cols[:, 0:1], mx[:, :], axis=AX).then_inc(S("vGc"), 1)
                v.wait_ge(S("vGc"), 1)
                v.wait_ge(S("peG"), 1)
                v.reduce_max(gsc[0:1, 2:3], psb[0:1, 0:128], axis=AX
                             ).then_inc(S("vGr"), 1)
                v.wait_ge(S("vGr"), 1)
                v.tensor_copy(gsc[0:1, 3:4], gsc[0:1, 2:3]).then_inc(S("vG"), 1)

            @blk.tensor
            def _(t):
                qRow.wait(t, 0)
                for half in range(2):
                    t.matmul(psb[:, 0:512], Pc["ones_f32"][0:1, :],
                             lrow[0:1, half * 512:(half + 1) * 512],
                             start=True, stop=True).then_inc(S("peGB"), 1)
                    t.wait_ge(S("sGBe"), half + 1)
                qRow.wait(t, 1)
                for half in range(2):
                    t.matmul(psb[:, 0:512], Pc["ones_f32"][0:1, :],
                             lrow[0:1, half * 512:(half + 1) * 512],
                             start=True, stop=True).then_inc(S("peGB"), 1)
                    t.wait_ge(S("sGBe"), half + 3)
                for tt in range(NT):
                    t.wait_ge(S("gXn"), tt + 1)
                    for ei in range(NE):
                        kk = tt * 8 + ei
                        if kk >= 2:
                            t.wait_ge(S("sC"), kk - 1)
                        t.transpose(pst[kk % 2][:, 0:128],
                                    xnb[:, ei * 128:(ei + 1) * 128],
                                    Pc["ident_bf"][:, :]).then_inc(S("pA"), 1)
                t.wait_ge(S("vWs"), 1)
                t.matmul(psb[0:1, 0:1], cols[:, 0:1], ones128[:, :],
                         start=True, stop=True).then_inc(S("peB"), 1)
                t.wait_ge(S("vBinv"), 1)
                t.matmul(psb[:, 4:5], Pc["ones_f32"][0:1, :], gsc[0:1, 4:5],
                         start=True, stop=True).then_inc(S("peWbM"), 1)
                t.wait_ge(S("vGc"), 1)
                t.wait_ge(S("sWb"), 1)
                t.transpose(psb[0:1, 0:128], cols[:, 0:1],
                            Pc["ident_f32"][:, :]).then_inc(S("peG"), 1)
        S.release()



def emit_phase_d(nc, es, Dr, Pc):
    with ExitStack() as wes:
        xq = [wes.enter_context(nc.sbuf_tensor(f"d_xq{i}", [128, NE * 512], BF16))
              for i in range(2)]
        wq_sb = wes.enter_context(nc.sbuf_tensor("d_wq", [128, NE * D], BF16))
        rr = [wes.enter_context(nc.sbuf_tensor(f"d_rr{i}", [128, D], F32))
              for i in range(2)]
        x2t = [wes.enter_context(nc.sbuf_tensor(f"d_x2{i}", [128, D], F32))
               for i in range(2)]
        xsq = wes.enter_context(nc.sbuf_tensor("d_xsq", [128, D], F32))
        xnb = wes.enter_context(nc.sbuf_tensor("d_xnb", [128, D], BF16))
        cols = wes.enter_context(nc.sbuf_tensor("d_cols", [128, 8], F32))
        gsc = wes.enter_context(nc.sbuf_tensor("d_gsc", [1, 8], F32))
        scol = wes.enter_context(nc.sbuf_tensor("d_scol", [128, 2], F32))
        xfm = wes.enter_context(nc.sbuf_tensor("d_xfm", [128, NE * 512], BF16))
        ps = [wes.enter_context(nc.psum_tensor(f"d_ps{i}", [128, 512], F32))
              for i in range(2)]
        pst = [wes.enter_context(nc.psum_tensor(f"d_pst{i}", [128, 1024], BF16))
               for i in range(2)]
        psb = wes.enter_context(nc.psum_tensor("d_psb", [128, 512], F32))
        S = Sems(nc, "d")
        qX = DmaQ(S, "dX", 2)
        qR = DmaQ(S, "dR", 2)
        qO = DmaQ(S, "dO", 2)
        qN = DmaQ(S, "dN", 2)
        qMs = DmaQ(S, "dMs", 2)
        qMw = DmaQ(S, "dMw", 2)

        with nc.Block() as blk:
            @blk.sync
            def _(sync):
                qMs.inc(sync.dma_start(gsc[0:1, 0:1], Dr["gmax_d"][:, :]))
                qMs.inc(sync.dma_start(gsc[0:1, 1:2], Dr["beta_d"][:, :]))
                for g in range(NG):
                    if g >= 2:
                        sync.wait_ge(S("peM"), 8 * (g - 1))
                    qX.inc(sync.dma_start(
                        xq[g % 2][:, :].rearrange("p (e t) -> p e t", t=512),
                        Dr["xn1g"][g].rearrange("e p t -> p e t")))
                    for k in range(4):
                        tt = g * 4 + k
                        if tt >= 2:
                            sync.wait_ge(S("vE"), 2 * (tt - 1))
                        t0 = tt * 128
                        qR.inc(sync.dma_start(rr[tt % 2][:, :],
                                              Dr["r1"][t0:t0 + 128, :]))
                        sync.wait_ge(S("vE"), 2 * tt + 2)
                        qO.inc(sync.dma_start(Dr["r2"][t0:t0 + 128, :],
                                              x2t[tt % 2][:, :]))
                    sync.wait_ge(S("sC"), 32 * (g + 1))
                    qN.inc(sync.dma_start(
                        Dr["xn2g"][g].rearrange("e p t -> p e t"),
                        xfm[:, :].rearrange("p (e t) -> p e t", t=512)))
                qN.wait(sync, NG - 1)
                qN.wait(sync, NG - 2)
                qO.wait(sync, NT - 1)
                qO.wait(sync, NT - 2)

            @blk.gpsimd
            def _(g_):
                for dc in range(8):
                    qMw.inc(g_.dma_start(wq_sb[:, dc * D:(dc + 1) * D],
                                         Dr["wqn_d"][dc * 128:(dc + 1) * 128, :]))

            @blk.vector
            def _(v):
                qMs.wait(v, 0)
                qMs.wait(v, 1)
                v.reciprocal(gsc[0:1, 2:3], gsc[0:1, 0:1]).then_inc(S("vS"), 1)
                v.wait_ge(S("vS"), 1)
                v.tensor_scalar(gsc[0:1, 3:4], gsc[0:1, 2:3], 128.0, None, ALU.mult)
                v.tensor_tensor(gsc[0:1, 4:5], gsc[0:1, 0:1], gsc[0:1, 1:2],
                                ALU.mult).then_inc(S("vS1b"), 1)
                v.wait_ge(S("vS1b"), 1)
                v.tensor_scalar(gsc[0:1, 5:6], gsc[0:1, 4:5], 1.0 / 128.0, None,
                                ALU.mult).then_inc(S("vS2"), 1)
                for g in range(NG):
                    qX.wait(v, g)
                    if g == 0:
                        v.wait_ge(S("sSc"), 1)
                    v.tensor_scalar(xq[g % 2][:, :], xq[g % 2][:, :], scol[:, 0:1],
                                    127.99999, ALU.mult,
                                    ALU.min).then_inc(S("vQa"), 1)
                    v.wait_ge(S("vQa"), g + 1)
                    v.tensor_scalar(xq[g % 2][:, :], xq[g % 2][:, :], -127.99999, None,
                                    ALU.max).then_inc(S("vQ"), 1)
                    for k in range(4):
                        tt = g * 4 + k
                        if tt >= 2:
                            qO.wait(v, tt - 2)
                            v.wait_ge(S("lnsA"), tt - 1)
                        for half in range(2):
                            n = 2 * tt + half
                            v.wait_ge(S("peM"), n + 1)
                            qR.wait(v, tt)
                            v.scalar_tensor_tensor(
                                x2t[tt % 2][:, half * 512:(half + 1) * 512],
                                ps[n % 2][:, :], scol[:, 1:2],
                                rr[tt % 2][:, half * 512:(half + 1) * 512],
                                ALU.mult, ALU.add).then_inc(S("vE"), 1)
                        v.wait_ge(S("vE"), 2 * tt + 2)
                        v.reduce_sum(cols[:, 0:1], x2t[tt % 2][:, :],
                                     axis=AX).then_inc(S("lnvA"), 1)
                        v.wait_ge(S("lnsA"), tt + 1)
                        v.reduce_sum(cols[:, 1:2], xsq[:, :],
                                     axis=AX).then_inc(S("lnvA"), 1)
                        v.wait_ge(S("lnvA"), 2 * tt + 2)
                        v.tensor_scalar(cols[:, 2:3], cols[:, 0:1], 1.0 / D, None, ALU.mult)
                        v.tensor_scalar(cols[:, 3:4], cols[:, 1:2], 1.0 / D, None,
                                        ALU.mult).then_inc(S("lnvM"), 1)
                        v.wait_ge(S("lnvM"), tt + 1)
                        v.scalar_tensor_tensor(cols[:, 5:6], cols[:, 2:3], cols[:, 2:3],
                                               cols[:, 3:4], ALU.mult,
                                               ALU.subtract).then_inc(S("lnvB"), 1)
                        v.wait_ge(S("lnsB"), tt + 1)
                        v.reciprocal(cols[:, 7:8], cols[:, 6:7]).then_inc(S("lnvC"), 1)
                        v.wait_ge(S("lnvC"), tt + 1)
                        if tt >= 1:
                            v.wait_ge(S("pA"), 8 * tt)
                        v.tensor_scalar(xnb[:, :], x2t[tt % 2][:, :], cols[:, 2:3],
                                        cols[:, 7:8], ALU.subtract,
                                        ALU.mult).then_inc(S("vD"), 1)

            @blk.scalar
            def _(s_):
                s_.wait_ge(S("peSc"), 2)
                s_.copy(scol[:, 0:2], psb[:, 0:2]).then_inc(S("sSc"), 1)
                for tt in range(NT):
                    v_n = 2 * tt
                    s_.wait_ge(S("vE"), 2 * tt + 2)
                    if tt >= 1:
                        s_.wait_ge(S("lnvA"), 2 * tt)
                    s_.activation(xsq[:, :], x2t[tt % 2][:, :],
                                  AF.Square).then_inc(S("lnsA"), 1)
                    s_.wait_ge(S("lnvB"), tt + 1)
                    s_.activation(cols[:, 6:7], cols[:, 5:6], AF.Sqrt, scale=-1.0,
                                  bias=Pc["c1e5"]).then_inc(S("lnsB"), 1)
                    for ei in range(NE):
                        kk = tt * 8 + ei
                        s_.wait_ge(S("pA"), kk + 1)
                        if tt % 4 == 0 and ei == 0 and tt >= 4:
                            qN.wait(s_, tt // 4 - 1)
                        k = tt % 4
                        s_.copy(xfm[:, ei * 512 + k * 128: ei * 512 + (k + 1) * 128],
                                pst[kk % 2][:, 0:128]).then_inc(S("sC"), 1)

            @blk.tensor
            def _(t):
                t.wait_ge(S("vS2"), 1)
                t.matmul(psb[:, 0:1], Pc["ones_f32"][0:1, :], gsc[0:1, 3:4],
                         start=True, stop=True)
                t.matmul(psb[:, 1:2], Pc["ones_f32"][0:1, :], gsc[0:1, 5:6],
                         start=True, stop=True).then_inc(S("peSc"), 2)
                qMw.wait(t, 6)
                qMw.wait(t, 7)
                for g in range(NG):
                    t.wait_ge(S("vQ"), g + 1)
                    for k in range(4):
                        tt = g * 4 + k
                        for half in range(2):
                            n = 2 * tt + half
                            if n >= 2:
                                t.wait_ge(S("vE"), n - 1)
                            for dc in range(NE):
                                ins = t.matmul(
                                    ps[n % 2][:, :],
                                    xq[g % 2][:, dc * 512 + k * 128: dc * 512 + (k + 1) * 128],
                                    wq_sb[:, dc * D + half * 512: dc * D + (half + 1) * 512],
                                    start=(dc == 0), stop=(dc == 7))
                            ins.then_inc(S("peM"), 1)
                        t.wait_ge(S("vD"), tt + 1)
                        for ei in range(NE):
                            kk = tt * 8 + ei
                            if kk >= 2:
                                t.wait_ge(S("sC"), kk - 1)
                            t.transpose(pst[kk % 2][:, 0:128],
                                        xnb[:, ei * 128:(ei + 1) * 128],
                                        Pc["ident_bf"][:, :]).then_inc(S("pA"), 1)
        S.release()



def emit_phase_e(nc, es, Dr, Pc):
    with ExitStack() as wes:
        wts = emit_load_wT(nc, wes, [Dr["wq1"], Dr["wk1"], Dr["wv1"]], "ew",
                           Pc["ident_bf"])
        bcols = emit_load_bias_cols(nc, wes, [Dr["bq1"], Dr["bk1"], Dr["bv1"]],
                                    "eb", Pc["ident_f32"])
        xg = [wes.enter_context(nc.sbuf_tensor(f"e_xg{i}", [128, NE * 512], BF16))
              for i in range(2)]
        qt = [wes.enter_context(nc.sbuf_tensor(f"e_qt{i}", [128, 512], BF16))
              for i in range(2)]
        vfm = wes.enter_context(nc.sbuf_tensor("e_vfm", [128, NE * 512], BF16))
        vtm = [wes.enter_context(nc.sbuf_tensor(f"e_vtm{i}", [128, D], BF16))
               for i in range(2)]
        pstv = [wes.enter_context(nc.psum_tensor(f"e_pstv{i}", [128, 1024], BF16))
                for i in range(2)]
        psq = [wes.enter_context(nc.psum_tensor(f"e_psq{i}", [128, 512], F32))
               for i in range(2)]
        S = Sems(nc, "e")
        qX = DmaQ(S, "dX", 2)
        qQ = DmaQ(S, "dQ", 2)
        qV = DmaQ(S, "dV", 2)

        with nc.Block() as blk:
            @blk.sync
            def _(sync):
                for g in range(NG):
                    sq = g // 4
                    tg0 = (g % 4) * 512
                    if g >= 2:
                        sync.wait_ge(S("pB"), 24 * (g - 1))
                    qX.inc(sync.dma_start(
                        xg[g % 2][:, :].rearrange("p (e t) -> p e t", t=512),
                        Dr["xn2g"][g].rearrange("e p t -> p e t")))
                    for wi in range(2):
                        for fi in range(NE):
                            sync.wait_ge(S("sEv"), g * 24 + wi * 8 + fi + 1)
                            m = g * 16 + wi * 8 + fi + 1
                            dst = Dr["qf1"] if wi == 0 else Dr["kf1"]
                            qQ.inc(sync.dma_start(
                                dst[sq, fi * 128:(fi + 1) * 128, tg0:tg0 + 512],
                                qt[m % 2][:, :]))
                    for k2 in range(4):
                        sync.wait_ge(S("sV2"), g * 32 + k2 * 8 + 8)
                        n = g * 4 + k2 + 1
                        qV.inc(sync.dma_start(
                            Dr["vt1"][sq, tg0 + k2 * 128: tg0 + (k2 + 1) * 128, :],
                            vtm[n % 2][:, :]))
                qQ.wait(sync, NG * 16 - 1)
                qQ.wait(sync, NG * 16 - 2)
                qV.wait(sync, NG * 4 - 1)
                qV.wait(sync, NG * 4 - 2)

            @blk.scalar
            def _(s_):
                for g in range(NG):
                    for wi in range(3):
                        for fi in range(NE):
                            n = g * 24 + wi * 8 + fi + 1
                            s_.wait_ge(S("pB"), n)
                            if wi < 2:
                                m = g * 16 + wi * 8 + fi + 1
                                if m > 2:
                                    qQ.wait(s_, m - 3)
                                s_.activation(qt[m % 2][:, :], psq[n % 2][:, :],
                                              AF.Identity,
                                              bias=bcols[wi][:, fi:fi + 1]
                                              ).then_inc(S("sEv"), 1)
                            else:
                                if g >= 1 and fi == 0:
                                    s_.wait_ge(S("pV"), 32 * g)
                                s_.activation(vfm[:, fi * 512:(fi + 1) * 512],
                                              psq[n % 2][:, :], AF.Identity,
                                              bias=bcols[2][:, fi:fi + 1]
                                              ).then_inc(S("sEv"), 1)
                    for k2 in range(4):
                        for fi in range(NE):
                            kk = g * 32 + k2 * 8 + fi
                            s_.wait_ge(S("pV"), kk + 1)
                            n = g * 4 + k2 + 1
                            if n > 2 and fi == 0:
                                qV.wait(s_, n - 3)
                            s_.copy(vtm[n % 2][:, fi * 128:(fi + 1) * 128],
                                    pstv[kk % 2][:, 0:128]).then_inc(S("sV2"), 1)

            @blk.tensor
            def _(t):
                for g in range(NG):
                    qX.wait(t, g)
                    for wi in range(3):
                        for fi in range(NE):
                            n = g * 24 + wi * 8 + fi + 1
                            for ei in range(NE):
                                if ei == 0 and n > 2:
                                    t.wait_ge(S("sEv"), n - 2)
                                ins = t.matmul(
                                    psq[n % 2][:, :],
                                    wts[wi][:, ei * D + fi * 128: ei * D + fi * 128 + 128],
                                    xg[g % 2][:, ei * 512:(ei + 1) * 512],
                                    start=(ei == 0), stop=(ei == 7))
                                if ei == 7:
                                    ins.then_inc(S("pB"), 1)
                    t.wait_ge(S("sEv"), g * 24 + 24)
                    for k2 in range(4):
                        for fi in range(NE):
                            kk = g * 32 + k2 * 8 + fi
                            if kk >= 2:
                                t.wait_ge(S("sV2"), kk - 1)
                            t.transpose(pstv[kk % 2][:, 0:128],
                                        vfm[:, fi * 512 + k2 * 128: fi * 512 + (k2 + 1) * 128],
                                        Pc["ident_bf"][:, :]).then_inc(S("pV"), 1)
        S.release()




def emit_phase_f(nc, es, Dr, Pc):
    NAV = 32
    with ExitStack() as wes:
        k2 = wes.enter_context(nc.sbuf_tensor("f_k2", [128, NE * T], BF16))
        q2 = wes.enter_context(nc.sbuf_tensor("f_q2", [128, NE * T], BF16))
        v2 = wes.enter_context(nc.sbuf_tensor("f_v2", [128, 16 * D], BF16))
        Aall = wes.enter_context(nc.sbuf_tensor("f_A", [128, 16 * T], BF16))
        lcol = wes.enter_context(nc.sbuf_tensor("f_lcol", [128, 2], F32))
        rr = [wes.enter_context(nc.sbuf_tensor(f"f_rr{i}", [128, D], F32))
              for i in range(2)]
        x3t = [wes.enter_context(nc.sbuf_tensor(f"f_x3{i}", [128, D], F32))
               for i in range(2)]
        xsq = wes.enter_context(nc.sbuf_tensor("f_xsq", [128, D], F32))
        xnb = wes.enter_context(nc.sbuf_tensor("f_xnb", [128, D], BF16))
        cols = wes.enter_context(nc.sbuf_tensor("f_cols", [128, 8], F32))
        xfm = wes.enter_context(nc.sbuf_tensor("f_xfm", [128, NE * 512], BF16))
        ps_sc = [wes.enter_context(nc.psum_tensor(f"f_psc{i}", [128, 512], F32))
                 for i in range(2)]
        ps_av = [wes.enter_context(nc.psum_tensor(f"f_av{i}", [128, D], F32))
                 for i in range(2)]
        pst = [wes.enter_context(nc.psum_tensor(f"f_pst{i}", [128, 1024], BF16))
               for i in range(2)]
        S = Sems(nc, "f")
        qL = DmaQ(S, "dL", 4)
        qR = DmaQ(S, "dR", 2)
        qO = DmaQ(S, "dO", 2)
        qN = DmaQ(S, "dN", 2)

        def chunks(jt):
            j0 = jt * 128
            out = []
            for ac in range(j0 // 512, 4):
                lo = max(j0, ac * 512)
                out.append((lo, (ac + 1) * 512))
            return out

        NCH = [len(chunks(jt)) for jt in range(16)]
        CUM = [0] * 17
        for jt in range(16):
            CUM[jt + 1] = CUM[jt] + NCH[jt]
        NSC = CUM[16]

        with nc.Block() as blk:
            @blk.sync
            def _(sync):
                for s in range(NSEQ):
                    if s >= 1:
                        sync.wait_ge(S("peA"), NAV * s)
                    qL.inc(sync.dma_start(
                        k2[:, :].rearrange("p (e t) -> p e t", t=T),
                        Dr["kf1"][s].rearrange("(e p) t -> p e t", p=128)))
                    qL.inc(sync.dma_start(
                        q2[:, :].rearrange("p (e t) -> p e t", t=T),
                        Dr["qf1"][s].rearrange("(e p) t -> p e t", p=128)))
                    qL.inc(sync.dma_start(
                        v2[:, :].rearrange("p (jt f) -> p jt f", f=D),
                        Dr["vt1"][s].rearrange("(jt p) f -> p jt f", p=128)))
                    for it in range(16):
                        wt = s * 16 + it
                        if wt >= 2:
                            sync.wait_ge(S("vE"), 2 * (wt - 1))
                        t0 = wt * 128
                        qR.inc(sync.dma_start(rr[wt % 2][:, :],
                                              Dr["r2"][t0:t0 + 128, :]))
                        sync.wait_ge(S("vE"), 2 * wt + 2)
                        qO.inc(sync.dma_start(Dr["r3"][t0:t0 + 128, :],
                                              x3t[wt % 2][:, :]))
                        if it % 4 == 3:
                            g = wt // 4
                            sync.wait_ge(S("sC"), 32 * (g + 1))
                            qN.inc(sync.dma_start(
                                Dr["xn3g"][g].rearrange("e p t -> p e t"),
                                xfm[:, :].rearrange("p (e t) -> p e t", t=512)))
                qO.wait(sync, NT - 1)
                qO.wait(sync, NT - 2)
                qN.wait(sync, NG - 1)
                qN.wait(sync, NG - 2)

            @blk.tensor
            def _(t):
                for s in range(NSEQ):
                    qL.wait(t, 3 * s)
                    qL.wait(t, 3 * s + 1)
                    for jt in range(16):
                        j0 = jt * 128
                        for ci, (lo, hi) in enumerate(chunks(jt)):
                            idx = NSC * s + CUM[jt] + ci
                            if idx >= 2:
                                t.wait_ge(S("sE"), idx - 1)
                            for ei in range(NE):
                                ins = t.matmul(
                                    ps_sc[idx % 2][:, 0:hi - lo],
                                    k2[:, ei * T + j0: ei * T + j0 + 128],
                                    q2[:, ei * T + lo: ei * T + hi],
                                    start=(ei == 0), stop=(ei == 7))
                            ins.then_inc(S("peS"), 1)
                    qL.wait(t, 3 * s + 2)
                    for it in range(16):
                        wt = s * 16 + it
                        t.wait_ge(S("vN"), 16 * s + it + 1)
                        if wt >= 2:
                            t.wait_ge(S("vE"), 2 * (wt - 1))
                        for half in range(2):
                            for jt in range(it + 1):
                                ins = t.matmul(
                                    ps_av[it % 2][:, half * 512:(half + 1) * 512],
                                    Aall[:, jt * T + it * 128: jt * T + (it + 1) * 128],
                                    v2[:, jt * D + half * 512: jt * D + (half + 1) * 512],
                                    start=(jt == 0), stop=(jt == it))
                            ins.then_inc(S("peA"), 1)
                        t.wait_ge(S("vD"), wt + 1)
                        for ei in range(NE):
                            kk = wt * 8 + ei
                            if kk >= 2:
                                t.wait_ge(S("sC"), kk - 1)
                            t.transpose(pst[kk % 2][:, 0:128],
                                        xnb[:, ei * 128:(ei + 1) * 128],
                                        Pc["ident_bf"][:, :]).then_inc(S("pA"), 1)

            @blk.scalar
            def _(s_):
                for s in range(NSEQ):
                    for jt in range(16):
                        j0 = jt * 128
                        for ci, (lo, hi) in enumerate(chunks(jt)):
                            idx = NSC * s + CUM[jt] + ci
                            s_.wait_ge(S("peS"), idx + 1)
                            if ci == 0:
                                s_.wait_ge(S("vMk"), 16 * s + jt + 1)
                            s_.activation(Aall[:, jt * T + lo: jt * T + hi],
                                          ps_sc[idx % 2][:, 0:hi - lo],
                                          AF.Exp, scale=1.0 / 32.0
                                          ).then_inc(S("sE"), 1)
                    for it in range(16):
                        wt = s * 16 + it
                        s_.wait_ge(S("vE"), 2 * wt + 2)
                        if wt >= 1:
                            s_.wait_ge(S("lnvA"), 2 * wt)
                        s_.activation(xsq[:, :], x3t[wt % 2][:, :],
                                      AF.Square).then_inc(S("lnsA"), 1)
                        s_.wait_ge(S("lnvB"), wt + 1)
                        s_.activation(cols[:, 6:7], cols[:, 5:6], AF.Sqrt, scale=-1.0,
                                      bias=Pc["c1e5"]).then_inc(S("lnsB"), 1)
                        for ei in range(NE):
                            kk = wt * 8 + ei
                            s_.wait_ge(S("pA"), kk + 1)
                            if wt % 4 == 0 and ei == 0 and wt >= 4:
                                qN.wait(s_, wt // 4 - 1)
                            k = wt % 4
                            s_.copy(xfm[:, ei * 512 + k * 128: ei * 512 + (k + 1) * 128],
                                    pst[kk % 2][:, 0:128]).then_inc(S("sC"), 1)

            @blk.vector
            def _(v):
                for s in range(NSEQ):
                    for jt in range(16):
                        j0 = jt * 128
                        idx0 = NSC * s + CUM[jt]
                        v.wait_ge(S("peS"), idx0 + 1)
                        v.tensor_tensor(ps_sc[idx0 % 2][:, 0:128],
                                        ps_sc[idx0 % 2][:, 0:128],
                                        Pc["maskband"][:, 0:128],
                                        ALU.add).then_inc(S("vMk"), 1)
                        v.wait_ge(S("sE"), NSC * s + CUM[jt + 1])
                        v.reduce_sum(lcol[:, 0:1], Aall[:, jt * T + j0: (jt + 1) * T],
                                     axis=AX).then_inc(S("vL"), 1)
                        v.wait_ge(S("vL"), 16 * s + jt + 1)
                        v.reciprocal(lcol[:, 1:2], lcol[:, 0:1]).then_inc(S("vLr"), 1)
                        v.wait_ge(S("vLr"), 16 * s + jt + 1)
                        v.tensor_scalar(Aall[:, jt * T + j0: (jt + 1) * T],
                                        Aall[:, jt * T + j0: (jt + 1) * T],
                                        lcol[:, 1:2], None,
                                        ALU.mult).then_inc(S("vN"), 1)
                    for it in range(16):
                        wt = s * 16 + it
                        if wt >= 2:
                            qO.wait(v, wt - 2)
                            v.wait_ge(S("lnsA"), wt - 1)
                        for half in range(2):
                            n = 2 * wt + half
                            v.wait_ge(S("peA"), 2 * wt + half + 1)
                            qR.wait(v, wt)
                            v.tensor_tensor(
                                x3t[wt % 2][:, half * 512:(half + 1) * 512],
                                ps_av[it % 2][:, half * 512:(half + 1) * 512],
                                rr[wt % 2][:, half * 512:(half + 1) * 512],
                                ALU.add).then_inc(S("vE"), 1)
                        v.wait_ge(S("vE"), 2 * wt + 2)
                        v.reduce_sum(cols[:, 0:1], x3t[wt % 2][:, :],
                                     axis=AX).then_inc(S("lnvA"), 1)
                        v.wait_ge(S("lnsA"), wt + 1)
                        v.reduce_sum(cols[:, 1:2], xsq[:, :],
                                     axis=AX).then_inc(S("lnvA"), 1)
                        v.wait_ge(S("lnvA"), 2 * wt + 2)
                        v.tensor_scalar(cols[:, 2:3], cols[:, 0:1], 1.0 / D, None, ALU.mult)
                        v.tensor_scalar(cols[:, 3:4], cols[:, 1:2], 1.0 / D, None,
                                        ALU.mult).then_inc(S("lnvM"), 1)
                        v.wait_ge(S("lnvM"), wt + 1)
                        v.scalar_tensor_tensor(cols[:, 5:6], cols[:, 2:3], cols[:, 2:3],
                                               cols[:, 3:4], ALU.mult,
                                               ALU.subtract).then_inc(S("lnvB"), 1)
                        v.wait_ge(S("lnsB"), wt + 1)
                        v.reciprocal(cols[:, 7:8], cols[:, 6:7]).then_inc(S("lnvC"), 1)
                        v.wait_ge(S("lnvC"), wt + 1)
                        if wt >= 1:
                            v.wait_ge(S("pA"), 8 * wt)
                        v.tensor_scalar(xnb[:, :], x3t[wt % 2][:, :], cols[:, 2:3],
                                        cols[:, 7:8], ALU.subtract,
                                        ALU.mult).then_inc(S("vD"), 1)

            @blk.gpsimd
            def _(g_):
                pass

        S.release()




def emit_phase_g(nc, es, Dr, Pc):
    with ExitStack() as wes:
        gate_sb = wes.enter_context(nc.sbuf_tensor("g_gate", [128, NE * 8], BF16))
        xng = [wes.enter_context(nc.sbuf_tensor(f"g_xng{i}", [128, NE * 512], BF16))
               for i in range(2)]
        gwork = wes.enter_context(nc.sbuf_tensor("g_gw", [128, 48], F32))
        gcols = wes.enter_context(nc.sbuf_tensor("g_gc", [128, 16], F32))
        comb = wes.enter_context(nc.sbuf_tensor("g_comb", [128, 8], F32))
        comb8 = wes.enter_context(nc.sbuf_tensor("g_c8", [8, NT * 128], F32))
        crow_t = [wes.enter_context(nc.sbuf_tensor(f"g_crow{i}", [1, 512], F32))
                  for i in range(2)]
        combb = wes.enter_context(nc.sbuf_tensor("g_cb", [128, 512], BF16))
        w1sb = [wes.enter_context(nc.sbuf_tensor(f"g_w1{i}", [128, NE * 2048], BF16))
                for i in range(2)]
        w2sb = [wes.enter_context(nc.sbuf_tensor(f"g_w2{i}", [128, 16 * D], BF16))
                for i in range(2)]
        hsb = wes.enter_context(nc.sbuf_tensor("g_h", [128, 16 * 512], BF16))
        ot = [wes.enter_context(nc.sbuf_tensor(f"g_ot{i}", [128, 512], F32))
              for i in range(2)]
        pg = wes.enter_context(nc.psum_tensor("g_pg", [128, 512], F32))
        psc = wes.enter_context(nc.psum_tensor("g_psc", [128, 512], F32))
        psh = [wes.enter_context(nc.psum_tensor(f"g_psh{i}", [128, 512], F32))
               for i in range(2)]
        pso = [wes.enter_context(nc.psum_tensor(f"g_pso{i}", [128, 512], F32))
               for i in range(2)]
        psb = wes.enter_context(nc.psum_tensor("g_psb", [128, 512], F32))
        S = Sems(nc, "g")
        qXg = DmaQ(S, "dXg", 2)
        qXe = DmaQ(S, "dXe", 2)
        qW1 = DmaQ(S, "dW1", 2)
        qW2 = DmaQ(S, "dW2", 2)
        qI = DmaQ(S, "dI", 2)
        qY = DmaQ(S, "dY", 2)
        qCr = DmaQ(S, "dCr", 2)
        qG = DmaQ(S, "dG", 2)

        with nc.Block() as blk:
            @blk.sync
            def _(sync):
                for g in range(NG):
                    if g >= 2:
                        sync.wait_ge(S("peG"), 4 * (g - 1))
                    qXg.inc(sync.dma_start(
                        xng[g % 2][:, :].rearrange("p (e t) -> p e t", t=512),
                        Dr["xn3g"][g].rearrange("e p t -> p e t")))
                for e in range(8):
                    for g in range(NG):
                        eg = e * NG + g
                        if eg < 2:
                            sync.wait_ge(S("peG"), NT)
                        else:
                            sync.wait_ge(S("peH"), 16 * (eg - 1))
                        qXe.inc(sync.dma_start(
                            xng[eg % 2][:, :].rearrange("p (e t) -> p e t", t=512),
                            Dr["xn3g"][g].rearrange("e p t -> p e t")))

            @blk.gpsimd
            def _(g_):
                qG.inc(g_.dma_start(
                    gate_sb[:, :].rearrange("p (e u) -> p e u", u=8),
                    Dr["gate_w"].rearrange("(e p) u -> p e u", p=128)))
                for i in range(8):
                    qI.inc(g_.dma_start(Dr["y"][i * 1024:(i + 1) * 1024, :],
                                        Dr["r3"][i * 1024:(i + 1) * 1024, :]))
                g_.wait_ge(S("sC8"), NT)
                g_.copy(comb8[0:1, 0:1], comb8[0:1, 0:1]) if False else None
                for e in range(8):
                    if e >= 2:
                        g_.wait_ge(S("peH"), 256 * (e - 1))
                    for dc in range(NE):
                        qW1.inc(g_.dma_start(
                            w1sb[e % 2][:, dc * 2048:(dc + 1) * 2048],
                            Dr["w1"][e, dc * 128:(dc + 1) * 128, :]))
                    if e >= 2:
                        g_.wait_ge(S("peW2"), 128 * (e - 1))
                    for fc in range(16):
                        qW2.inc(g_.dma_start(
                            w2sb[e % 2][:, fc * D:(fc + 1) * D],
                            Dr["w2"][e, fc * 128:(fc + 1) * 128, :]))
                    if e == 0:
                        qI.wait(g_, 7)
                        qI.wait(g_, 6)
                    for g in range(NG):
                        eg = e * NG + g
                        if eg >= 2:
                            g_.wait_ge(S("peB"), eg - 1)
                        qCr.inc(g_.dma_start(
                            crow_t[eg % 2][0:1, :],
                            comb8[e:e + 1, g * 512:(g + 1) * 512]))
                        for tt in range(4):
                            for half in range(2):
                                n = eg * 8 + tt * 2 + half
                                g_.wait_ge(S("vO"), n + 1)
                                t0 = g * 512 + tt * 128
                                qY.inc(g_.dma_start(
                                    Dr["y"][t0:t0 + 128, half * 512:(half + 1) * 512],
                                    ot[n % 2][:, :], accum_op=ALU.add))
                qY.wait(g_, 8 * 128 - 1)
                qY.wait(g_, 8 * 128 - 2)

            @blk.tensor
            def _(t):
                qG.wait(t, 0)
                for g in range(NG):
                    qXg.wait(t, g)
                    for k in range(4):
                        tile = g * 4 + k
                        if tile >= 1:
                            t.wait_ge(S("sGc"), tile)
                        for dc in range(NE):
                            ins = t.matmul(
                                pg[:, 0:8],
                                xng[g % 2][:, dc * 512 + k * 128: dc * 512 + (k + 1) * 128],
                                gate_sb[:, dc * 8:(dc + 1) * 8],
                                start=(dc == 0), stop=(dc == 7))
                        ins.then_inc(S("peG"), 1)
                        t.wait_ge(S("vCb"), tile + 1)
                        if tile >= 1:
                            t.wait_ge(S("sC8"), tile)
                        t.transpose(psc[0:8, 0:128], comb[:, 0:8],
                                    Pc["ident_f32"][:, :]).then_inc(S("peT"), 1)
                for e in range(8):
                    qW1.wait(t, e * 8 + 6)
                    qW1.wait(t, e * 8 + 7)
                    qW2.wait(t, e * 16 + 14)
                    qW2.wait(t, e * 16 + 15)
                    for g in range(NG):
                        eg = e * NG + g
                        qXe.wait(t, eg)
                        qCr.wait(t, eg)
                        if eg >= 1:
                            t.wait_ge(S("sCb"), eg)
                        t.matmul(psb[:, :], Pc["ones_f32"][0:1, :],
                                 crow_t[eg % 2][0:1, :],
                                 start=True, stop=True).then_inc(S("peB"), 1)
                        for fc in range(16):
                            m = eg * 16 + fc
                            if m >= 2:
                                t.wait_ge(S("sH"), m - 1)
                            for dc in range(NE):
                                ins = t.matmul(
                                    psh[m % 2][:, :],
                                    w1sb[e % 2][:, dc * 2048 + fc * 128: dc * 2048 + fc * 128 + 128],
                                    xng[eg % 2][:, dc * 512:(dc + 1) * 512],
                                    start=(dc == 0), stop=(dc == 7))
                            ins.then_inc(S("peH"), 1)
                        t.wait_ge(S("vH"), 16 * (eg + 1))
                        for tt in range(4):
                            for half in range(2):
                                n = eg * 8 + tt * 2 + half
                                if n >= 2:
                                    t.wait_ge(S("vO"), n - 1)
                                for fc in range(16):
                                    ins = t.matmul(
                                        pso[n % 2][:, :],
                                        hsb[:, fc * 512 + tt * 128: fc * 512 + (tt + 1) * 128],
                                        w2sb[e % 2][:, fc * D + half * 512: fc * D + (half + 1) * 512],
                                        start=(fc == 0), stop=(fc == 15))
                                ins.then_inc(S("peW2"), 1)

            @blk.scalar
            def _(s_):
                for tile in range(NT):
                    s_.wait_ge(S("peG"), tile + 1)
                    if tile >= 1:
                        s_.wait_ge(S("vExp"), tile)
                    s_.copy(gwork[:, 0:8], pg[:, 0:8]).then_inc(S("sGc"), 1)
                    s_.wait_ge(S("vExp"), tile + 1)
                    s_.activation(gwork[:, 32:40], gwork[:, 0:8], AF.Exp,
                                  bias=gcols[:, 7:8]).then_inc(S("sExp"), 1)
                    s_.wait_ge(S("peT"), tile + 1)
                    s_.copy(comb8[0:8, tile * 128:(tile + 1) * 128],
                            psc[0:8, 0:128]).then_inc(S("sC8"), 1)
                for e in range(8):
                    for g in range(NG):
                        eg = e * NG + g
                        s_.wait_ge(S("peB"), eg + 1)
                        if eg >= 1:
                            s_.wait_ge(S("vH"), 16 * eg)
                        s_.copy(combb[:, :], psb[:, :]).then_inc(S("sCb"), 1)
                        for fc in range(16):
                            m = eg * 16 + fc
                            s_.wait_ge(S("peH"), m + 1)
                            if fc == 0 and eg >= 1:
                                s_.wait_ge(S("peW2"), 8 * eg)
                            s_.activation(hsb[:, fc * 512:(fc + 1) * 512],
                                          psh[m % 2][:, :],
                                          AF.Silu).then_inc(S("sH"), 1)

            @blk.vector
            def _(v):
                for tile in range(NT):
                    v.wait_ge(S("sGc"), tile + 1)
                    v.reduce_max(gcols[:, 0:1], gwork[:, 0:8],
                                 axis=AX).then_inc(S("v1"), 1)
                    v.wait_ge(S("v1"), tile + 1)
                    v.tensor_scalar(gwork[:, 8:16], gwork[:, 0:8], gcols[:, 0:1],
                                    None, ALU.is_ge).then_inc(S("v2"), 1)
                    v.wait_ge(S("v2"), tile + 1)
                    v.reduce_sum(gcols[:, 1:2], gwork[:, 8:16],
                                 axis=AX).then_inc(S("v3"), 1)
                    v.scalar_tensor_tensor(gwork[:, 16:24], gwork[:, 8:16], -1e30,
                                           gwork[:, 0:8], ALU.mult,
                                           ALU.add).then_inc(S("v4"), 1)
                    v.wait_ge(S("v4"), tile + 1)
                    v.reduce_max(gcols[:, 2:3], gwork[:, 16:24],
                                 axis=AX).then_inc(S("v5"), 1)
                    v.wait_ge(S("v3"), tile + 1)
                    v.tensor_scalar(gcols[:, 3:4], gcols[:, 1:2], -1.0, 1.0,
                                    ALU.add, ALU.min).then_inc(S("v6"), 1)
                    v.wait_ge(S("v5"), tile + 1)
                    v.wait_ge(S("v6"), tile + 1)
                    v.tensor_tensor(gcols[:, 4:5], gcols[:, 0:1], gcols[:, 2:3],
                                    ALU.subtract).then_inc(S("v7"), 1)
                    v.wait_ge(S("v7"), tile + 1)
                    v.tensor_tensor(gcols[:, 5:6], gcols[:, 4:5], gcols[:, 3:4],
                                    ALU.mult).then_inc(S("v8"), 1)
                    v.wait_ge(S("v8"), tile + 1)
                    v.tensor_tensor(gcols[:, 6:7], gcols[:, 2:3], gcols[:, 5:6],
                                    ALU.add).then_inc(S("v9"), 1)
                    v.wait_ge(S("v9"), tile + 1)
                    v.tensor_scalar(gwork[:, 24:32], gwork[:, 0:8], gcols[:, 6:7],
                                    None, ALU.is_ge)
                    v.tensor_scalar(gcols[:, 7:8], gcols[:, 0:1], -1.0, None,
                                    ALU.mult).then_inc(S("vExp"), 1)
                    v.wait_ge(S("sExp"), tile + 1)
                    v.tensor_tensor(gwork[:, 40:48], gwork[:, 32:40], gwork[:, 24:32],
                                    ALU.mult).then_inc(S("vA"), 1)
                    v.wait_ge(S("vA"), tile + 1)
                    v.reduce_sum(gcols[:, 8:9], gwork[:, 40:48],
                                 axis=AX).then_inc(S("vB"), 1)
                    v.wait_ge(S("vB"), tile + 1)
                    v.reciprocal(gcols[:, 9:10], gcols[:, 8:9]).then_inc(S("vC"), 1)
                    v.wait_ge(S("vC"), tile + 1)
                    if tile >= 1:
                        v.wait_ge(S("peT"), tile)
                    v.tensor_scalar(comb[:, 0:8], gwork[:, 40:48], gcols[:, 9:10],
                                    None, ALU.mult).then_inc(S("vCb"), 1)
                for e in range(8):
                    for g in range(NG):
                        eg = e * NG + g
                        for fc in range(16):
                            m = eg * 16 + fc
                            v.wait_ge(S("sH"), m + 1)
                            if fc == 0:
                                v.wait_ge(S("sCb"), eg + 1)
                            v.tensor_tensor(hsb[:, fc * 512:(fc + 1) * 512],
                                            hsb[:, fc * 512:(fc + 1) * 512],
                                            combb[:, :], ALU.mult).then_inc(S("vH"), 1)
                        for tt in range(4):
                            for half in range(2):
                                n = eg * 8 + tt * 2 + half
                                v.wait_ge(S("peW2"), n + 1)
                                if n >= 2:
                                    qY.wait(v, n - 2)
                                v.tensor_copy(ot[n % 2][:, :],
                                              pso[n % 2][:, :]).then_inc(S("vO"), 1)
        S.release()



PHASE_IO = {
    "a": (["x", "wq0", "wk0", "wv0", "bq0", "bk0", "bv0"],
          ["qf0", "kf0", "vt0"]),
    "b": (["x", "qf0", "kf0", "vt0", "wo0", "bo0", "rel"],
          ["r1", "tcorr_d", "dbg"]),
    "c": (["r1", "wffn", "lng", "lnb"],
          ["xn1g", "wqn_d", "beta_d", "gmax_d", "cc_in", "cc_out"]),
    "d": (["r1", "xn1g", "wqn_d", "beta_d", "gmax_d"],
          ["r2", "xn2g"]),
    "e": (["xn2g", "wq1", "wk1", "wv1", "bq1", "bk1", "bv1"],
          ["qf1", "kf1", "vt1"]),
    "f": (["qf1", "kf1", "vt1", "r2"],
          ["r3", "xn3g"]),
    "g": (["xn3g", "r3", "gate_w", "w1", "w2"],
          ["y"]),
}

DRAM_SPECS = {
    "x": ([TOK, D], F32),
    "wq0": ([D, D], F32), "wk0": ([D, D], F32), "wv0": ([D, D], F32),
    "wo0": ([D, D], F32),
    "bq0": ([NE, 128], F32), "bk0": ([NE, 128], F32), "bv0": ([NE, 128], F32),
    "bo0": ([1, D], F32),
    "rel": ([32, 16], F32),
    "qf0": ([NSEQ, D, T], BF16), "kf0": ([NSEQ, D, T], BF16),
    "vt0": ([NSEQ, T, D], BF16),
    "r1": ([TOK, D], F32),
    "r2": ([TOK, D], F32),
    "tcorr_d": ([16, 128 * 256], F32),
    "wffn": ([D, D], F32),
    "lng": ([1, D], F32), "lnb": ([1, D], F32),
    "xn1g": ([NG, NE, 128, 512], BF16),
    "xn2g": ([NG, NE, 128, 512], BF16),
    "wqn_d": ([D, D], BF16),
    "wq1": ([D, D], F32), "wk1": ([D, D], F32), "wv1": ([D, D], F32),
    "bq1": ([NE, 128], F32), "bk1": ([NE, 128], F32), "bv1": ([NE, 128], F32),
    "qf1": ([NSEQ, D, T], BF16), "kf1": ([NSEQ, D, T], BF16),
    "vt1": ([NSEQ, T, D], BF16),
    "r3": ([TOK, D], F32),
    "xn3g": ([NG, NE, 128, 512], BF16),
    "gate_w": ([D, 8], F32),
    "w1": ([8, D, 2048], F32),
    "w2": ([8, 2048, D], F32),
    "y": ([TOK, D], F32),
    "beta_d": ([1, 1], F32),
    "gmax_d": ([1, 1], F32),
    "cc_in": ([1, 1], F32),
    "cc_out": ([1, 1], F32),
    "dbg": ([128, 2064], F32),
}

CONST_SPECS = {
    "ident_bf": ([128, 128], BF16),
    "ident_f32": ([128, 128], F32),
    "ones_f32": ([1, 128], F32),
    "selmat": ([32, 128 * 256], BF16, "dram_only"),
    "maskband": ([128, 256], F32),
}

CONST_COLS = {"c1e5": 1e-5, "c1e3": 1e-3, "c44": 44.0, "cLE": 13.5015240554}

PHASE_EMIT = {"a": emit_phase_a2, "b": emit_phase_b, "c": emit_phase_c, "d": emit_phase_d, "e": emit_phase_e, "f": emit_phase_f, "g": emit_phase_g}


def build_program(phases, external):
    nc = bass.Bass("TRN2", target_bir_lowering=False)
    consumes = set()
    produces = set()
    for p in phases:
        c, pr = PHASE_IO[p]
        consumes.update(x for x in c if x not in produces)
        produces.update(pr)
    Dr = {}
    ext_inputs = []
    for name in sorted(consumes | produces):
        shape, dt = DRAM_SPECS[name]
        if name in produces and name in external:
            kind = "ExternalOutput"
        elif name in consumes and name not in produces:
            kind = "ExternalInput"
            ext_inputs.append(name)
        else:
            kind = "Internal"
        Dr[name] = nc.dram_tensor(name, shape, dt, kind=kind)
        if kind == "ExternalInput":
            nc._ext_inputs = getattr(nc, "_ext_inputs", []) + [name]
    Pc = {}
    with ExitStack() as es:
        for cn, spec in CONST_SPECS.items():
            shape, dt = spec[0], spec[1]
            dr = nc.dram_tensor("c_" + cn, shape, dt, kind="ExternalInput")
            Pc[cn + "_dram"] = dr
            if len(spec) > 2 and spec[2] == "dram_only":
                continue
            sb = es.enter_context(nc.sbuf_tensor("sb_" + cn, shape, dt))
            Pc[cn] = sb
        ccols = es.enter_context(nc.sbuf_tensor("sb_ccols", [128, len(CONST_COLS)], F32))
        for i, cn in enumerate(CONST_COLS):
            Pc[cn] = ccols[:, i:i + 1]
        S0 = Sems(nc, "init")
        with nc.Block() as blk:
            @blk.sync
            def _(sync):
                n = 0
                for cn, spec in CONST_SPECS.items():
                    if len(spec) > 2 and spec[2] == "dram_only":
                        continue
                    sync.dma_start(Pc[cn][:, :], Pc[cn + "_dram"][:, :]
                                   ).then_inc(S0("dma"), 16)
                    n += 16
                sync.wait_ge(S0("dma"), n)
            @blk.gpsimd
            def _(g):
                for i, (cn, val) in enumerate(CONST_COLS.items()):
                    g.memset(ccols[:, i:i + 1], val)

        for p in phases:
            PHASE_EMIT[p](nc, es, Dr, Pc)
    return nc


FUSED = True
LAUNCH_GROUPS = ([["a", "b", "c", "d", "e", "f", "g"]] if FUSED else
                 [["a"], ["b"], ["c"], ["d"], ["e"], ["f"], ["g"]])

_INTERNAL_ONLY = {"tcorr_d", "cc_in", "cc_out", "dbg"}


def _base_maps(inputs):
    consts = host_consts()
    x = np.ascontiguousarray(np.asarray(inputs["L0_attn_x"], np.float32))
    maps = []
    for c in range(NCORES):
        m = {
            "x": np.ascontiguousarray(x[4 * c:4 * c + 4].reshape(TOK, D)),
            "wq0": np.asarray(inputs["L0_attn_Wq"], np.float32),
            "wk0": np.asarray(inputs["L0_attn_Wk"], np.float32),
            "wv0": np.asarray(inputs["L0_attn_Wv"], np.float32),
            "wo0": np.asarray(inputs["L0_attn_Wo"], np.float32),
            "bq0": np.asarray(inputs["L0_attn_bq"], np.float32).reshape(NE, 128),
            "bk0": np.asarray(inputs["L0_attn_bk"], np.float32).reshape(NE, 128),
            "bv0": np.asarray(inputs["L0_attn_bv"], np.float32).reshape(NE, 128),
            "bo0": np.asarray(inputs["L0_attn_bo"], np.float32).reshape(1, D),
            "rel": np.asarray(inputs["L0_attn_rel_emb"], np.float32),
            "wffn": np.asarray(inputs["L0_ffn_weight"], np.float32),
            "lng": np.asarray(inputs["L0_ffn_ln_gamma"], np.float32).reshape(1, D),
            "lnb": np.asarray(inputs["L0_ffn_ln_beta"], np.float32).reshape(1, D),
            "wq1": np.asarray(inputs["L1_attn_Wq"], np.float32),
            "wk1": np.asarray(inputs["L1_attn_Wk"], np.float32),
            "wv1": np.asarray(inputs["L1_attn_Wv"], np.float32),
            "bq1": np.asarray(inputs["L1_attn_bq"], np.float32).reshape(NE, 128),
            "bk1": np.asarray(inputs["L1_attn_bk"], np.float32).reshape(NE, 128),
            "bv1": np.asarray(inputs["L1_attn_bv"], np.float32).reshape(NE, 128),
            "gate_w": np.asarray(inputs["L1_ffn_gate_w"], np.float32),
            "w1": np.asarray(inputs["L1_ffn_w1"], np.float32),
            "w2": np.asarray(inputs["L1_ffn_w2"], np.float32),
        }
        for cn in CONST_SPECS:
            m["c_" + cn] = consts[cn]
        maps.append(m)
    return maps


def kernel(**inputs):
    state = _base_maps(inputs)
    n_groups = len(LAUNCH_GROUPS)
    later_consumes = []
    acc = {"y"}
    for gi in range(n_groups - 1, -1, -1):
        later_consumes.insert(0, set(acc))
        for p in LAUNCH_GROUPS[gi]:
            acc |= set(PHASE_IO[p][0])
    for gi, group in enumerate(LAUNCH_GROUPS):
        produced = set()
        for p in group:
            produced |= set(PHASE_IO[p][1])
        ext = (produced & later_consumes[gi]) - _INTERNAL_ONLY
        if gi == n_groups - 1:
            ext |= {"y"}
        nc = build_program(group, external=ext)
        ext_in = set(getattr(nc, "_ext_inputs", [])) | {"c_" + c for c in CONST_SPECS}
        in_maps = [{k: v for k, v in state[c].items() if k in ext_in}
                   for c in range(NCORES)]
        res = run_bass_kernel_spmd(nc, in_maps, core_ids=list(range(NCORES)))
        for c in range(NCORES):
            for k, v in res.results[c].items():
                state[c][k] = np.asarray(v)
    out = np.concatenate(
        [state[c]["y"].reshape(NSEQ, T, D) for c in range(NCORES)], axis=0)
    return out


# version 03
# speedup vs baseline: 1.0459x; 1.0459x over previous
import math
import numpy as np
import ml_dtypes
from contextlib import ExitStack

import concourse.bass as bass
import concourse.mybir as mybir
from concourse.bass_utils import run_bass_kernel_spmd

F32 = mybir.dt.float32
BF16 = mybir.dt.bfloat16
AF = mybir.ActivationFunctionType
ALU = mybir.AluOpType
AX = mybir.AxisListType.X
BF = ml_dtypes.bfloat16

NCORES = 8
NSEQ = 4
T = 2048
D = 1024
TOK = NSEQ * T
NE = D // 128
NG = TOK // 512
NT = TOK // 128
HEADS = 16
HD = 64
NUM_BUCKETS = 32
MAX_DISTANCE = 128
SCALE = (D // HEADS) ** (-0.5)
NEG = -1e30
B_DBG = False
B_NOBIAS = False


def _bucket(d):
    max_exact = NUM_BUCKETS // 2
    if d < max_exact:
        return d
    v = max_exact + int(
        math.log(d / max_exact) / math.log(MAX_DISTANCE / max_exact)
        * (NUM_BUCKETS - max_exact))
    return min(v, NUM_BUCKETS - 1)


def host_consts():
    c = {}
    c["ident_bf"] = np.eye(128, dtype=BF)
    c["ident_f32"] = np.eye(128, dtype=np.float32)
    c["ones_f32"] = np.ones((1, 128), dtype=np.float32)
    A = np.zeros((NUM_BUCKETS, 128 * 256), dtype=np.float32)
    for p in range(128):
        for ip in range(256):
            d = ip - p
            if d < 0:
                continue
            pos = p * 256 + ip
            A[_bucket(d), pos] += 1.0
            A[NUM_BUCKETS - 1, pos] -= 1.0
    c["selmat"] = A.astype(BF)
    mb = np.zeros((128, 256), dtype=np.float32)
    for p in range(128):
        mb[p, :p] = NEG
    c["maskband"] = mb
    return c


class Sems:
    def __init__(self, nc, prefix):
        self.nc = nc
        self.prefix = prefix
        self.h = {}

    def __call__(self, name):
        if name not in self.h:
            self.h[name] = self.nc.alloc_semaphore(self.prefix + "_" + name)
        return self.h[name]

    def release(self):
        if self.h:
            self.nc.clear_and_free_semaphores(list(self.h.values()))
            self.nc.all_engine_barrier()
            self.h = {}

class DmaQ:

    def __init__(self, S, name, nsem=2):
        self.sems = [S(f"{name}{i}") for i in range(nsem)]
        self.P = nsem
        self.n = 0

    def inc(self, instr):
        instr.then_inc(self.sems[self.n % self.P], 16)
        self.n += 1

    def wait(self, eng, k):
        eng.wait_ge(self.sems[k % self.P], 16 * (k // self.P + 1))



def emit_load_wT(nc, es, w_drams, tag, ident_bf):
    outs = [es.enter_context(nc.sbuf_tensor(f"{tag}_wt{i}", [128, NE * D], BF16))
            for i in range(len(w_drams))]
    les = ExitStack()
    stage = [les.enter_context(nc.sbuf_tensor(f"{tag}_wstg{i}", [128, D], BF16))
             for i in range(2)]
    ps = [les.enter_context(nc.psum_tensor(f"{tag}_wps{i}", [128, 1024], BF16))
          for i in range(2)]
    S = Sems(nc, tag + "_lw")
    NW = len(w_drams)
    qW = DmaQ(S, "dma", 2)
    with les, nc.Block() as blk:
        @blk.gpsimd
        def _(g):
            n = 0
            for wi in range(NW):
                for fi in range(NE):
                    n += 1
                    if n > 2:
                        g.wait_ge(S("pe"), 8 * (n - 2))
                    qW.inc(g.dma_start(stage[n % 2][:, :],
                                       w_drams[wi][fi * 128:(fi + 1) * 128, :]))

        @blk.tensor
        def _(t):
            n = 0
            for wi in range(NW):
                for fi in range(NE):
                    n += 1
                    qW.wait(t, n - 1)
                    for ei in range(NE):
                        k = (n - 1) * 8 + ei
                        if k >= 2:
                            t.wait_ge(S("se"), k - 1)
                        t.transpose(ps[k % 2][:, 0:128],
                                    stage[n % 2][:, ei * 128:(ei + 1) * 128],
                                    ident_bf[:, :]).then_inc(S("pe"), 1)

        @blk.scalar
        def _(s):
            n = 0
            for wi in range(NW):
                for fi in range(NE):
                    n += 1
                    for ei in range(NE):
                        k = (n - 1) * 8 + ei
                        s.wait_ge(S("pe"), k + 1)
                        s.copy(outs[wi][:, ei * D + fi * 128: ei * D + (fi + 1) * 128],
                               ps[k % 2][:, 0:128]).then_inc(S("se"), 1)
    S.release()
    return outs


def emit_load_bias_cols(nc, es, b_drams, tag, ident_f32):
    outs = [es.enter_context(nc.sbuf_tensor(f"{tag}_bc{i}", [128, NE], F32))
            for i in range(len(b_drams))]
    les = ExitStack()
    stage = [les.enter_context(nc.sbuf_tensor(f"{tag}_bstg{i}", [NE, 128], F32))
             for i in range(len(b_drams))]
    ps = les.enter_context(nc.psum_tensor(f"{tag}_bps", [128, 512], F32))
    S = Sems(nc, tag + "_lb")
    with les, nc.Block() as blk:
        @blk.sync
        def _(sync):
            for i, b in enumerate(b_drams):
                sync.dma_start(stage[i][:, :], b[:, :]).then_inc(S("dma"), 16)

        @blk.tensor
        def _(t):
            t.wait_ge(S("dma"), 16 * len(b_drams))
            for i in range(len(b_drams)):
                if i >= 1:
                    t.wait_ge(S("se"), i)
                t.transpose(ps[:, 0:NE], stage[i][:, :],
                            ident_f32[0:NE, 0:NE]).then_inc(S("pe"), 1)

        @blk.scalar
        def _(s):
            for i in range(len(b_drams)):
                s.wait_ge(S("pe"), i + 1)
                s.copy(outs[i][:, :], ps[:, 0:NE]).then_inc(S("se"), 1)
    S.release()
    return outs


def emit_phase_a2(nc, es, Dr, Pc):
    with ExitStack() as wes:
        wts = emit_load_wT(nc, wes, [Dr["wq0"], Dr["wk0"], Dr["wv0"]], "aw",
                           Pc["ident_bf"])
        bcols = emit_load_bias_cols(nc, wes, [Dr["bq0"], Dr["bk0"], Dr["bv0"]],
                                    "ab", Pc["ident_f32"])
        xin = [wes.enter_context(nc.sbuf_tensor(f"a_xin{i}", [128, D], F32))
               for i in range(2)]
        xsq = wes.enter_context(nc.sbuf_tensor("a_xsq", [128, D], F32))
        cols = wes.enter_context(nc.sbuf_tensor("a_cols", [128, 8], F32))
        xlnb = wes.enter_context(nc.sbuf_tensor("a_xlnb", [128, D], BF16))
        xfm = wes.enter_context(nc.sbuf_tensor("a_xfm", [128, NE * 512], BF16))
        qt = [wes.enter_context(nc.sbuf_tensor(f"a_qt{i}", [128, 512], BF16))
              for i in range(2)]
        vfm = wes.enter_context(nc.sbuf_tensor("a_vfm", [128, NE * 512], BF16))
        vtm = [wes.enter_context(nc.sbuf_tensor(f"a_vtm{i}", [128, D], BF16))
               for i in range(2)]
        pst = [wes.enter_context(nc.psum_tensor(f"a_pst{i}", [128, 1024], BF16))
               for i in range(2)]
        pstv = [wes.enter_context(nc.psum_tensor(f"a_pstv{i}", [128, 1024], BF16))
                for i in range(2)]
        psq = [wes.enter_context(nc.psum_tensor(f"a_psq{i}", [128, 512], F32))
               for i in range(2)]
        S = Sems(nc, "a")
        qA = DmaQ(S, "dA", 2)
        qQ = DmaQ(S, "dQ", 2)
        qV = DmaQ(S, "dV", 2)

        with nc.Block() as blk:
            @blk.sync
            def _(sync):
                for g in range(NG):
                    sq = g // 4
                    tg0 = (g % 4) * 512
                    for k in range(4):
                        tt = g * 4 + k
                        if tt >= 2:
                            sync.wait_ge(S("vD"), tt - 1)
                        t0 = tt * 128
                        qA.inc(sync.dma_start(xin[tt % 2][:, :],
                                              Dr["x"][t0:t0 + 128, :]))
                    for wi in range(2):
                        for fi in range(NE):
                            sync.wait_ge(S("sEv"), g * 24 + wi * 8 + fi + 1)
                            m = g * 16 + wi * 8 + fi + 1
                            dst = Dr["qf0"] if wi == 0 else Dr["kf0"]
                            qQ.inc(sync.dma_start(
                                dst[sq, fi * 128:(fi + 1) * 128, tg0:tg0 + 512],
                                qt[m % 2][:, :]))
                    for k2 in range(4):
                        sync.wait_ge(S("sV2"), g * 32 + k2 * 8 + 8)
                        n = g * 4 + k2 + 1
                        qV.inc(sync.dma_start(
                            Dr["vt0"][sq, tg0 + k2 * 128: tg0 + (k2 + 1) * 128, :],
                            vtm[n % 2][:, :]))
                qQ.wait(sync, NG * 16 - 1)
                qQ.wait(sync, NG * 16 - 2)
                qV.wait(sync, NG * 4 - 1)
                qV.wait(sync, NG * 4 - 2)

            @blk.scalar
            def _(s):
                for g in range(NG):
                    for k in range(4):
                        tt = g * 4 + k
                        qA.wait(s, tt)
                        if tt >= 1:
                            s.wait_ge(S("lnvA"), 2 * tt)
                        s.activation(xsq[:, :], xin[tt % 2][:, :],
                                     AF.Square).then_inc(S("lnsA"), 1)
                        s.wait_ge(S("lnvB"), tt + 1)
                        s.activation(cols[:, 6:7], cols[:, 5:6], AF.Sqrt, scale=-1.0,
                                     bias=Pc["c1e5"]).then_inc(S("lnsB"), 1)
                        for ei in range(NE):
                            kk = tt * 8 + ei
                            s.wait_ge(S("pA"), kk + 1)
                            if g >= 1 and k == 0 and ei == 0:
                                s.wait_ge(S("pB"), 24 * g)
                            s.copy(xfm[:, ei * 512 + k * 128: ei * 512 + (k + 1) * 128],
                                   pst[kk % 2][:, 0:128]).then_inc(S("sC"), 1)
                    for wi in range(3):
                        for fi in range(NE):
                            n = g * 24 + wi * 8 + fi + 1
                            s.wait_ge(S("pB"), n)
                            if wi < 2:
                                m = g * 16 + wi * 8 + fi + 1
                                if m > 2:
                                    qQ.wait(s, m - 3)
                                s.activation(qt[m % 2][:, :], psq[n % 2][:, :],
                                             AF.Identity, bias=bcols[wi][:, fi:fi + 1]
                                             ).then_inc(S("sEv"), 1)
                            else:
                                if g >= 1 and fi == 0:
                                    s.wait_ge(S("pV"), 32 * g)
                                s.activation(vfm[:, fi * 512:(fi + 1) * 512],
                                             psq[n % 2][:, :], AF.Identity,
                                             bias=bcols[2][:, fi:fi + 1]
                                             ).then_inc(S("sEv"), 1)
                    for k2 in range(4):
                        for fi in range(NE):
                            kk = g * 32 + k2 * 8 + fi
                            s.wait_ge(S("pV"), kk + 1)
                            n = g * 4 + k2 + 1
                            if n > 2 and fi == 0:
                                qV.wait(s, n - 3)
                            s.copy(vtm[n % 2][:, fi * 128:(fi + 1) * 128],
                                   pstv[kk % 2][:, 0:128]).then_inc(S("sV2"), 1)

            @blk.vector
            def _(v):
                for g in range(NG):
                    for k in range(4):
                        tt = g * 4 + k
                        qA.wait(v, tt)
                        v.reduce_sum(cols[:, 0:1], xin[tt % 2][:, :],
                                     axis=AX).then_inc(S("lnvA"), 1)
                        v.wait_ge(S("lnsA"), tt + 1)
                        v.reduce_sum(cols[:, 1:2], xsq[:, :],
                                     axis=AX).then_inc(S("lnvA"), 1)
                        v.wait_ge(S("lnvA"), 2 * (tt + 1))
                        v.tensor_scalar(cols[:, 2:3], cols[:, 0:1], 1.0 / D, None, ALU.mult)
                        v.tensor_scalar(cols[:, 3:4], cols[:, 1:2], 1.0 / D, None,
                                        ALU.mult).then_inc(S("lnvM"), 1)
                        v.wait_ge(S("lnvM"), tt + 1)
                        v.scalar_tensor_tensor(cols[:, 5:6], cols[:, 2:3], cols[:, 2:3],
                                               cols[:, 3:4], ALU.mult,
                                               ALU.subtract).then_inc(S("lnvB"), 1)
                        v.wait_ge(S("lnsB"), tt + 1)
                        v.reciprocal(cols[:, 7:8], cols[:, 6:7]).then_inc(S("lnvC"), 1)
                        v.wait_ge(S("lnvC"), tt + 1)
                        if tt >= 1:
                            v.wait_ge(S("pA"), 8 * tt)
                        v.tensor_scalar(xlnb[:, :], xin[tt % 2][:, :], cols[:, 2:3],
                                        cols[:, 7:8], ALU.subtract,
                                        ALU.mult).then_inc(S("vD"), 1)

            @blk.tensor
            def _(t):
                for g in range(NG):
                    for k in range(4):
                        tt = g * 4 + k
                        t.wait_ge(S("vD"), tt + 1)
                        for ei in range(NE):
                            kk = tt * 8 + ei
                            if kk >= 2:
                                t.wait_ge(S("sC"), kk - 1)
                            t.transpose(pst[kk % 2][:, 0:128],
                                        xlnb[:, ei * 128:(ei + 1) * 128],
                                        Pc["ident_bf"][:, :]).then_inc(S("pA"), 1)
                    t.wait_ge(S("sC"), 32 * (g + 1))
                    for wi in range(3):
                        for fi in range(NE):
                            n = g * 24 + wi * 8 + fi + 1
                            for ei in range(NE):
                                if ei == 0 and n > 2:
                                    t.wait_ge(S("sEv"), n - 2)
                                ins = t.matmul(
                                    psq[n % 2][:, :],
                                    wts[wi][:, ei * D + fi * 128: ei * D + fi * 128 + 128],
                                    xfm[:, ei * 512:(ei + 1) * 512],
                                    start=(ei == 0), stop=(ei == 7))
                                if ei == 7:
                                    ins.then_inc(S("pB"), 1)
                    t.wait_ge(S("sEv"), g * 24 + 24)
                    for k2 in range(4):
                        for fi in range(NE):
                            kk = g * 32 + k2 * 8 + fi
                            if kk >= 2:
                                t.wait_ge(S("sV2"), kk - 1)
                            t.transpose(pstv[kk % 2][:, 0:128],
                                        vfm[:, fi * 512 + k2 * 128: fi * 512 + (k2 + 1) * 128],
                                        Pc["ident_bf"][:, :]).then_inc(S("pV"), 1)
        S.release()



def _b_chunks(jt):
    j0 = jt * 128
    out = []
    for ac in range(j0 // 512, 4):
        lo = max(j0, ac * 512)
        hi = (ac + 1) * 512
        out.append((ac, lo, hi))
    return out


def _b_bias_segs(jt):
    j0 = jt * 128
    segs = []
    end = min(j0 + 256, 2048)
    c0 = j0 // 512
    s1hi = min(end, (c0 + 1) * 512)
    segs.append((c0, j0, s1hi))
    if end > s1hi and c0 + 1 <= 3:
        segs.append((c0 + 1, s1hi, end))
    return segs


_B_NSC = sum(len(_b_chunks(jt)) for jt in range(16))
_B_NBI = sum(len(_b_bias_segs(jt)) for jt in range(16))
_B_NMEM = sum(1 for jt in range(16) if (jt * 128) % 512)


def _b_sc_idx():
    m = {}
    i = 0
    for jt in range(16):
        for (ac, lo, hi) in _b_chunks(jt):
            m[(jt, ac)] = i
            i += 1
    return m


_B_IDX = _b_sc_idx()
_B_CUMAV = [0] * 17
for _jt in range(16):
    _B_CUMAV[_jt + 1] = _B_CUMAV[_jt] + len(_b_chunks(_jt))


def emit_phase_b(nc, es, Dr, Pc):
    with ExitStack() as wes:
        wo_t = emit_load_wT(nc, wes, [Dr["wo0"]], "bw", Pc["ident_bf"])[0]
        kh = [wes.enter_context(nc.sbuf_tensor(f"b_kh{i}", [64, T], BF16))
              for i in range(2)]
        qh = [wes.enter_context(nc.sbuf_tensor(f"b_qh{i}", [64, T], BF16))
              for i in range(2)]
        v_sb = wes.enter_context(nc.sbuf_tensor("b_vsb", [128, 16 * D], BF16))
        v65 = wes.enter_context(nc.sbuf_tensor("b_v65", [128, 16 * 65], BF16))
        A = [wes.enter_context(nc.sbuf_tensor(f"b_A{i}", [128, T], BF16))
             for i in range(2)]
        ao = wes.enter_context(nc.sbuf_tensor("b_ao", [128, 8 * T], BF16))
        ao_tmp = [wes.enter_context(nc.sbuf_tensor(f"b_aot{i}", [64, T], BF16))
                  for i in range(2)]
        tcorr = wes.enter_context(nc.sbuf_tensor("b_tcorr", [128, 16 * 256], F32))
        relf = wes.enter_context(nc.sbuf_tensor("b_relf", [32, 16], F32))
        relb = wes.enter_context(nc.sbuf_tensor("b_relb", [32, 16], BF16))
        sel = [wes.enter_context(nc.sbuf_tensor(f"b_sel{i}", [32, 4096], BF16))
               for i in range(2)]
        tcsb = [wes.enter_context(nc.sbuf_tensor(f"b_tcsb{i}", [16, 512], F32))
                for i in range(2)]
        lrow = wes.enter_context(nc.sbuf_tensor("b_lrow", [1, T], F32))
        lrow2 = wes.enter_context(nc.sbuf_tensor("b_lrow2", [1, T], F32))
        linv = wes.enter_context(nc.sbuf_tensor("b_linv", [1, T], F32))
        bcb = [wes.enter_context(nc.sbuf_tensor(f"b_bcb{i}", [64, 512], F32))
               for i in range(2)]
        bo_row = wes.enter_context(nc.sbuf_tensor("b_bor", [1, D], F32))
        bo_bc = wes.enter_context(nc.sbuf_tensor("b_bobc", [128, D], F32))
        xr = [wes.enter_context(nc.sbuf_tensor(f"b_xr{i}", [128, D], F32))
              for i in range(2)]
        x1t = [wes.enter_context(nc.sbuf_tensor(f"b_x1t{i}", [128, D], F32))
               for i in range(2)]
        ps_sc = [wes.enter_context(nc.psum_tensor(f"b_psc{i}", [128, 512], F32))
                 for i in range(2)]
        av_ps = wes.enter_context(nc.psum_tensor("b_av", [65, T], F32))
        wo_ps = wes.enter_context(nc.psum_tensor("b_wo", [128, D], F32))

        dbg = (wes.enter_context(nc.sbuf_tensor("b_dbg", [128, 2064], F32))
               if B_DBG else None)
        S = Sems(nc, "b")
        qK = DmaQ(S, "dK", 4)
        qVS = DmaQ(S, "dVS", 2)
        qSel = DmaQ(S, "dSel", 2)
        qTc = DmaQ(S, "dTc", 2)
        qTr = DmaQ(S, "dTr", 2)
        qX = DmaQ(S, "dX", 2)
        qR = DmaQ(S, "dR", 2)
        qAo = DmaQ(S, "dAo", 2)
        qMisc = DmaQ(S, "dM", 2)

        NSC, NBI, NMEM = _B_NSC, _B_NBI, _B_NMEM
        IDX, CUMAV = _B_IDX, _B_CUMAV

        with nc.Block() as blk:
            @blk.sync
            def _(sync):
                qMisc.inc(sync.dma_start(relf[:, :], Dr["rel"][:, :]))
                qMisc.inc(sync.dma_start(bo_row[:, :], Dr["bo0"][:, :]))
                for s in range(NSEQ):
                    for h in range(HEADS):
                        hh = s * 16 + h
                        if hh >= 2:
                            sync.wait_ge(S("peS"), NSC * (hh - 1))
                        qK.inc(sync.dma_start(kh[hh % 2][:, :],
                                              Dr["kf0"][s, h * 64:h * 64 + 64, :]))
                        qK.inc(sync.dma_start(qh[hh % 2][:, :],
                                              Dr["qf0"][s, h * 64:h * 64 + 64, :]))
                    for it in range(16):
                        wt = s * 16 + it
                        if wt >= 2:
                            sync.wait_ge(S("gR"), 2 * (wt - 1))
                        t0 = (s * 16 + it) * 128
                        qX.inc(sync.dma_start(xr[wt % 2][:, :],
                                              Dr["x"][t0:t0 + 128, :]))
                        sync.wait_ge(S("gR"), 2 * wt + 2)
                        qR.inc(sync.dma_start(Dr["r1"][t0:t0 + 128, :],
                                              x1t[wt % 2][:, :]))
                if B_DBG:
                    sync.wait_ge(S("vDbg"), 1)
                    sync.dma_start(Dr["dbg"][:, :], dbg[:, :]).then_inc(S("dDbg"), 16)
                    sync.wait_ge(S("dDbg"), 16)
                qR.wait(sync, NSEQ * 16 - 1)
                qR.wait(sync, NSEQ * 16 - 2)

            @blk.gpsimd
            def _(g):
                for c in range(8):
                    if c >= 2:
                        g.wait_ge(S("peTc"), 8 * (c - 1))
                    qSel.inc(g.dma_start(
                        sel[c % 2][:, :],
                        Pc["selmat_dram"][:, c * 4096:(c + 1) * 4096]))
                for s in range(NSEQ):
                    g.wait_ge(S("peA"), NSC * 16 * s)
                    qVS.inc(g.dma_start(
                        v_sb[:, :].rearrange("p (jt f) -> p jt f", f=D),
                        Dr["vt0"][s].rearrange("(jt p) f -> p jt f", p=128)))
                    for h in range(1, HEADS, 2):
                        hh = s * 16 + h
                        oh = hh // 2
                        g.wait_ge(S("vAo"), 4 * (hh + 1))
                        qAo.inc(g.dma_start(
                            ao[64:128, (h // 2) * T:(h // 2 + 1) * T],
                            ao_tmp[oh % 2][:, :]))
                    for it in range(16):
                        wt = s * 16 + it
                        qX.wait(g, wt)
                        for half in range(2):
                            g.wait_ge(S("vW"), 2 * wt + half + 1)
                            g.tensor_tensor(
                                x1t[wt % 2][:, half * 512:(half + 1) * 512],
                                x1t[wt % 2][:, half * 512:(half + 1) * 512],
                                xr[wt % 2][:, half * 512:(half + 1) * 512],
                                ALU.add).then_inc(S("gR"), 1)

            @blk.tensor
            def _(t):
                qMisc.wait(t, 0)
                t.wait_ge(S("vRel"), 1)
                for c in range(64):
                    qSel.wait(t, c // 8)
                    if c >= 1:
                        t.wait_ge(S("sTc"), c)
                    t.matmul(ps_sc[0][0:16, 0:512], relb[:, :],
                             sel[(c // 8) % 2][:, (c % 8) * 512:(c % 8 + 1) * 512],
                             start=True, stop=True).then_inc(S("peTc"), 1)
                qMisc.wait(t, 1)
                t.wait_ge(S("sTc"), 64)
                for half in range(2):
                    t.matmul(ps_sc[1][:, 0:512], Pc["ones_f32"][0:1, :],
                             bo_row[0:1, half * 512:(half + 1) * 512],
                             start=True, stop=True).then_inc(S("peBo"), 1)
                    t.wait_ge(S("sBo"), half + 1)
                t.wait_ge(S("vOne"), 16)
                for s in range(NSEQ):
                    for h in range(HEADS):
                        hh = s * 16 + h
                        qK.wait(t, 2 * hh)
                        qK.wait(t, 2 * hh + 1)
                        if hh >= 1:
                            t.wait_ge(S("sBc"), 4 * hh)
                            t.wait_ge(S("vAo"), 4 * hh)
                        for jt in range(16):
                            j0 = jt * 128
                            for (ac, lo, hi) in _b_chunks(jt):
                                idx = NSC * hh + IDX[(jt, ac)]
                                if idx >= 2:
                                    t.wait_ge(S("sE"), idx - 1)
                                t.matmul(ps_sc[idx % 2][:, 0:hi - lo],
                                         kh[hh % 2][:, j0:j0 + 128],
                                         qh[hh % 2][:, lo:hi],
                                         start=True, stop=True
                                         ).then_inc(S("peS"), 1)
                            for (ac, lo2, hi2) in _b_chunks(jt):
                                eidx = NSC * hh + IDX[(jt, ac)]
                                t.wait_ge(S("sE"), eidx + 1)
                                if (jt * 128) % 512 and ac == jt // 4:
                                    t.wait_ge(S("vM"), NMEM * hh
                                              + sum(1 for j in range(jt + 1)
                                                    if (j * 128) % 512))
                                t.wait_ge(S("vC"), 16 * hh + jt + 1)
                                t.matmul(av_ps[:, ac * 512:(ac + 1) * 512],
                                         v65[:, jt * 65:jt * 65 + 65],
                                         A[jt % 2][:, ac * 512:(ac + 1) * 512],
                                         start=(jt == 0),
                                         stop=(jt == min(15, 4 * ac + 3))
                                         ).then_inc(S("peA"), 1)
                        t.wait_ge(S("sL2"), hh + 1)
                        for c in range(4):
                            t.wait_ge(S("sBc"), 4 * hh + c)
                            t.matmul(ps_sc[0][0:64, 0:512],
                                     Pc["ones_f32"][0:1, 0:64],
                                     linv[0:1, c * 512:(c + 1) * 512],
                                     start=True, stop=True).then_inc(S("peBc"), 1)
                    t.wait_ge(S("vAo"), 64 * (s + 1))
                    qAo.wait(t, 8 * (s + 1) - 1)
                    qAo.wait(t, 8 * (s + 1) - 2)
                    for it in range(16):
                        wt = s * 16 + it
                        if wt >= 1:
                            t.wait_ge(S("vW"), 2 * wt)
                        for half in range(2):
                            for ec in range(NE):
                                ins = t.matmul(
                                    wo_ps[:, half * 512:(half + 1) * 512],
                                    ao[:, ec * T + it * 128: ec * T + (it + 1) * 128],
                                    wo_t[:, ec * D + half * 512: ec * D + (half + 1) * 512],
                                    start=(ec == 0), stop=(ec == 7))
                            ins.then_inc(S("peW"), 1)

            @blk.scalar
            def _(s_):
                for c in range(64):
                    s_.wait_ge(S("peTc"), c + 1)
                    if c >= 2:
                        qTc.wait(s_, c - 2)
                    s_.copy(tcsb[c % 2][:, :], ps_sc[0][0:16, 0:512]
                            ).then_inc(S("sTc"), 1)
                    s_.wait_ge(S("sTc"), c + 1)
                    qTc.inc(s_.dma_start(Dr["tcorr_d"][:, c * 512:(c + 1) * 512],
                                         tcsb[c % 2][:, :]))
                for h in range(16):
                    qTc.wait(s_, 63)
                    qTc.wait(s_, 62)
                    qTr.inc(s_.dma_start(
                        tcorr[:, h * 256:(h + 1) * 256],
                        Dr["tcorr_d"][h].rearrange("(p i) -> p i", p=128)))
                for half in range(2):
                    s_.wait_ge(S("peBo"), half + 1)
                    s_.copy(bo_bc[:, half * 512:(half + 1) * 512],
                            ps_sc[1][:, 0:512]).then_inc(S("sBo"), 1)
                for s in range(NSEQ):
                    for h in range(HEADS):
                        hh = s * 16 + h
                        nbi = 0
                        for jt in range(16):
                            segs = _b_bias_segs(jt)
                            for (ac, lo, hi) in _b_chunks(jt):
                                idx = NSC * hh + IDX[(jt, ac)]
                                s_.wait_ge(S("peS"), idx + 1)
                                has_bias = any(a == ac for (a, _, _) in segs)
                                if has_bias:
                                    nbi += sum(1 for (a, _, _) in segs if a == ac)
                                    s_.wait_ge(S("vB"), NBI * hh + nbi)
                                if jt >= 2 and ac == jt // 4:
                                    s_.wait_ge(S("peA"),
                                               NSC * hh + CUMAV[jt - 1])
                                s_.activation(A[jt % 2][:, lo:hi],
                                              ps_sc[idx % 2][:, 0:hi - lo],
                                              AF.Exp).then_inc(S("sE"), 1)
                        s_.wait_ge(S("peA"), NSC * (hh + 1))
                        s_.activation(lrow[0:1, :], av_ps[64:65, :],
                                      AF.Ln, scale=2.0 ** -44).then_inc(S("sL"), 1)
                        s_.wait_ge(S("sL"), 2 * hh + 1)
                        s_.activation(lrow2[0:1, :], lrow[0:1, :], AF.Relu,
                                      bias=Pc["c44"][0:1, :]).then_inc(S("sL"), 1)
                        s_.wait_ge(S("sL"), 2 * hh + 2)
                        s_.activation(linv[0:1, :], lrow2[0:1, :], AF.Exp,
                                      scale=-1.0,
                                      bias=Pc["cLE"][0:1, :]).then_inc(S("sL2"), 1)
                        for c in range(4):
                            s_.wait_ge(S("peBc"), 4 * hh + c + 1)
                            if 4 * hh + c >= 2:
                                s_.wait_ge(S("vAo"), 4 * hh + c - 1)
                            s_.copy(bcb[c % 2][:, :], ps_sc[0][0:64, 0:512]
                                    ).then_inc(S("sBc"), 1)

            @blk.vector
            def _(v):
                for jt in range(16):
                    v.memset(v65[:, jt * 65 + 64: jt * 65 + 65],
                             1.0).then_inc(S("vOne"), 1)
                qMisc.wait(v, 0)
                v.tensor_scalar(relb[:, :], relf[:, :], SCALE, None,
                                ALU.mult).then_inc(S("vRel"), 1)
                qTr.wait(v, 14)
                qTr.wait(v, 15)
                for h in range(16):
                    if B_NOBIAS:
                        v.tensor_copy(tcorr[:, h * 256:(h + 1) * 256],
                                      Pc["maskband"][:, :]).then_inc(S("vTc"), 1)
                    else:
                        v.tensor_tensor(tcorr[:, h * 256:(h + 1) * 256],
                                        tcorr[:, h * 256:(h + 1) * 256],
                                        Pc["maskband"][:, :], ALU.add
                                        ).then_inc(S("vTc"), 1)
                for s in range(NSEQ):
                    for h in range(HEADS):
                        hh = s * 16 + h
                        if h == 0:
                            qVS.wait(v, s)
                        if hh >= 1:
                            v.wait_ge(S("peA"), NSC * hh)
                        for jt in range(16):
                            v.tensor_copy(v65[:, jt * 65:jt * 65 + 64],
                                          v_sb[:, jt * D + h * 64: jt * D + h * 64 + 64]
                                          ).then_inc(S("vC"), 1)
                        if hh == 0:
                            v.wait_ge(S("vTc"), 16)
                        nbi = 0
                        nmem = 0
                        for jt in range(16):
                            j0 = jt * 128
                            if jt >= 2:
                                v.wait_ge(S("peA"), NSC * hh + CUMAV[jt - 1])
                            if j0 % 512:
                                nmem += 1
                                c0 = j0 // 512
                                v.memset(A[jt % 2][:, c0 * 512:j0], 0.0
                                         ).then_inc(S("vM"), 1)
                            for (ac, blo, bhi) in _b_bias_segs(jt):
                                idx = NSC * hh + IDX[(jt, ac)]
                                clo = max(j0, ac * 512)
                                v.wait_ge(S("peS"), idx + 1)
                                nbi += 1
                                v.tensor_tensor(
                                    ps_sc[idx % 2][:, blo - clo:bhi - clo],
                                    ps_sc[idx % 2][:, blo - clo:bhi - clo],
                                    tcorr[:, h * 256 + blo - j0: h * 256 + bhi - j0],
                                    ALU.add).then_inc(S("vB"), 1)
                        oh = hh // 2
                        for c in range(4):
                            v.wait_ge(S("sBc"), 4 * hh + c + 1)
                            if hh >= 1 and c == 0:
                                v.wait_ge(S("peW"), 32 * s)
                            if h % 2 and c == 0 and oh >= 2:
                                qAo.wait(v, oh - 2)
                            dst = (ao[0:64, (h // 2) * T + c * 512:
                                       (h // 2) * T + (c + 1) * 512]
                                   if h % 2 == 0 else
                                   ao_tmp[oh % 2][:, c * 512:(c + 1) * 512])
                            v.tensor_tensor(
                                dst, av_ps[0:64, c * 512:(c + 1) * 512],
                                bcb[c % 2][:, :], ALU.mult).then_inc(S("vAo"), 1)
                        if hh == 1 and B_DBG:
                            v.tensor_copy(dbg[0:64, 2048:2056], av_ps[0:64, 118:126])
                            v.tensor_copy(dbg[64:65, 2056:2060], av_ps[64:65, 119:123])
                            v.tensor_copy(dbg[0:1, 2060:2064], linv[0:1, 119:123])
                        if hh == 0 and B_DBG:
                            v.tensor_copy(dbg[:, 0:256], tcorr[:, 0:256])
                            v.wait_ge(S("sE"), 4)
                            v.tensor_copy(dbg[:, 256:768], A[0][:, 0:512])
                            v.tensor_copy(dbg[0:64, 768:1280], av_ps[0:64, 0:512])
                            v.tensor_copy(dbg[0:1, 1280:1792], linv[0:1, 0:512])
                            v.tensor_copy(dbg[0:1, 1792:2048], lrow[0:1, 0:256])
                            v.tensor_copy(dbg[64:65, 1280:1281], av_ps[64:65, 0:1]
                                          ).then_inc(S("vDbg"), 1)
                    for it in range(16):
                        wt = s * 16 + it
                        if wt >= 2:
                            qR.wait(v, wt - 2)
                        for half in range(2):
                            v.wait_ge(S("peW"), 2 * wt + half + 1)
                            v.scalar_tensor_tensor(
                                x1t[wt % 2][:, half * 512:(half + 1) * 512],
                                wo_ps[:, half * 512:(half + 1) * 512],
                                1.0, bo_bc[:, half * 512:(half + 1) * 512],
                                ALU.mult, ALU.add).then_inc(S("vW"), 1)
                    if s == 0 and B_DBG:
                        v.tensor_copy(dbg[:, 0:2048], ao[:, 0:2048]
                                      ).then_inc(S("vDbg"), 1)
        S.release()



def emit_phase_c(nc, es, Dr, Pc):
    with ExitStack() as wes:
        rin = [wes.enter_context(nc.sbuf_tensor(f"c_rin{i}", [128, D], F32))
               for i in range(2)]
        xsq = wes.enter_context(nc.sbuf_tensor("c_xsq", [128, D], F32))
        xi = wes.enter_context(nc.sbuf_tensor("c_xi", [128, D], F32))
        t1 = wes.enter_context(nc.sbuf_tensor("c_t1", [128, D], F32))
        t2 = wes.enter_context(nc.sbuf_tensor("c_t2", [128, D], F32))
        xnb = wes.enter_context(nc.sbuf_tensor("c_xnb", [128, D], BF16))
        cols = wes.enter_context(nc.sbuf_tensor("c_cols", [128, 16], F32))
        mx = wes.enter_context(nc.sbuf_tensor("c_mx", [128, NT], F32))
        xfm = wes.enter_context(nc.sbuf_tensor("c_xfm", [128, NE * 512], BF16))
        gb = wes.enter_context(nc.sbuf_tensor("c_gb", [128, D], F32))
        bb = wes.enter_context(nc.sbuf_tensor("c_bb", [128, D], F32))
        gsc = wes.enter_context(nc.sbuf_tensor("c_gsc", [1, 8], F32))
        lrow = wes.enter_context(nc.sbuf_tensor("c_lrow", [1, D], F32))
        wstg = [wes.enter_context(nc.sbuf_tensor(f"c_wstg{i}", [128, D], F32))
                for i in range(2)]
        wqt = [wes.enter_context(nc.sbuf_tensor(f"c_wqt{i}", [128, D], BF16))
               for i in range(2)]
        wcols = wes.enter_context(nc.sbuf_tensor("c_wcols", [128, 8], F32))
        ones128 = wes.enter_context(nc.sbuf_tensor("c_ones", [128, 1], F32))
        pst = [wes.enter_context(nc.psum_tensor(f"c_pst{i}", [128, 1024], BF16))
               for i in range(2)]
        psb = wes.enter_context(nc.psum_tensor("c_psb", [128, 512], F32))
        S = Sems(nc, "c")
        qC = DmaQ(S, "dC", 2)
        qW = DmaQ(S, "dW", 2)
        qN = DmaQ(S, "dN", 2)
        qQn = DmaQ(S, "dQn", 2)
        qRow = DmaQ(S, "dRow", 2)
        qG = DmaQ(S, "dG", 2)

        with nc.Block() as blk:
            @blk.sync
            def _(sync):
                qRow.inc(sync.dma_start(lrow[0:1, :], Dr["lng"][:, :]))
                for g in range(NG):
                    for k in range(4):
                        tt = g * 4 + k
                        if tt >= 2:
                            sync.wait_ge(S("lnvXi"), tt - 1)
                        t0 = tt * 128
                        qC.inc(sync.dma_start(rin[tt % 2][:, :],
                                              Dr["r1"][t0:t0 + 128, :]))
                    sync.wait_ge(S("sC"), 32 * (g + 1))
                    qN.inc(sync.dma_start(
                        Dr["xn1g"][g].rearrange("e p t -> p e t"),
                        xfm[:, :].rearrange("p (e t) -> p e t", t=512)))
                qN.wait(sync, NG - 1)
                qN.wait(sync, NG - 2)

            @blk.gpsimd
            def _(g_):
                g_.memset(ones128[:, :], 1.0)
                for tt in range(NT):
                    g_.wait_ge(S("vT2"), tt + 1)
                    if tt >= 1:
                        g_.wait_ge(S("pA"), 8 * tt)
                        g_.wait_ge(S("vD"), tt)
                    g_.tensor_tensor(xnb[:, :], t2[:, :], bb[:, :],
                                     ALU.add).then_inc(S("gXn"), 1)
                for n in range(16):
                    dc = n % 8
                    if 2 <= n < 10:
                        g_.wait_ge(S("vWc"), n - 1)
                    elif n >= 10:
                        g_.wait_ge(S("vWq"), n - 9)
                    qW.inc(g_.dma_start(wstg[n % 2][:, :],
                                        Dr["wffn"][dc * 128:(dc + 1) * 128, :]))
                g_.wait_ge(S("vBeta"), 1)
                qG.inc(g_.dma_start(Dr["beta_d"][:, :], gsc[0:1, 0:1]))
                g_.wait_ge(S("vG"), 1)
                qG.inc(g_.dma_start(Dr["cc_in"][:, :], gsc[0:1, 2:3]))
                qG.wait(g_, 1)
                g_.collective_compute(
                    "AllReduce", ALU.max,
                    replica_groups=[list(range(NCORES))],
                    ins=[Dr["cc_in"].ap().opt()],
                    outs=[Dr["cc_out"].ap().opt()]).then_inc(S("cc"), 1)
                g_.wait_ge(S("cc"), 1)
                qG.inc(g_.dma_start(Dr["gmax_d"][:, :], Dr["cc_out"][:, :]))
                qG.wait(g_, 2)
                qG.wait(g_, 0)

            @blk.scalar
            def _(s_):
                for half in range(2):
                    s_.wait_ge(S("peGB"), half + 1)
                    s_.copy(gb[:, half * 512:(half + 1) * 512],
                            psb[:, 0:512]).then_inc(S("sGBe"), 1)
                qRow.inc(s_.dma_start(lrow[0:1, :], Dr["lnb"][:, :]))
                for half in range(2):
                    s_.wait_ge(S("peGB"), half + 3)
                    s_.copy(bb[:, half * 512:(half + 1) * 512],
                            psb[:, 0:512]).then_inc(S("sGBe"), 1)
                for tt in range(NT):
                    qC.wait(s_, tt)
                    if tt >= 1:
                        s_.wait_ge(S("lnvA"), 4 * tt)
                    s_.activation(xsq[:, :], rin[tt % 2][:, :],
                                  AF.Square).then_inc(S("lnsA"), 1)
                    s_.wait_ge(S("lnvB"), 2 * tt + 1)
                    s_.activation(cols[:, 6:7], cols[:, 5:6], AF.Sqrt, scale=-1.0,
                                  bias=Pc["c1e5"]).then_inc(S("lnsB"), 1)
                    s_.wait_ge(S("lnvXi"), tt + 1)
                    s_.activation(xsq[:, :], xi[:, :],
                                  AF.Square).then_inc(S("lnsA"), 1)
                    s_.wait_ge(S("lnvB"), 2 * tt + 2)
                    s_.activation(cols[:, 14:15], cols[:, 13:14], AF.Sqrt, scale=-1.0,
                                  bias=Pc["c1e3"]).then_inc(S("lnsB"), 1)
                    for ei in range(NE):
                        kk = tt * 8 + ei
                        s_.wait_ge(S("pA"), kk + 1)
                        if tt % 4 == 0 and ei == 0 and tt >= 4:
                            qN.wait(s_, tt // 4 - 1)
                        k = tt % 4
                        s_.copy(xfm[:, ei * 512 + k * 128: ei * 512 + (k + 1) * 128],
                                pst[kk % 2][:, 0:128]).then_inc(S("sC"), 1)
                s_.wait_ge(S("peWbM"), 1)
                s_.copy(cols[:, 12:13], psb[:, 4:5]).then_inc(S("sWb"), 1)
                for dc in range(8):
                    s_.wait_ge(S("vWq"), dc + 1)
                    qQn.inc(s_.dma_start(Dr["wqn_d"][dc * 128:(dc + 1) * 128, :],
                                         wqt[dc % 2][:, :]))
                qQn.wait(s_, 7)
                qQn.wait(s_, 6)

            @blk.vector
            def _(v):
                for tt in range(NT):
                    qC.wait(v, tt)
                    v.reduce_sum(cols[:, 0:1], rin[tt % 2][:, :],
                                 axis=AX).then_inc(S("lnvA"), 1)
                    v.wait_ge(S("lnsA"), 2 * tt + 1)
                    v.reduce_sum(cols[:, 1:2], xsq[:, :],
                                 axis=AX).then_inc(S("lnvA"), 1)
                    v.wait_ge(S("lnvA"), 4 * tt + 2)
                    v.tensor_scalar(cols[:, 2:3], cols[:, 0:1], 1.0 / D, None, ALU.mult)
                    v.tensor_scalar(cols[:, 3:4], cols[:, 1:2], 1.0 / D, None,
                                    ALU.mult).then_inc(S("lnvM"), 1)
                    v.wait_ge(S("lnvM"), 2 * tt + 1)
                    v.scalar_tensor_tensor(cols[:, 5:6], cols[:, 2:3], cols[:, 2:3],
                                           cols[:, 3:4], ALU.mult,
                                           ALU.subtract).then_inc(S("lnvB"), 1)
                    v.wait_ge(S("lnsB"), 2 * tt + 1)
                    v.reciprocal(cols[:, 7:8], cols[:, 6:7]).then_inc(S("lnvC"), 1)
                    v.wait_ge(S("lnvC"), 2 * tt + 1)
                    v.tensor_scalar(xi[:, :], rin[tt % 2][:, :], cols[:, 2:3],
                                    cols[:, 7:8], ALU.subtract,
                                    ALU.mult).then_inc(S("lnvXi"), 1)
                    v.wait_ge(S("lnvXi"), tt + 1)
                    v.reduce_sum(cols[:, 8:9], xi[:, :], axis=AX).then_inc(S("lnvA"), 1)
                    v.wait_ge(S("lnsA"), 2 * tt + 2)
                    v.reduce_sum(cols[:, 9:10], xsq[:, :],
                                 axis=AX).then_inc(S("lnvA"), 1)
                    v.wait_ge(S("lnvA"), 4 * tt + 4)
                    v.tensor_scalar(cols[:, 10:11], cols[:, 8:9], 1.0 / D, None, ALU.mult)
                    v.tensor_scalar(cols[:, 11:12], cols[:, 9:10], 1.0 / D, None,
                                    ALU.mult).then_inc(S("lnvM"), 1)
                    v.wait_ge(S("lnvM"), 2 * tt + 2)
                    v.scalar_tensor_tensor(cols[:, 13:14], cols[:, 10:11], cols[:, 10:11],
                                           cols[:, 11:12], ALU.mult,
                                           ALU.subtract).then_inc(S("lnvB"), 1)
                    v.wait_ge(S("lnsB"), 2 * tt + 2)
                    v.reciprocal(cols[:, 15:16], cols[:, 14:15]).then_inc(S("lnvC"), 1)
                    v.wait_ge(S("lnvC"), 2 * tt + 2)
                    if tt == 0:
                        v.wait_ge(S("sGBe"), 4)
                    if tt >= 1:
                        v.wait_ge(S("gXn"), tt)
                    v.tensor_scalar(t1[:, :], xi[:, :], cols[:, 10:11],
                                    cols[:, 15:16], ALU.subtract,
                                    ALU.mult).then_inc(S("vT1"), 1)
                    v.wait_ge(S("vT1"), tt + 1)
                    v.tensor_tensor(t2[:, :], t1[:, :], gb[:, :],
                                    ALU.mult).then_inc(S("vT2"), 1)
                    v.wait_ge(S("gXn"), tt + 1)
                    v.reduce_max(mx[:, tt:tt + 1], xnb[:, :], axis=AX,
                                 apply_absolute_value=True).then_inc(S("vD"), 1)
                for dc in range(8):
                    qW.wait(v, dc)
                    v.reduce_sum(wcols[:, dc:dc + 1], wstg[dc % 2][:, :], axis=AX,
                                 apply_absolute_value=True).then_inc(S("vWc"), 1)
                v.wait_ge(S("vWc"), 8)
                v.reduce_sum(cols[:, 0:1], wcols[:, 0:8], axis=AX).then_inc(S("vWs"), 1)
                v.wait_ge(S("vWs"), 1)
                v.wait_ge(S("peB"), 1)
                v.tensor_scalar(gsc[0:1, 0:1], psb[0:1, 0:1], 1.0 / (D * D), None,
                                ALU.mult).then_inc(S("vBeta"), 1)
                v.wait_ge(S("vBeta"), 1)
                v.tensor_scalar(gsc[0:1, 1:2], gsc[0:1, 0:1], 1e-5, None, ALU.add
                                ).then_inc(S("vBeta2"), 1)
                v.wait_ge(S("vBeta2"), 1)
                v.reciprocal(gsc[0:1, 4:5], gsc[0:1, 1:2]).then_inc(S("vBinv"), 1)
                v.wait_ge(S("vBinv"), 1)
                for dc in range(8):
                    n = 8 + dc
                    qW.wait(v, n)
                    v.wait_ge(S("sWb"), 1)
                    if dc >= 2:
                        qQn.wait(v, dc - 2)
                    v.tensor_scalar(t1[:, :], wstg[n % 2][:, :], cols[:, 12:13],
                                    12582912.0, ALU.mult,
                                    ALU.add).then_inc(S("vWq1"), 1)
                    v.wait_ge(S("vWq1"), dc + 1)
                    v.tensor_scalar(t2[:, :], t1[:, :], 12582912.0, 1.0,
                                    ALU.subtract, ALU.min).then_inc(S("vWq2"), 1)
                    v.wait_ge(S("vWq2"), dc + 1)
                    v.tensor_scalar(wqt[dc % 2][:, :], t2[:, :], -1.0, None,
                                    ALU.max).then_inc(S("vWq"), 1)
                v.wait_ge(S("vD"), NT)
                v.reduce_max(cols[:, 0:1], mx[:, :], axis=AX).then_inc(S("vGc"), 1)
                v.wait_ge(S("vGc"), 1)
                v.wait_ge(S("peG"), 1)
                v.reduce_max(gsc[0:1, 2:3], psb[0:1, 0:128], axis=AX
                             ).then_inc(S("vGr"), 1)
                v.wait_ge(S("vGr"), 1)
                v.tensor_copy(gsc[0:1, 3:4], gsc[0:1, 2:3]).then_inc(S("vG"), 1)

            @blk.tensor
            def _(t):
                qRow.wait(t, 0)
                for half in range(2):
                    t.matmul(psb[:, 0:512], Pc["ones_f32"][0:1, :],
                             lrow[0:1, half * 512:(half + 1) * 512],
                             start=True, stop=True).then_inc(S("peGB"), 1)
                    t.wait_ge(S("sGBe"), half + 1)
                qRow.wait(t, 1)
                for half in range(2):
                    t.matmul(psb[:, 0:512], Pc["ones_f32"][0:1, :],
                             lrow[0:1, half * 512:(half + 1) * 512],
                             start=True, stop=True).then_inc(S("peGB"), 1)
                    t.wait_ge(S("sGBe"), half + 3)
                for tt in range(NT):
                    t.wait_ge(S("gXn"), tt + 1)
                    for ei in range(NE):
                        kk = tt * 8 + ei
                        if kk >= 2:
                            t.wait_ge(S("sC"), kk - 1)
                        t.transpose(pst[kk % 2][:, 0:128],
                                    xnb[:, ei * 128:(ei + 1) * 128],
                                    Pc["ident_bf"][:, :]).then_inc(S("pA"), 1)
                t.wait_ge(S("vWs"), 1)
                t.matmul(psb[0:1, 0:1], cols[:, 0:1], ones128[:, :],
                         start=True, stop=True).then_inc(S("peB"), 1)
                t.wait_ge(S("vBinv"), 1)
                t.matmul(psb[:, 4:5], Pc["ones_f32"][0:1, :], gsc[0:1, 4:5],
                         start=True, stop=True).then_inc(S("peWbM"), 1)
                t.wait_ge(S("vGc"), 1)
                t.wait_ge(S("sWb"), 1)
                t.transpose(psb[0:1, 0:128], cols[:, 0:1],
                            Pc["ident_f32"][:, :]).then_inc(S("peG"), 1)
        S.release()



def emit_phase_d(nc, es, Dr, Pc):
    with ExitStack() as wes:
        xq = [wes.enter_context(nc.sbuf_tensor(f"d_xq{i}", [128, NE * 512], BF16))
              for i in range(2)]
        wq_sb = wes.enter_context(nc.sbuf_tensor("d_wq", [128, NE * D], BF16))
        rr = [wes.enter_context(nc.sbuf_tensor(f"d_rr{i}", [128, D], F32))
              for i in range(2)]
        x2t = [wes.enter_context(nc.sbuf_tensor(f"d_x2{i}", [128, D], F32))
               for i in range(2)]
        xsq = wes.enter_context(nc.sbuf_tensor("d_xsq", [128, D], F32))
        xnb = wes.enter_context(nc.sbuf_tensor("d_xnb", [128, D], BF16))
        cols = wes.enter_context(nc.sbuf_tensor("d_cols", [128, 8], F32))
        gsc = wes.enter_context(nc.sbuf_tensor("d_gsc", [1, 8], F32))
        scol = wes.enter_context(nc.sbuf_tensor("d_scol", [128, 2], F32))
        xfm = wes.enter_context(nc.sbuf_tensor("d_xfm", [128, NE * 512], BF16))
        ps = [wes.enter_context(nc.psum_tensor(f"d_ps{i}", [128, 512], F32))
              for i in range(2)]
        pst = [wes.enter_context(nc.psum_tensor(f"d_pst{i}", [128, 1024], BF16))
               for i in range(2)]
        psb = wes.enter_context(nc.psum_tensor("d_psb", [128, 512], F32))
        S = Sems(nc, "d")
        qX = DmaQ(S, "dX", 2)
        qR = DmaQ(S, "dR", 2)
        qO = DmaQ(S, "dO", 2)
        qN = DmaQ(S, "dN", 2)
        qMs = DmaQ(S, "dMs", 2)
        qMw = DmaQ(S, "dMw", 2)

        with nc.Block() as blk:
            @blk.sync
            def _(sync):
                qMs.inc(sync.dma_start(gsc[0:1, 0:1], Dr["gmax_d"][:, :]))
                qMs.inc(sync.dma_start(gsc[0:1, 1:2], Dr["beta_d"][:, :]))
                for g in range(NG):
                    if g >= 2:
                        sync.wait_ge(S("peM"), 8 * (g - 1))
                    qX.inc(sync.dma_start(
                        xq[g % 2][:, :].rearrange("p (e t) -> p e t", t=512),
                        Dr["xn1g"][g].rearrange("e p t -> p e t")))
                    for k in range(4):
                        tt = g * 4 + k
                        if tt >= 2:
                            sync.wait_ge(S("vE"), 2 * (tt - 1))
                        t0 = tt * 128
                        qR.inc(sync.dma_start(rr[tt % 2][:, :],
                                              Dr["r1"][t0:t0 + 128, :]))
                        sync.wait_ge(S("vE"), 2 * tt + 2)
                        qO.inc(sync.dma_start(Dr["r2"][t0:t0 + 128, :],
                                              x2t[tt % 2][:, :]))
                    sync.wait_ge(S("sC"), 32 * (g + 1))
                    qN.inc(sync.dma_start(
                        Dr["xn2g"][g].rearrange("e p t -> p e t"),
                        xfm[:, :].rearrange("p (e t) -> p e t", t=512)))
                qN.wait(sync, NG - 1)
                qN.wait(sync, NG - 2)
                qO.wait(sync, NT - 1)
                qO.wait(sync, NT - 2)

            @blk.gpsimd
            def _(g_):
                for dc in range(8):
                    qMw.inc(g_.dma_start(wq_sb[:, dc * D:(dc + 1) * D],
                                         Dr["wqn_d"][dc * 128:(dc + 1) * 128, :]))

            @blk.vector
            def _(v):
                qMs.wait(v, 0)
                qMs.wait(v, 1)
                v.reciprocal(gsc[0:1, 2:3], gsc[0:1, 0:1]).then_inc(S("vS"), 1)
                v.wait_ge(S("vS"), 1)
                v.tensor_scalar(gsc[0:1, 3:4], gsc[0:1, 2:3], 128.0, None, ALU.mult)
                v.tensor_tensor(gsc[0:1, 4:5], gsc[0:1, 0:1], gsc[0:1, 1:2],
                                ALU.mult).then_inc(S("vS1b"), 1)
                v.wait_ge(S("vS1b"), 1)
                v.tensor_scalar(gsc[0:1, 5:6], gsc[0:1, 4:5], 1.0 / 128.0, None,
                                ALU.mult).then_inc(S("vS2"), 1)
                for g in range(NG):
                    qX.wait(v, g)
                    if g == 0:
                        v.wait_ge(S("sSc"), 1)
                    v.tensor_scalar(xq[g % 2][:, :], xq[g % 2][:, :], scol[:, 0:1],
                                    127.99999, ALU.mult,
                                    ALU.min).then_inc(S("vQa"), 1)
                    v.wait_ge(S("vQa"), g + 1)
                    v.tensor_scalar(xq[g % 2][:, :], xq[g % 2][:, :], -127.99999, None,
                                    ALU.max).then_inc(S("vQ"), 1)
                    for k in range(4):
                        tt = g * 4 + k
                        if tt >= 2:
                            qO.wait(v, tt - 2)
                            v.wait_ge(S("lnsA"), tt - 1)
                        for half in range(2):
                            n = 2 * tt + half
                            v.wait_ge(S("peM"), n + 1)
                            qR.wait(v, tt)
                            v.scalar_tensor_tensor(
                                x2t[tt % 2][:, half * 512:(half + 1) * 512],
                                ps[n % 2][:, :], scol[:, 1:2],
                                rr[tt % 2][:, half * 512:(half + 1) * 512],
                                ALU.mult, ALU.add).then_inc(S("vE"), 1)
                        v.wait_ge(S("vE"), 2 * tt + 2)
                        v.reduce_sum(cols[:, 0:1], x2t[tt % 2][:, :],
                                     axis=AX).then_inc(S("lnvA"), 1)
                        v.wait_ge(S("lnsA"), tt + 1)
                        v.reduce_sum(cols[:, 1:2], xsq[:, :],
                                     axis=AX).then_inc(S("lnvA"), 1)
                        v.wait_ge(S("lnvA"), 2 * tt + 2)
                        v.tensor_scalar(cols[:, 2:3], cols[:, 0:1], 1.0 / D, None, ALU.mult)
                        v.tensor_scalar(cols[:, 3:4], cols[:, 1:2], 1.0 / D, None,
                                        ALU.mult).then_inc(S("lnvM"), 1)
                        v.wait_ge(S("lnvM"), tt + 1)
                        v.scalar_tensor_tensor(cols[:, 5:6], cols[:, 2:3], cols[:, 2:3],
                                               cols[:, 3:4], ALU.mult,
                                               ALU.subtract).then_inc(S("lnvB"), 1)
                        v.wait_ge(S("lnsB"), tt + 1)
                        v.reciprocal(cols[:, 7:8], cols[:, 6:7]).then_inc(S("lnvC"), 1)
                        v.wait_ge(S("lnvC"), tt + 1)
                        if tt >= 1:
                            v.wait_ge(S("pA"), 8 * tt)
                        v.tensor_scalar(xnb[:, :], x2t[tt % 2][:, :], cols[:, 2:3],
                                        cols[:, 7:8], ALU.subtract,
                                        ALU.mult).then_inc(S("vD"), 1)

            @blk.scalar
            def _(s_):
                s_.wait_ge(S("peSc"), 2)
                s_.copy(scol[:, 0:2], psb[:, 0:2]).then_inc(S("sSc"), 1)
                for tt in range(NT):
                    v_n = 2 * tt
                    s_.wait_ge(S("vE"), 2 * tt + 2)
                    if tt >= 1:
                        s_.wait_ge(S("lnvA"), 2 * tt)
                    s_.activation(xsq[:, :], x2t[tt % 2][:, :],
                                  AF.Square).then_inc(S("lnsA"), 1)
                    s_.wait_ge(S("lnvB"), tt + 1)
                    s_.activation(cols[:, 6:7], cols[:, 5:6], AF.Sqrt, scale=-1.0,
                                  bias=Pc["c1e5"]).then_inc(S("lnsB"), 1)
                    for ei in range(NE):
                        kk = tt * 8 + ei
                        s_.wait_ge(S("pA"), kk + 1)
                        if tt % 4 == 0 and ei == 0 and tt >= 4:
                            qN.wait(s_, tt // 4 - 1)
                        k = tt % 4
                        s_.copy(xfm[:, ei * 512 + k * 128: ei * 512 + (k + 1) * 128],
                                pst[kk % 2][:, 0:128]).then_inc(S("sC"), 1)

            @blk.tensor
            def _(t):
                t.wait_ge(S("vS2"), 1)
                t.matmul(psb[:, 0:1], Pc["ones_f32"][0:1, :], gsc[0:1, 3:4],
                         start=True, stop=True)
                t.matmul(psb[:, 1:2], Pc["ones_f32"][0:1, :], gsc[0:1, 5:6],
                         start=True, stop=True).then_inc(S("peSc"), 2)
                qMw.wait(t, 6)
                qMw.wait(t, 7)
                for g in range(NG):
                    t.wait_ge(S("vQ"), g + 1)
                    for k in range(4):
                        tt = g * 4 + k
                        for half in range(2):
                            n = 2 * tt + half
                            if n >= 2:
                                t.wait_ge(S("vE"), n - 1)
                            for dc in range(NE):
                                ins = t.matmul(
                                    ps[n % 2][:, :],
                                    xq[g % 2][:, dc * 512 + k * 128: dc * 512 + (k + 1) * 128],
                                    wq_sb[:, dc * D + half * 512: dc * D + (half + 1) * 512],
                                    start=(dc == 0), stop=(dc == 7))
                            ins.then_inc(S("peM"), 1)
                        t.wait_ge(S("vD"), tt + 1)
                        for ei in range(NE):
                            kk = tt * 8 + ei
                            if kk >= 2:
                                t.wait_ge(S("sC"), kk - 1)
                            t.transpose(pst[kk % 2][:, 0:128],
                                        xnb[:, ei * 128:(ei + 1) * 128],
                                        Pc["ident_bf"][:, :]).then_inc(S("pA"), 1)
        S.release()



def emit_phase_e(nc, es, Dr, Pc):
    with ExitStack() as wes:
        wts = emit_load_wT(nc, wes, [Dr["wq1"], Dr["wk1"], Dr["wv1"]], "ew",
                           Pc["ident_bf"])
        bcols = emit_load_bias_cols(nc, wes, [Dr["bq1"], Dr["bk1"], Dr["bv1"]],
                                    "eb", Pc["ident_f32"])
        xg = [wes.enter_context(nc.sbuf_tensor(f"e_xg{i}", [128, NE * 512], BF16))
              for i in range(2)]
        qt = [wes.enter_context(nc.sbuf_tensor(f"e_qt{i}", [128, 512], BF16))
              for i in range(2)]
        vfm = wes.enter_context(nc.sbuf_tensor("e_vfm", [128, NE * 512], BF16))
        vtm = [wes.enter_context(nc.sbuf_tensor(f"e_vtm{i}", [128, D], BF16))
               for i in range(2)]
        pstv = [wes.enter_context(nc.psum_tensor(f"e_pstv{i}", [128, 1024], BF16))
                for i in range(2)]
        psq = [wes.enter_context(nc.psum_tensor(f"e_psq{i}", [128, 512], F32))
               for i in range(2)]
        S = Sems(nc, "e")
        qX = DmaQ(S, "dX", 2)
        qQ = DmaQ(S, "dQ", 2)
        qV = DmaQ(S, "dV", 2)

        with nc.Block() as blk:
            @blk.sync
            def _(sync):
                for g in range(NG):
                    sq = g // 4
                    tg0 = (g % 4) * 512
                    if g >= 2:
                        sync.wait_ge(S("pB"), 24 * (g - 1))
                    qX.inc(sync.dma_start(
                        xg[g % 2][:, :].rearrange("p (e t) -> p e t", t=512),
                        Dr["xn2g"][g].rearrange("e p t -> p e t")))
                    for wi in range(2):
                        for fi in range(NE):
                            sync.wait_ge(S("sEv"), g * 24 + wi * 8 + fi + 1)
                            m = g * 16 + wi * 8 + fi + 1
                            dst = Dr["qf1"] if wi == 0 else Dr["kf1"]
                            qQ.inc(sync.dma_start(
                                dst[sq, fi * 128:(fi + 1) * 128, tg0:tg0 + 512],
                                qt[m % 2][:, :]))
                    for k2 in range(4):
                        sync.wait_ge(S("sV2"), g * 32 + k2 * 8 + 8)
                        n = g * 4 + k2 + 1
                        qV.inc(sync.dma_start(
                            Dr["vt1"][sq, tg0 + k2 * 128: tg0 + (k2 + 1) * 128, :],
                            vtm[n % 2][:, :]))
                qQ.wait(sync, NG * 16 - 1)
                qQ.wait(sync, NG * 16 - 2)
                qV.wait(sync, NG * 4 - 1)
                qV.wait(sync, NG * 4 - 2)

            @blk.scalar
            def _(s_):
                for g in range(NG):
                    for wi in range(3):
                        for fi in range(NE):
                            n = g * 24 + wi * 8 + fi + 1
                            s_.wait_ge(S("pB"), n)
                            if wi < 2:
                                m = g * 16 + wi * 8 + fi + 1
                                if m > 2:
                                    qQ.wait(s_, m - 3)
                                s_.activation(qt[m % 2][:, :], psq[n % 2][:, :],
                                              AF.Identity,
                                              bias=bcols[wi][:, fi:fi + 1]
                                              ).then_inc(S("sEv"), 1)
                            else:
                                if g >= 1 and fi == 0:
                                    s_.wait_ge(S("pV"), 32 * g)
                                s_.activation(vfm[:, fi * 512:(fi + 1) * 512],
                                              psq[n % 2][:, :], AF.Identity,
                                              bias=bcols[2][:, fi:fi + 1]
                                              ).then_inc(S("sEv"), 1)
                    for k2 in range(4):
                        for fi in range(NE):
                            kk = g * 32 + k2 * 8 + fi
                            s_.wait_ge(S("pV"), kk + 1)
                            n = g * 4 + k2 + 1
                            if n > 2 and fi == 0:
                                qV.wait(s_, n - 3)
                            s_.copy(vtm[n % 2][:, fi * 128:(fi + 1) * 128],
                                    pstv[kk % 2][:, 0:128]).then_inc(S("sV2"), 1)

            @blk.tensor
            def _(t):
                for g in range(NG):
                    qX.wait(t, g)
                    for wi in range(3):
                        for fi in range(NE):
                            n = g * 24 + wi * 8 + fi + 1
                            for ei in range(NE):
                                if ei == 0 and n > 2:
                                    t.wait_ge(S("sEv"), n - 2)
                                ins = t.matmul(
                                    psq[n % 2][:, :],
                                    wts[wi][:, ei * D + fi * 128: ei * D + fi * 128 + 128],
                                    xg[g % 2][:, ei * 512:(ei + 1) * 512],
                                    start=(ei == 0), stop=(ei == 7))
                                if ei == 7:
                                    ins.then_inc(S("pB"), 1)
                    t.wait_ge(S("sEv"), g * 24 + 24)
                    for k2 in range(4):
                        for fi in range(NE):
                            kk = g * 32 + k2 * 8 + fi
                            if kk >= 2:
                                t.wait_ge(S("sV2"), kk - 1)
                            t.transpose(pstv[kk % 2][:, 0:128],
                                        vfm[:, fi * 512 + k2 * 128: fi * 512 + (k2 + 1) * 128],
                                        Pc["ident_bf"][:, :]).then_inc(S("pV"), 1)
        S.release()




def emit_phase_f(nc, es, Dr, Pc):
    NAV = 32
    with ExitStack() as wes:
        k2 = wes.enter_context(nc.sbuf_tensor("f_k2", [128, NE * T], BF16))
        q2 = wes.enter_context(nc.sbuf_tensor("f_q2", [128, NE * T], BF16))
        v2 = wes.enter_context(nc.sbuf_tensor("f_v2", [128, 16 * D], BF16))
        Aall = wes.enter_context(nc.sbuf_tensor("f_A", [128, 16 * T], BF16))
        lcol = wes.enter_context(nc.sbuf_tensor("f_lcol", [128, 2], F32))
        rr = [wes.enter_context(nc.sbuf_tensor(f"f_rr{i}", [128, D], F32))
              for i in range(2)]
        x3t = [wes.enter_context(nc.sbuf_tensor(f"f_x3{i}", [128, D], F32))
               for i in range(2)]
        xsq = wes.enter_context(nc.sbuf_tensor("f_xsq", [128, D], F32))
        xnb = wes.enter_context(nc.sbuf_tensor("f_xnb", [128, D], BF16))
        cols = wes.enter_context(nc.sbuf_tensor("f_cols", [128, 8], F32))
        xfm = wes.enter_context(nc.sbuf_tensor("f_xfm", [128, NE * 512], BF16))
        ps_sc = [wes.enter_context(nc.psum_tensor(f"f_psc{i}", [128, 512], F32))
                 for i in range(2)]
        ps_av = [wes.enter_context(nc.psum_tensor(f"f_av{i}", [128, D], F32))
                 for i in range(2)]
        pst = [wes.enter_context(nc.psum_tensor(f"f_pst{i}", [128, 1024], BF16))
               for i in range(2)]
        S = Sems(nc, "f")
        qL = DmaQ(S, "dL", 4)
        qR = DmaQ(S, "dR", 2)
        qO = DmaQ(S, "dO", 2)
        qN = DmaQ(S, "dN", 2)

        def chunks(jt):
            j0 = jt * 128
            out = []
            for ac in range(j0 // 512, 4):
                lo = max(j0, ac * 512)
                out.append((lo, (ac + 1) * 512))
            return out

        NCH = [len(chunks(jt)) for jt in range(16)]
        CUM = [0] * 17
        for jt in range(16):
            CUM[jt + 1] = CUM[jt] + NCH[jt]
        NSC = CUM[16]

        with nc.Block() as blk:
            @blk.sync
            def _(sync):
                for s in range(NSEQ):
                    if s >= 1:
                        sync.wait_ge(S("peA"), NAV * s)
                    qL.inc(sync.dma_start(
                        k2[:, :].rearrange("p (e t) -> p e t", t=T),
                        Dr["kf1"][s].rearrange("(e p) t -> p e t", p=128)))
                    qL.inc(sync.dma_start(
                        q2[:, :].rearrange("p (e t) -> p e t", t=T),
                        Dr["qf1"][s].rearrange("(e p) t -> p e t", p=128)))
                    qL.inc(sync.dma_start(
                        v2[:, :].rearrange("p (jt f) -> p jt f", f=D),
                        Dr["vt1"][s].rearrange("(jt p) f -> p jt f", p=128)))
                    for it in range(16):
                        wt = s * 16 + it
                        if wt >= 2:
                            sync.wait_ge(S("vE"), 2 * (wt - 1))
                        t0 = wt * 128
                        qR.inc(sync.dma_start(rr[wt % 2][:, :],
                                              Dr["r2"][t0:t0 + 128, :]))
                        sync.wait_ge(S("vE"), 2 * wt + 2)
                        qO.inc(sync.dma_start(Dr["r3"][t0:t0 + 128, :],
                                              x3t[wt % 2][:, :]))
                        if it % 4 == 3:
                            g = wt // 4
                            sync.wait_ge(S("sC"), 32 * (g + 1))
                            qN.inc(sync.dma_start(
                                Dr["xn3g"][g].rearrange("e p t -> p e t"),
                                xfm[:, :].rearrange("p (e t) -> p e t", t=512)))
                qO.wait(sync, NT - 1)
                qO.wait(sync, NT - 2)
                qN.wait(sync, NG - 1)
                qN.wait(sync, NG - 2)

            @blk.tensor
            def _(t):
                for s in range(NSEQ):
                    qL.wait(t, 3 * s)
                    qL.wait(t, 3 * s + 1)
                    for jt in range(16):
                        j0 = jt * 128
                        for ci, (lo, hi) in enumerate(chunks(jt)):
                            idx = NSC * s + CUM[jt] + ci
                            if idx >= 2:
                                t.wait_ge(S("sE"), idx - 1)
                            for ei in range(NE):
                                ins = t.matmul(
                                    ps_sc[idx % 2][:, 0:hi - lo],
                                    k2[:, ei * T + j0: ei * T + j0 + 128],
                                    q2[:, ei * T + lo: ei * T + hi],
                                    start=(ei == 0), stop=(ei == 7))
                            ins.then_inc(S("peS"), 1)
                    qL.wait(t, 3 * s + 2)
                    for it in range(16):
                        wt = s * 16 + it
                        t.wait_ge(S("vN"), 16 * s + it + 1)
                        if wt >= 2:
                            t.wait_ge(S("vE"), 2 * (wt - 1))
                        for half in range(2):
                            for jt in range(it + 1):
                                ins = t.matmul(
                                    ps_av[it % 2][:, half * 512:(half + 1) * 512],
                                    Aall[:, jt * T + it * 128: jt * T + (it + 1) * 128],
                                    v2[:, jt * D + half * 512: jt * D + (half + 1) * 512],
                                    start=(jt == 0), stop=(jt == it))
                            ins.then_inc(S("peA"), 1)
                        t.wait_ge(S("vD"), wt + 1)
                        for ei in range(NE):
                            kk = wt * 8 + ei
                            if kk >= 2:
                                t.wait_ge(S("sC"), kk - 1)
                            t.transpose(pst[kk % 2][:, 0:128],
                                        xnb[:, ei * 128:(ei + 1) * 128],
                                        Pc["ident_bf"][:, :]).then_inc(S("pA"), 1)

            @blk.scalar
            def _(s_):
                for s in range(NSEQ):
                    for jt in range(16):
                        j0 = jt * 128
                        for ci, (lo, hi) in enumerate(chunks(jt)):
                            idx = NSC * s + CUM[jt] + ci
                            s_.wait_ge(S("peS"), idx + 1)
                            if ci == 0:
                                s_.wait_ge(S("vMk"), 16 * s + jt + 1)
                            s_.activation(Aall[:, jt * T + lo: jt * T + hi],
                                          ps_sc[idx % 2][:, 0:hi - lo],
                                          AF.Exp, scale=1.0 / 32.0
                                          ).then_inc(S("sE"), 1)
                    for it in range(16):
                        wt = s * 16 + it
                        s_.wait_ge(S("vE"), 2 * wt + 2)
                        if wt >= 1:
                            s_.wait_ge(S("lnvA"), 2 * wt)
                        s_.activation(xsq[:, :], x3t[wt % 2][:, :],
                                      AF.Square).then_inc(S("lnsA"), 1)
                        s_.wait_ge(S("lnvB"), wt + 1)
                        s_.activation(cols[:, 6:7], cols[:, 5:6], AF.Sqrt, scale=-1.0,
                                      bias=Pc["c1e5"]).then_inc(S("lnsB"), 1)
                        for ei in range(NE):
                            kk = wt * 8 + ei
                            s_.wait_ge(S("pA"), kk + 1)
                            if wt % 4 == 0 and ei == 0 and wt >= 4:
                                qN.wait(s_, wt // 4 - 1)
                            k = wt % 4
                            s_.copy(xfm[:, ei * 512 + k * 128: ei * 512 + (k + 1) * 128],
                                    pst[kk % 2][:, 0:128]).then_inc(S("sC"), 1)

            @blk.vector
            def _(v):
                for s in range(NSEQ):
                    for jt in range(16):
                        j0 = jt * 128
                        idx0 = NSC * s + CUM[jt]
                        v.wait_ge(S("peS"), idx0 + 1)
                        v.tensor_tensor(ps_sc[idx0 % 2][:, 0:128],
                                        ps_sc[idx0 % 2][:, 0:128],
                                        Pc["maskband"][:, 0:128],
                                        ALU.add).then_inc(S("vMk"), 1)
                        v.wait_ge(S("sE"), NSC * s + CUM[jt + 1])
                        v.reduce_sum(lcol[:, 0:1], Aall[:, jt * T + j0: (jt + 1) * T],
                                     axis=AX).then_inc(S("vL"), 1)
                        v.wait_ge(S("vL"), 16 * s + jt + 1)
                        v.reciprocal(lcol[:, 1:2], lcol[:, 0:1]).then_inc(S("vLr"), 1)
                        v.wait_ge(S("vLr"), 16 * s + jt + 1)
                        v.tensor_scalar(Aall[:, jt * T + j0: (jt + 1) * T],
                                        Aall[:, jt * T + j0: (jt + 1) * T],
                                        lcol[:, 1:2], None,
                                        ALU.mult).then_inc(S("vN"), 1)
                    for it in range(16):
                        wt = s * 16 + it
                        if wt >= 2:
                            qO.wait(v, wt - 2)
                            v.wait_ge(S("lnsA"), wt - 1)
                        for half in range(2):
                            n = 2 * wt + half
                            v.wait_ge(S("peA"), 2 * wt + half + 1)
                            qR.wait(v, wt)
                            v.tensor_tensor(
                                x3t[wt % 2][:, half * 512:(half + 1) * 512],
                                ps_av[it % 2][:, half * 512:(half + 1) * 512],
                                rr[wt % 2][:, half * 512:(half + 1) * 512],
                                ALU.add).then_inc(S("vE"), 1)
                        v.wait_ge(S("vE"), 2 * wt + 2)
                        v.reduce_sum(cols[:, 0:1], x3t[wt % 2][:, :],
                                     axis=AX).then_inc(S("lnvA"), 1)
                        v.wait_ge(S("lnsA"), wt + 1)
                        v.reduce_sum(cols[:, 1:2], xsq[:, :],
                                     axis=AX).then_inc(S("lnvA"), 1)
                        v.wait_ge(S("lnvA"), 2 * wt + 2)
                        v.tensor_scalar(cols[:, 2:3], cols[:, 0:1], 1.0 / D, None, ALU.mult)
                        v.tensor_scalar(cols[:, 3:4], cols[:, 1:2], 1.0 / D, None,
                                        ALU.mult).then_inc(S("lnvM"), 1)
                        v.wait_ge(S("lnvM"), wt + 1)
                        v.scalar_tensor_tensor(cols[:, 5:6], cols[:, 2:3], cols[:, 2:3],
                                               cols[:, 3:4], ALU.mult,
                                               ALU.subtract).then_inc(S("lnvB"), 1)
                        v.wait_ge(S("lnsB"), wt + 1)
                        v.reciprocal(cols[:, 7:8], cols[:, 6:7]).then_inc(S("lnvC"), 1)
                        v.wait_ge(S("lnvC"), wt + 1)
                        if wt >= 1:
                            v.wait_ge(S("pA"), 8 * wt)
                        v.tensor_scalar(xnb[:, :], x3t[wt % 2][:, :], cols[:, 2:3],
                                        cols[:, 7:8], ALU.subtract,
                                        ALU.mult).then_inc(S("vD"), 1)

            @blk.gpsimd
            def _(g_):
                pass

        S.release()




def emit_phase_g(nc, es, Dr, Pc):
    with ExitStack() as wes:
        gate_sb = wes.enter_context(nc.sbuf_tensor("g_gate", [128, NE * 8], BF16))
        xng = [wes.enter_context(nc.sbuf_tensor(f"g_xng{i}", [128, NE * 512], BF16))
               for i in range(2)]
        gwork = wes.enter_context(nc.sbuf_tensor("g_gw", [128, 48], F32))
        gcols = wes.enter_context(nc.sbuf_tensor("g_gc", [128, 16], F32))
        comb_all = wes.enter_context(nc.sbuf_tensor("g_comb", [128, NT * 8], F32))
        w1sb = [wes.enter_context(nc.sbuf_tensor(f"g_w1{i}", [128, NE * 2048], BF16))
                for i in range(2)]
        w2sb = [wes.enter_context(nc.sbuf_tensor(f"g_w2{i}", [128, 16 * D], BF16))
                for i in range(2)]
        hsb = wes.enter_context(nc.sbuf_tensor("g_h", [128, 16 * 512], BF16))
        ot = [wes.enter_context(nc.sbuf_tensor(f"g_ot{i}", [128, 512], F32))
              for i in range(2)]
        pg = wes.enter_context(nc.psum_tensor("g_pg", [128, 512], F32))
        psh = [wes.enter_context(nc.psum_tensor(f"g_psh{i}", [128, 512], F32))
               for i in range(2)]
        pso = [wes.enter_context(nc.psum_tensor(f"g_pso{i}", [128, 512], F32))
               for i in range(2)]
        S = Sems(nc, "g")
        qXg = DmaQ(S, "dXg", 2)
        qXe = DmaQ(S, "dXe", 2)
        qW1 = DmaQ(S, "dW1", 2)
        qW2 = DmaQ(S, "dW2", 2)
        qI = DmaQ(S, "dI", 2)
        qY = DmaQ(S, "dY", 2)
        qG = DmaQ(S, "dG", 2)

        with nc.Block() as blk:
            @blk.sync
            def _(sync):
                for g in range(NG):
                    if g >= 2:
                        sync.wait_ge(S("peG"), 4 * (g - 1))
                    qXg.inc(sync.dma_start(
                        xng[g % 2][:, :].rearrange("p (e t) -> p e t", t=512),
                        Dr["xn3g"][g].rearrange("e p t -> p e t")))
                for e in range(8):
                    for g in range(NG):
                        eg = e * NG + g
                        if eg < 2:
                            sync.wait_ge(S("peG"), NT)
                        else:
                            sync.wait_ge(S("peH"), 16 * (eg - 1))
                        qXe.inc(sync.dma_start(
                            xng[eg % 2][:, :].rearrange("p (e t) -> p e t", t=512),
                            Dr["xn3g"][g].rearrange("e p t -> p e t")))

            @blk.gpsimd
            def _(g_):
                qG.inc(g_.dma_start(
                    gate_sb[:, :].rearrange("p (e u) -> p e u", u=8),
                    Dr["gate_w"].rearrange("(e p) u -> p e u", p=128)))
                for i in range(8):
                    qI.inc(g_.dma_start(Dr["y"][i * 1024:(i + 1) * 1024, :],
                                        Dr["r3"][i * 1024:(i + 1) * 1024, :]))
                for e in range(8):
                    if e >= 2:
                        g_.wait_ge(S("peH"), 256 * (e - 1))
                    for dc in range(NE):
                        qW1.inc(g_.dma_start(
                            w1sb[e % 2][:, dc * 2048:(dc + 1) * 2048],
                            Dr["w1"][e, dc * 128:(dc + 1) * 128, :]))
                    if e >= 2:
                        g_.wait_ge(S("peW2"), 128 * (e - 1))
                    for fc in range(16):
                        qW2.inc(g_.dma_start(
                            w2sb[e % 2][:, fc * D:(fc + 1) * D],
                            Dr["w2"][e, fc * 128:(fc + 1) * 128, :]))
                    if e == 0:
                        qI.wait(g_, 7)
                        qI.wait(g_, 6)
                    for g in range(NG):
                        eg = e * NG + g
                        for tt in range(4):
                            for half in range(2):
                                n = eg * 8 + tt * 2 + half
                                g_.wait_ge(S("vO"), n + 1)
                                t0 = g * 512 + tt * 128
                                qY.inc(g_.dma_start(
                                    Dr["y"][t0:t0 + 128, half * 512:(half + 1) * 512],
                                    ot[n % 2][:, :], accum_op=ALU.add))
                qY.wait(g_, 8 * 128 - 1)
                qY.wait(g_, 8 * 128 - 2)

            @blk.tensor
            def _(t):
                qG.wait(t, 0)
                for g in range(NG):
                    qXg.wait(t, g)
                    for k in range(4):
                        tile = g * 4 + k
                        if tile >= 1:
                            t.wait_ge(S("sGc"), tile)
                        for dc in range(NE):
                            ins = t.matmul(
                                pg[:, 0:8],
                                xng[g % 2][:, dc * 512 + k * 128: dc * 512 + (k + 1) * 128],
                                gate_sb[:, dc * 8:(dc + 1) * 8],
                                start=(dc == 0), stop=(dc == 7))
                        ins.then_inc(S("peG"), 1)
                for e in range(8):
                    qW1.wait(t, e * 8 + 6)
                    qW1.wait(t, e * 8 + 7)
                    qW2.wait(t, e * 16 + 14)
                    qW2.wait(t, e * 16 + 15)
                    for g in range(NG):
                        eg = e * NG + g
                        qXe.wait(t, eg)
                        for fc in range(16):
                            m = eg * 16 + fc
                            if m >= 2:
                                t.wait_ge(S("sH"), m - 1)
                            for dc in range(NE):
                                ins = t.matmul(
                                    psh[m % 2][:, :],
                                    w1sb[e % 2][:, dc * 2048 + fc * 128: dc * 2048 + fc * 128 + 128],
                                    xng[eg % 2][:, dc * 512:(dc + 1) * 512],
                                    start=(dc == 0), stop=(dc == 7))
                            ins.then_inc(S("peH"), 1)
                        t.wait_ge(S("sH"), 16 * (eg + 1))
                        for tt in range(4):
                            for half in range(2):
                                n = eg * 8 + tt * 2 + half
                                if n >= 2:
                                    t.wait_ge(S("vO"), n - 1)
                                for fc in range(16):
                                    ins = t.matmul(
                                        pso[n % 2][:, :],
                                        hsb[:, fc * 512 + tt * 128: fc * 512 + (tt + 1) * 128],
                                        w2sb[e % 2][:, fc * D + half * 512: fc * D + (half + 1) * 512],
                                        start=(fc == 0), stop=(fc == 15))
                                ins.then_inc(S("peW2"), 1)

            @blk.scalar
            def _(s_):
                for tile in range(NT):
                    s_.wait_ge(S("peG"), tile + 1)
                    if tile >= 1:
                        s_.wait_ge(S("vExp"), tile)
                        s_.wait_ge(S("sExp"), tile)
                    s_.copy(gwork[:, 0:8], pg[:, 0:8]).then_inc(S("sGc"), 1)
                    s_.wait_ge(S("vExp"), tile + 1)
                    s_.activation(gwork[:, 32:40], gwork[:, 0:8], AF.Exp,
                                  bias=gcols[:, 7:8]).then_inc(S("sExp"), 1)
                for e in range(8):
                    for g in range(NG):
                        eg = e * NG + g
                        for fc in range(16):
                            m = eg * 16 + fc
                            s_.wait_ge(S("peH"), m + 1)
                            if fc == 0 and eg >= 1:
                                s_.wait_ge(S("peW2"), 8 * eg)
                            s_.activation(hsb[:, fc * 512:(fc + 1) * 512],
                                          psh[m % 2][:, :],
                                          AF.Silu).then_inc(S("sH"), 1)

            @blk.vector
            def _(v):
                for tile in range(NT):
                    v.wait_ge(S("sGc"), tile + 1)
                    v.reduce_max(gcols[:, 0:1], gwork[:, 0:8],
                                 axis=AX).then_inc(S("v1"), 1)
                    v.wait_ge(S("v1"), tile + 1)
                    v.tensor_scalar(gwork[:, 8:16], gwork[:, 0:8], gcols[:, 0:1],
                                    None, ALU.is_ge).then_inc(S("v2"), 1)
                    v.wait_ge(S("v2"), tile + 1)
                    v.reduce_sum(gcols[:, 1:2], gwork[:, 8:16],
                                 axis=AX).then_inc(S("v3"), 1)
                    v.scalar_tensor_tensor(gwork[:, 16:24], gwork[:, 8:16], -1e30,
                                           gwork[:, 0:8], ALU.mult,
                                           ALU.add).then_inc(S("v4"), 1)
                    v.wait_ge(S("v4"), tile + 1)
                    v.reduce_max(gcols[:, 2:3], gwork[:, 16:24],
                                 axis=AX).then_inc(S("v5"), 1)
                    v.wait_ge(S("v3"), tile + 1)
                    v.tensor_scalar(gcols[:, 3:4], gcols[:, 1:2], -1.0, 1.0,
                                    ALU.add, ALU.min).then_inc(S("v6"), 1)
                    v.wait_ge(S("v5"), tile + 1)
                    v.wait_ge(S("v6"), tile + 1)
                    v.tensor_tensor(gcols[:, 4:5], gcols[:, 0:1], gcols[:, 2:3],
                                    ALU.subtract).then_inc(S("v7"), 1)
                    v.wait_ge(S("v7"), tile + 1)
                    v.tensor_tensor(gcols[:, 5:6], gcols[:, 4:5], gcols[:, 3:4],
                                    ALU.mult).then_inc(S("v8"), 1)
                    v.wait_ge(S("v8"), tile + 1)
                    v.tensor_tensor(gcols[:, 6:7], gcols[:, 2:3], gcols[:, 5:6],
                                    ALU.add).then_inc(S("v9"), 1)
                    v.wait_ge(S("v9"), tile + 1)
                    v.tensor_scalar(gwork[:, 24:32], gwork[:, 0:8], gcols[:, 6:7],
                                    None, ALU.is_ge)
                    v.tensor_scalar(gcols[:, 7:8], gcols[:, 0:1], -1.0, None,
                                    ALU.mult).then_inc(S("vExp"), 1)
                    v.wait_ge(S("sExp"), tile + 1)
                    v.tensor_tensor(gwork[:, 40:48], gwork[:, 32:40], gwork[:, 24:32],
                                    ALU.mult).then_inc(S("vA"), 1)
                    v.wait_ge(S("vA"), tile + 1)
                    v.reduce_sum(gcols[:, 8:9], gwork[:, 40:48],
                                 axis=AX).then_inc(S("vB"), 1)
                    v.wait_ge(S("vB"), tile + 1)
                    v.reciprocal(gcols[:, 9:10], gcols[:, 8:9]).then_inc(S("vC"), 1)
                    v.wait_ge(S("vC"), tile + 1)
                    v.tensor_scalar(comb_all[:, tile * 8:(tile + 1) * 8],
                                    gwork[:, 40:48], gcols[:, 9:10],
                                    None, ALU.mult).then_inc(S("vCb"), 1)
                v.wait_ge(S("vCb"), NT)
                for e in range(8):
                    for g in range(NG):
                        eg = e * NG + g
                        for tt in range(4):
                            tile = g * 4 + tt
                            for half in range(2):
                                n = eg * 8 + tt * 2 + half
                                v.wait_ge(S("peW2"), n + 1)
                                if n >= 2:
                                    qY.wait(v, n - 2)
                                v.tensor_scalar(ot[n % 2][:, :],
                                                pso[n % 2][:, :],
                                                comb_all[:, tile * 8 + e: tile * 8 + e + 1],
                                                None, ALU.mult).then_inc(S("vO"), 1)
        S.release()


PHASE_IO = {
    "a": (["x", "wq0", "wk0", "wv0", "bq0", "bk0", "bv0"],
          ["qf0", "kf0", "vt0"]),
    "b": (["x", "qf0", "kf0", "vt0", "wo0", "bo0", "rel"],
          ["r1", "tcorr_d", "dbg"]),
    "c": (["r1", "wffn", "lng", "lnb"],
          ["xn1g", "wqn_d", "beta_d", "gmax_d", "cc_in", "cc_out"]),
    "d": (["r1", "xn1g", "wqn_d", "beta_d", "gmax_d"],
          ["r2", "xn2g"]),
    "e": (["xn2g", "wq1", "wk1", "wv1", "bq1", "bk1", "bv1"],
          ["qf1", "kf1", "vt1"]),
    "f": (["qf1", "kf1", "vt1", "r2"],
          ["r3", "xn3g"]),
    "g": (["xn3g", "r3", "gate_w", "w1", "w2"],
          ["y"]),
}

DRAM_SPECS = {
    "x": ([TOK, D], F32),
    "wq0": ([D, D], F32), "wk0": ([D, D], F32), "wv0": ([D, D], F32),
    "wo0": ([D, D], F32),
    "bq0": ([NE, 128], F32), "bk0": ([NE, 128], F32), "bv0": ([NE, 128], F32),
    "bo0": ([1, D], F32),
    "rel": ([32, 16], F32),
    "qf0": ([NSEQ, D, T], BF16), "kf0": ([NSEQ, D, T], BF16),
    "vt0": ([NSEQ, T, D], BF16),
    "r1": ([TOK, D], F32),
    "r2": ([TOK, D], F32),
    "tcorr_d": ([16, 128 * 256], F32),
    "wffn": ([D, D], F32),
    "lng": ([1, D], F32), "lnb": ([1, D], F32),
    "xn1g": ([NG, NE, 128, 512], BF16),
    "xn2g": ([NG, NE, 128, 512], BF16),
    "wqn_d": ([D, D], BF16),
    "wq1": ([D, D], F32), "wk1": ([D, D], F32), "wv1": ([D, D], F32),
    "bq1": ([NE, 128], F32), "bk1": ([NE, 128], F32), "bv1": ([NE, 128], F32),
    "qf1": ([NSEQ, D, T], BF16), "kf1": ([NSEQ, D, T], BF16),
    "vt1": ([NSEQ, T, D], BF16),
    "r3": ([TOK, D], F32),
    "xn3g": ([NG, NE, 128, 512], BF16),
    "gate_w": ([D, 8], F32),
    "w1": ([8, D, 2048], F32),
    "w2": ([8, 2048, D], F32),
    "y": ([TOK, D], F32),
    "beta_d": ([1, 1], F32),
    "gmax_d": ([1, 1], F32),
    "cc_in": ([1, 1], F32),
    "cc_out": ([1, 1], F32),
    "dbg": ([128, 2064], F32),
}

CONST_SPECS = {
    "ident_bf": ([128, 128], BF16),
    "ident_f32": ([128, 128], F32),
    "ones_f32": ([1, 128], F32),
    "selmat": ([32, 128 * 256], BF16, "dram_only"),
    "maskband": ([128, 256], F32),
}

CONST_COLS = {"c1e5": 1e-5, "c1e3": 1e-3, "c44": 44.0, "cLE": 13.5015240554}

PHASE_EMIT = {"a": emit_phase_a2, "b": emit_phase_b, "c": emit_phase_c, "d": emit_phase_d, "e": emit_phase_e, "f": emit_phase_f, "g": emit_phase_g}


def build_program(phases, external):
    nc = bass.Bass("TRN2", target_bir_lowering=False)
    consumes = set()
    produces = set()
    for p in phases:
        c, pr = PHASE_IO[p]
        consumes.update(x for x in c if x not in produces)
        produces.update(pr)
    Dr = {}
    ext_inputs = []
    for name in sorted(consumes | produces):
        shape, dt = DRAM_SPECS[name]
        if name in produces and name in external:
            kind = "ExternalOutput"
        elif name in consumes and name not in produces:
            kind = "ExternalInput"
            ext_inputs.append(name)
        else:
            kind = "Internal"
        Dr[name] = nc.dram_tensor(name, shape, dt, kind=kind)
        if kind == "ExternalInput":
            nc._ext_inputs = getattr(nc, "_ext_inputs", []) + [name]
    Pc = {}
    with ExitStack() as es:
        for cn, spec in CONST_SPECS.items():
            shape, dt = spec[0], spec[1]
            dr = nc.dram_tensor("c_" + cn, shape, dt, kind="ExternalInput")
            Pc[cn + "_dram"] = dr
            if len(spec) > 2 and spec[2] == "dram_only":
                continue
            sb = es.enter_context(nc.sbuf_tensor("sb_" + cn, shape, dt))
            Pc[cn] = sb
        ccols = es.enter_context(nc.sbuf_tensor("sb_ccols", [128, len(CONST_COLS)], F32))
        for i, cn in enumerate(CONST_COLS):
            Pc[cn] = ccols[:, i:i + 1]
        S0 = Sems(nc, "init")
        with nc.Block() as blk:
            @blk.sync
            def _(sync):
                n = 0
                for cn, spec in CONST_SPECS.items():
                    if len(spec) > 2 and spec[2] == "dram_only":
                        continue
                    sync.dma_start(Pc[cn][:, :], Pc[cn + "_dram"][:, :]
                                   ).then_inc(S0("dma"), 16)
                    n += 16
                sync.wait_ge(S0("dma"), n)
            @blk.gpsimd
            def _(g):
                for i, (cn, val) in enumerate(CONST_COLS.items()):
                    g.memset(ccols[:, i:i + 1], val)

        for p in phases:
            PHASE_EMIT[p](nc, es, Dr, Pc)
    return nc


FUSED = True
LAUNCH_GROUPS = ([["a", "b", "c", "d", "e", "f", "g"]] if FUSED else
                 [["a"], ["b"], ["c"], ["d"], ["e"], ["f"], ["g"]])

_INTERNAL_ONLY = {"tcorr_d", "cc_in", "cc_out", "dbg"}


def _base_maps(inputs):
    consts = host_consts()
    x = np.ascontiguousarray(np.asarray(inputs["L0_attn_x"], np.float32))
    maps = []
    for c in range(NCORES):
        m = {
            "x": np.ascontiguousarray(x[4 * c:4 * c + 4].reshape(TOK, D)),
            "wq0": np.asarray(inputs["L0_attn_Wq"], np.float32),
            "wk0": np.asarray(inputs["L0_attn_Wk"], np.float32),
            "wv0": np.asarray(inputs["L0_attn_Wv"], np.float32),
            "wo0": np.asarray(inputs["L0_attn_Wo"], np.float32),
            "bq0": np.asarray(inputs["L0_attn_bq"], np.float32).reshape(NE, 128),
            "bk0": np.asarray(inputs["L0_attn_bk"], np.float32).reshape(NE, 128),
            "bv0": np.asarray(inputs["L0_attn_bv"], np.float32).reshape(NE, 128),
            "bo0": np.asarray(inputs["L0_attn_bo"], np.float32).reshape(1, D),
            "rel": np.asarray(inputs["L0_attn_rel_emb"], np.float32),
            "wffn": np.asarray(inputs["L0_ffn_weight"], np.float32),
            "lng": np.asarray(inputs["L0_ffn_ln_gamma"], np.float32).reshape(1, D),
            "lnb": np.asarray(inputs["L0_ffn_ln_beta"], np.float32).reshape(1, D),
            "wq1": np.asarray(inputs["L1_attn_Wq"], np.float32),
            "wk1": np.asarray(inputs["L1_attn_Wk"], np.float32),
            "wv1": np.asarray(inputs["L1_attn_Wv"], np.float32),
            "bq1": np.asarray(inputs["L1_attn_bq"], np.float32).reshape(NE, 128),
            "bk1": np.asarray(inputs["L1_attn_bk"], np.float32).reshape(NE, 128),
            "bv1": np.asarray(inputs["L1_attn_bv"], np.float32).reshape(NE, 128),
            "gate_w": np.asarray(inputs["L1_ffn_gate_w"], np.float32),
            "w1": np.asarray(inputs["L1_ffn_w1"], np.float32),
            "w2": np.asarray(inputs["L1_ffn_w2"], np.float32),
        }
        for cn in CONST_SPECS:
            m["c_" + cn] = consts[cn]
        maps.append(m)
    return maps


def kernel(**inputs):
    state = _base_maps(inputs)
    n_groups = len(LAUNCH_GROUPS)
    later_consumes = []
    acc = {"y"}
    for gi in range(n_groups - 1, -1, -1):
        later_consumes.insert(0, set(acc))
        for p in LAUNCH_GROUPS[gi]:
            acc |= set(PHASE_IO[p][0])
    for gi, group in enumerate(LAUNCH_GROUPS):
        produced = set()
        for p in group:
            produced |= set(PHASE_IO[p][1])
        ext = (produced & later_consumes[gi]) - _INTERNAL_ONLY
        if gi == n_groups - 1:
            ext |= {"y"}
        nc = build_program(group, external=ext)
        ext_in = set(getattr(nc, "_ext_inputs", [])) | {"c_" + c for c in CONST_SPECS}
        in_maps = [{k: v for k, v in state[c].items() if k in ext_in}
                   for c in range(NCORES)]
        res = run_bass_kernel_spmd(nc, in_maps, core_ids=list(range(NCORES)))
        for c in range(NCORES):
            for k, v in res.results[c].items():
                state[c][k] = np.asarray(v)
    out = np.concatenate(
        [state[c]["y"].reshape(NSEQ, T, D) for c in range(NCORES)], axis=0)
    return out


# version 04
# speedup vs baseline: 1.0477x; 1.0017x over previous
import math
import numpy as np
import ml_dtypes
from contextlib import ExitStack

import concourse.bass as bass
import concourse.mybir as mybir
from concourse.bass_utils import run_bass_kernel_spmd

F32 = mybir.dt.float32
BF16 = mybir.dt.bfloat16
AF = mybir.ActivationFunctionType
ALU = mybir.AluOpType
AX = mybir.AxisListType.X
BF = ml_dtypes.bfloat16

NCORES = 8
NSEQ = 4
T = 2048
D = 1024
TOK = NSEQ * T
NE = D // 128
NG = TOK // 512
NT = TOK // 128
HEADS = 16
HD = 64
NUM_BUCKETS = 32
MAX_DISTANCE = 128
SCALE = (D // HEADS) ** (-0.5)
NEG = -1e30
B_DBG = False
B_NOBIAS = False


def _bucket(d):
    max_exact = NUM_BUCKETS // 2
    if d < max_exact:
        return d
    v = max_exact + int(
        math.log(d / max_exact) / math.log(MAX_DISTANCE / max_exact)
        * (NUM_BUCKETS - max_exact))
    return min(v, NUM_BUCKETS - 1)


def host_consts():
    c = {}
    c["ident_bf"] = np.eye(128, dtype=BF)
    c["ident_f32"] = np.eye(128, dtype=np.float32)
    c["ones_f32"] = np.ones((1, 128), dtype=np.float32)
    A = np.zeros((NUM_BUCKETS, 128 * 256), dtype=np.float32)
    for p in range(128):
        for ip in range(256):
            d = ip - p
            if d < 0:
                continue
            pos = p * 256 + ip
            A[_bucket(d), pos] += 1.0
            A[NUM_BUCKETS - 1, pos] -= 1.0
    c["selmat"] = A.astype(BF)
    mb = np.zeros((128, 256), dtype=np.float32)
    for p in range(128):
        mb[p, :p] = NEG
    c["maskband"] = mb
    return c


class Sems:
    def __init__(self, nc, prefix):
        self.nc = nc
        self.prefix = prefix
        self.h = {}

    def __call__(self, name):
        if name not in self.h:
            self.h[name] = self.nc.alloc_semaphore(self.prefix + "_" + name)
        return self.h[name]

    def release(self):
        if self.h:
            self.nc.clear_and_free_semaphores(list(self.h.values()))
            self.nc.all_engine_barrier()
            self.h = {}

class DmaQ:

    def __init__(self, S, name, nsem=2):
        self.sems = [S(f"{name}{i}") for i in range(nsem)]
        self.P = nsem
        self.n = 0

    def inc(self, instr):
        instr.then_inc(self.sems[self.n % self.P], 16)
        self.n += 1

    def wait(self, eng, k):
        eng.wait_ge(self.sems[k % self.P], 16 * (k // self.P + 1))



def emit_load_wT(nc, es, w_drams, tag, ident_bf):
    outs = [es.enter_context(nc.sbuf_tensor(f"{tag}_wt{i}", [128, NE * D], BF16))
            for i in range(len(w_drams))]
    les = ExitStack()
    stage = [les.enter_context(nc.sbuf_tensor(f"{tag}_wstg{i}", [128, D], BF16))
             for i in range(2)]
    ps = [les.enter_context(nc.psum_tensor(f"{tag}_wps{i}", [128, 1024], BF16))
          for i in range(2)]
    S = Sems(nc, tag + "_lw")
    NW = len(w_drams)
    qW = DmaQ(S, "dma", 2)
    with les, nc.Block() as blk:
        @blk.gpsimd
        def _(g):
            n = 0
            for wi in range(NW):
                for fi in range(NE):
                    n += 1
                    if n > 2:
                        g.wait_ge(S("pe"), 8 * (n - 2))
                    qW.inc(g.dma_start(stage[n % 2][:, :],
                                       w_drams[wi][fi * 128:(fi + 1) * 128, :]))

        @blk.tensor
        def _(t):
            n = 0
            for wi in range(NW):
                for fi in range(NE):
                    n += 1
                    qW.wait(t, n - 1)
                    for ei in range(NE):
                        k = (n - 1) * 8 + ei
                        if k >= 2:
                            t.wait_ge(S("se"), k - 1)
                        t.transpose(ps[k % 2][:, 0:128],
                                    stage[n % 2][:, ei * 128:(ei + 1) * 128],
                                    ident_bf[:, :]).then_inc(S("pe"), 1)

        @blk.scalar
        def _(s):
            n = 0
            for wi in range(NW):
                for fi in range(NE):
                    n += 1
                    for ei in range(NE):
                        k = (n - 1) * 8 + ei
                        s.wait_ge(S("pe"), k + 1)
                        s.copy(outs[wi][:, ei * D + fi * 128: ei * D + (fi + 1) * 128],
                               ps[k % 2][:, 0:128]).then_inc(S("se"), 1)
    S.release()
    return outs


def emit_load_bias_cols(nc, es, b_drams, tag, ident_f32):
    outs = [es.enter_context(nc.sbuf_tensor(f"{tag}_bc{i}", [128, NE], F32))
            for i in range(len(b_drams))]
    les = ExitStack()
    stage = [les.enter_context(nc.sbuf_tensor(f"{tag}_bstg{i}", [NE, 128], F32))
             for i in range(len(b_drams))]
    ps = les.enter_context(nc.psum_tensor(f"{tag}_bps", [128, 512], F32))
    S = Sems(nc, tag + "_lb")
    with les, nc.Block() as blk:
        @blk.sync
        def _(sync):
            for i, b in enumerate(b_drams):
                sync.dma_start(stage[i][:, :], b[:, :]).then_inc(S("dma"), 16)

        @blk.tensor
        def _(t):
            t.wait_ge(S("dma"), 16 * len(b_drams))
            for i in range(len(b_drams)):
                if i >= 1:
                    t.wait_ge(S("se"), i)
                t.transpose(ps[:, 0:NE], stage[i][:, :],
                            ident_f32[0:NE, 0:NE]).then_inc(S("pe"), 1)

        @blk.scalar
        def _(s):
            for i in range(len(b_drams)):
                s.wait_ge(S("pe"), i + 1)
                s.copy(outs[i][:, :], ps[:, 0:NE]).then_inc(S("se"), 1)
    S.release()
    return outs


def emit_phase_a2(nc, es, Dr, Pc):
    with ExitStack() as wes:
        wts = emit_load_wT(nc, wes, [Dr["wq0"], Dr["wk0"], Dr["wv0"]], "aw",
                           Pc["ident_bf"])
        bcols = emit_load_bias_cols(nc, wes, [Dr["bq0"], Dr["bk0"], Dr["bv0"]],
                                    "ab", Pc["ident_f32"])
        xin = [wes.enter_context(nc.sbuf_tensor(f"a_xin{i}", [128, D], F32))
               for i in range(2)]
        xsq = wes.enter_context(nc.sbuf_tensor("a_xsq", [128, D], F32))
        cols = wes.enter_context(nc.sbuf_tensor("a_cols", [128, 8], F32))
        xlnb = wes.enter_context(nc.sbuf_tensor("a_xlnb", [128, D], BF16))
        xfm = wes.enter_context(nc.sbuf_tensor("a_xfm", [128, NE * 512], BF16))
        qt = [wes.enter_context(nc.sbuf_tensor(f"a_qt{i}", [128, 512], BF16))
              for i in range(2)]
        vfm = wes.enter_context(nc.sbuf_tensor("a_vfm", [128, NE * 512], BF16))
        vtm = [wes.enter_context(nc.sbuf_tensor(f"a_vtm{i}", [128, D], BF16))
               for i in range(2)]
        pst = [wes.enter_context(nc.psum_tensor(f"a_pst{i}", [128, 1024], BF16))
               for i in range(2)]
        pstv = [wes.enter_context(nc.psum_tensor(f"a_pstv{i}", [128, 1024], BF16))
                for i in range(2)]
        psq = [wes.enter_context(nc.psum_tensor(f"a_psq{i}", [128, 512], F32))
               for i in range(2)]
        S = Sems(nc, "a")
        qA = DmaQ(S, "dA", 2)
        qQ = DmaQ(S, "dQ", 2)
        qV = DmaQ(S, "dV", 2)

        with nc.Block() as blk:
            @blk.sync
            def _(sync):
                for g in range(NG):
                    sq = g // 4
                    tg0 = (g % 4) * 512
                    for k in range(4):
                        tt = g * 4 + k
                        if tt >= 2:
                            sync.wait_ge(S("vD"), tt - 1)
                        t0 = tt * 128
                        qA.inc(sync.dma_start(xin[tt % 2][:, :],
                                              Dr["x"][t0:t0 + 128, :]))
                    for wi in range(2):
                        for fi in range(NE):
                            sync.wait_ge(S("sEv"), g * 24 + wi * 8 + fi + 1)
                            m = g * 16 + wi * 8 + fi + 1
                            dst = Dr["qf0"] if wi == 0 else Dr["kf0"]
                            qQ.inc(sync.dma_start(
                                dst[sq, fi * 128:(fi + 1) * 128, tg0:tg0 + 512],
                                qt[m % 2][:, :]))
                    for k2 in range(4):
                        sync.wait_ge(S("sV2"), g * 32 + k2 * 8 + 8)
                        n = g * 4 + k2 + 1
                        qV.inc(sync.dma_start(
                            Dr["vt0"][sq, tg0 + k2 * 128: tg0 + (k2 + 1) * 128, :],
                            vtm[n % 2][:, :]))
                qQ.wait(sync, NG * 16 - 1)
                qQ.wait(sync, NG * 16 - 2)
                qV.wait(sync, NG * 4 - 1)
                qV.wait(sync, NG * 4 - 2)

            @blk.scalar
            def _(s):
                for g in range(NG):
                    for k in range(4):
                        tt = g * 4 + k
                        qA.wait(s, tt)
                        if tt >= 1:
                            s.wait_ge(S("lnvA"), 2 * tt)
                        s.activation(xsq[:, :], xin[tt % 2][:, :],
                                     AF.Square).then_inc(S("lnsA"), 1)
                        s.wait_ge(S("lnvB"), tt + 1)
                        s.activation(cols[:, 6:7], cols[:, 5:6], AF.Sqrt, scale=-1.0,
                                     bias=Pc["c1e5"]).then_inc(S("lnsB"), 1)
                        for ei in range(NE):
                            kk = tt * 8 + ei
                            s.wait_ge(S("pA"), kk + 1)
                            if g >= 1 and k == 0 and ei == 0:
                                s.wait_ge(S("pB"), 24 * g)
                            s.copy(xfm[:, ei * 512 + k * 128: ei * 512 + (k + 1) * 128],
                                   pst[kk % 2][:, 0:128]).then_inc(S("sC"), 1)
                    for wi in range(3):
                        for fi in range(NE):
                            n = g * 24 + wi * 8 + fi + 1
                            s.wait_ge(S("pB"), n)
                            if wi < 2:
                                m = g * 16 + wi * 8 + fi + 1
                                if m > 2:
                                    qQ.wait(s, m - 3)
                                s.activation(qt[m % 2][:, :], psq[n % 2][:, :],
                                             AF.Identity, bias=bcols[wi][:, fi:fi + 1]
                                             ).then_inc(S("sEv"), 1)
                            else:
                                if g >= 1 and fi == 0:
                                    s.wait_ge(S("pV"), 32 * g)
                                s.activation(vfm[:, fi * 512:(fi + 1) * 512],
                                             psq[n % 2][:, :], AF.Identity,
                                             bias=bcols[2][:, fi:fi + 1]
                                             ).then_inc(S("sEv"), 1)
                    for k2 in range(4):
                        for fi in range(NE):
                            kk = g * 32 + k2 * 8 + fi
                            s.wait_ge(S("pV"), kk + 1)
                            n = g * 4 + k2 + 1
                            if n > 2 and fi == 0:
                                qV.wait(s, n - 3)
                            s.copy(vtm[n % 2][:, fi * 128:(fi + 1) * 128],
                                   pstv[kk % 2][:, 0:128]).then_inc(S("sV2"), 1)

            @blk.vector
            def _(v):
                for g in range(NG):
                    for k in range(4):
                        tt = g * 4 + k
                        qA.wait(v, tt)
                        v.reduce_sum(cols[:, 0:1], xin[tt % 2][:, :],
                                     axis=AX).then_inc(S("lnvA"), 1)
                        v.wait_ge(S("lnsA"), tt + 1)
                        v.reduce_sum(cols[:, 1:2], xsq[:, :],
                                     axis=AX).then_inc(S("lnvA"), 1)
                        v.wait_ge(S("lnvA"), 2 * (tt + 1))
                        v.tensor_scalar(cols[:, 2:3], cols[:, 0:1], 1.0 / D, None, ALU.mult)
                        v.tensor_scalar(cols[:, 3:4], cols[:, 1:2], 1.0 / D, None,
                                        ALU.mult).then_inc(S("lnvM"), 1)
                        v.wait_ge(S("lnvM"), tt + 1)
                        v.scalar_tensor_tensor(cols[:, 5:6], cols[:, 2:3], cols[:, 2:3],
                                               cols[:, 3:4], ALU.mult,
                                               ALU.subtract).then_inc(S("lnvB"), 1)
                        v.wait_ge(S("lnsB"), tt + 1)
                        v.reciprocal(cols[:, 7:8], cols[:, 6:7]).then_inc(S("lnvC"), 1)
                        v.wait_ge(S("lnvC"), tt + 1)
                        if tt >= 1:
                            v.wait_ge(S("pA"), 8 * tt)
                        v.tensor_scalar(xlnb[:, :], xin[tt % 2][:, :], cols[:, 2:3],
                                        cols[:, 7:8], ALU.subtract,
                                        ALU.mult).then_inc(S("vD"), 1)

            @blk.tensor
            def _(t):
                for g in range(NG):
                    for k in range(4):
                        tt = g * 4 + k
                        t.wait_ge(S("vD"), tt + 1)
                        for ei in range(NE):
                            kk = tt * 8 + ei
                            if kk >= 2:
                                t.wait_ge(S("sC"), kk - 1)
                            t.transpose(pst[kk % 2][:, 0:128],
                                        xlnb[:, ei * 128:(ei + 1) * 128],
                                        Pc["ident_bf"][:, :]).then_inc(S("pA"), 1)
                    t.wait_ge(S("sC"), 32 * (g + 1))
                    for wi in range(3):
                        for fi in range(NE):
                            n = g * 24 + wi * 8 + fi + 1
                            for ei in range(NE):
                                if ei == 0 and n > 2:
                                    t.wait_ge(S("sEv"), n - 2)
                                ins = t.matmul(
                                    psq[n % 2][:, :],
                                    wts[wi][:, ei * D + fi * 128: ei * D + fi * 128 + 128],
                                    xfm[:, ei * 512:(ei + 1) * 512],
                                    start=(ei == 0), stop=(ei == 7))
                                if ei == 7:
                                    ins.then_inc(S("pB"), 1)
                    t.wait_ge(S("sEv"), g * 24 + 24)
                    for k2 in range(4):
                        for fi in range(NE):
                            kk = g * 32 + k2 * 8 + fi
                            if kk >= 2:
                                t.wait_ge(S("sV2"), kk - 1)
                            t.transpose(pstv[kk % 2][:, 0:128],
                                        vfm[:, fi * 512 + k2 * 128: fi * 512 + (k2 + 1) * 128],
                                        Pc["ident_bf"][:, :]).then_inc(S("pV"), 1)
        S.release()



def _b_chunks(jt):
    j0 = jt * 128
    out = []
    for ac in range(j0 // 512, 4):
        lo = max(j0, ac * 512)
        hi = (ac + 1) * 512
        out.append((ac, lo, hi))
    return out


def _b_bias_segs(jt):
    j0 = jt * 128
    segs = []
    end = min(j0 + 256, 2048)
    c0 = j0 // 512
    s1hi = min(end, (c0 + 1) * 512)
    segs.append((c0, j0, s1hi))
    if end > s1hi and c0 + 1 <= 3:
        segs.append((c0 + 1, s1hi, end))
    return segs


_B_NSC = sum(len(_b_chunks(jt)) for jt in range(16))
_B_NBI = sum(len(_b_bias_segs(jt)) for jt in range(16))
_B_NMEM = sum(1 for jt in range(16) if (jt * 128) % 512)


def _b_sc_idx():
    m = {}
    i = 0
    for jt in range(16):
        for (ac, lo, hi) in _b_chunks(jt):
            m[(jt, ac)] = i
            i += 1
    return m


_B_IDX = _b_sc_idx()
_B_CUMAV = [0] * 17
for _jt in range(16):
    _B_CUMAV[_jt + 1] = _B_CUMAV[_jt] + len(_b_chunks(_jt))


def emit_phase_b(nc, es, Dr, Pc):
    with ExitStack() as wes:
        wo_t = emit_load_wT(nc, wes, [Dr["wo0"]], "bw", Pc["ident_bf"])[0]
        kh = [wes.enter_context(nc.sbuf_tensor(f"b_kh{i}", [64, T], BF16))
              for i in range(2)]
        qh = [wes.enter_context(nc.sbuf_tensor(f"b_qh{i}", [64, T], BF16))
              for i in range(2)]
        v_sb = wes.enter_context(nc.sbuf_tensor("b_vsb", [128, 16 * D], BF16))
        v65 = wes.enter_context(nc.sbuf_tensor("b_v65", [128, 16 * 65], BF16))
        A = [wes.enter_context(nc.sbuf_tensor(f"b_A{i}", [128, T], BF16))
             for i in range(2)]
        ao = wes.enter_context(nc.sbuf_tensor("b_ao", [128, 8 * T], BF16))
        ao_tmp = [wes.enter_context(nc.sbuf_tensor(f"b_aot{i}", [64, T], BF16))
                  for i in range(2)]
        tcorr = wes.enter_context(nc.sbuf_tensor("b_tcorr", [128, 16 * 256], F32))
        relf = wes.enter_context(nc.sbuf_tensor("b_relf", [32, 16], F32))
        relb = wes.enter_context(nc.sbuf_tensor("b_relb", [32, 16], BF16))
        sel = [wes.enter_context(nc.sbuf_tensor(f"b_sel{i}", [32, 4096], BF16))
               for i in range(2)]
        tcsb = [wes.enter_context(nc.sbuf_tensor(f"b_tcsb{i}", [16, 512], F32))
                for i in range(2)]
        avc = wes.enter_context(nc.sbuf_tensor("b_avc", [65, T], F32))
        lrow = wes.enter_context(nc.sbuf_tensor("b_lrow", [1, T], F32))
        linv = wes.enter_context(nc.sbuf_tensor("b_linv", [1, T], F32))
        bcb = [wes.enter_context(nc.sbuf_tensor(f"b_bcb{i}", [64, 512], F32))
               for i in range(2)]
        bo_row = wes.enter_context(nc.sbuf_tensor("b_bor", [1, D], F32))
        bo_bc = wes.enter_context(nc.sbuf_tensor("b_bobc", [128, D], F32))
        xr = [wes.enter_context(nc.sbuf_tensor(f"b_xr{i}", [128, D], F32))
              for i in range(2)]
        x1t = [wes.enter_context(nc.sbuf_tensor(f"b_x1t{i}", [128, D], F32))
               for i in range(2)]
        ps_sc = [wes.enter_context(nc.psum_tensor(f"b_psc{i}", [128, 512], F32))
                 for i in range(2)]
        av_ps = wes.enter_context(nc.psum_tensor("b_av", [65, T], F32))
        wo_ps = wes.enter_context(nc.psum_tensor("b_wo", [128, D], F32))

        dbg = (wes.enter_context(nc.sbuf_tensor("b_dbg", [128, 2064], F32))
               if B_DBG else None)
        S = Sems(nc, "b")
        qK = DmaQ(S, "dK", 4)
        qVS = DmaQ(S, "dVS", 2)
        qSel = DmaQ(S, "dSel", 2)
        qTc = DmaQ(S, "dTc", 2)
        qTr = DmaQ(S, "dTr", 2)
        qX = DmaQ(S, "dX", 2)
        qR = DmaQ(S, "dR", 2)
        qAo = DmaQ(S, "dAo", 2)
        qMisc = DmaQ(S, "dM", 2)

        NSC, NBI, NMEM = _B_NSC, _B_NBI, _B_NMEM
        IDX, CUMAV = _B_IDX, _B_CUMAV

        with nc.Block() as blk:
            @blk.sync
            def _(sync):
                qMisc.inc(sync.dma_start(relf[:, :], Dr["rel"][:, :]))
                qMisc.inc(sync.dma_start(bo_row[:, :], Dr["bo0"][:, :]))
                for s in range(NSEQ):
                    for h in range(HEADS):
                        hh = s * 16 + h
                        if hh >= 2:
                            sync.wait_ge(S("peS"), NSC * (hh - 1))
                        qK.inc(sync.dma_start(kh[hh % 2][:, :],
                                              Dr["kf0"][s, h * 64:h * 64 + 64, :]))
                        qK.inc(sync.dma_start(qh[hh % 2][:, :],
                                              Dr["qf0"][s, h * 64:h * 64 + 64, :]))
                    for it in range(16):
                        wt = s * 16 + it
                        if wt >= 2:
                            sync.wait_ge(S("gR"), 2 * (wt - 1))
                        t0 = (s * 16 + it) * 128
                        qX.inc(sync.dma_start(xr[wt % 2][:, :],
                                              Dr["x"][t0:t0 + 128, :]))
                        sync.wait_ge(S("gR"), 2 * wt + 2)
                        qR.inc(sync.dma_start(Dr["r1"][t0:t0 + 128, :],
                                              x1t[wt % 2][:, :]))
                if B_DBG:
                    sync.wait_ge(S("vDbg"), 1)
                    sync.dma_start(Dr["dbg"][:, :], dbg[:, :]).then_inc(S("dDbg"), 16)
                    sync.wait_ge(S("dDbg"), 16)
                qR.wait(sync, NSEQ * 16 - 1)
                qR.wait(sync, NSEQ * 16 - 2)

            @blk.gpsimd
            def _(g):
                for c in range(8):
                    if c >= 2:
                        g.wait_ge(S("peTc"), 8 * (c - 1))
                    qSel.inc(g.dma_start(
                        sel[c % 2][:, :],
                        Pc["selmat_dram"][:, c * 4096:(c + 1) * 4096]))
                for s in range(NSEQ):
                    g.wait_ge(S("peA"), NSC * 16 * s)
                    qVS.inc(g.dma_start(
                        v_sb[:, :].rearrange("p (jt f) -> p jt f", f=D),
                        Dr["vt0"][s].rearrange("(jt p) f -> p jt f", p=128)))
                    for h in range(1, HEADS, 2):
                        hh = s * 16 + h
                        oh = hh // 2
                        g.wait_ge(S("vAo"), 4 * (hh + 1))
                        qAo.inc(g.dma_start(
                            ao[64:128, (h // 2) * T:(h // 2 + 1) * T],
                            ao_tmp[oh % 2][:, :]))
                    for it in range(16):
                        wt = s * 16 + it
                        qX.wait(g, wt)
                        for half in range(2):
                            g.wait_ge(S("vW"), 2 * wt + half + 1)
                            g.tensor_tensor(
                                x1t[wt % 2][:, half * 512:(half + 1) * 512],
                                x1t[wt % 2][:, half * 512:(half + 1) * 512],
                                xr[wt % 2][:, half * 512:(half + 1) * 512],
                                ALU.add).then_inc(S("gR"), 1)

            @blk.tensor
            def _(t):
                qMisc.wait(t, 0)
                t.wait_ge(S("vRel"), 1)
                for c in range(64):
                    qSel.wait(t, c // 8)
                    if c >= 1:
                        t.wait_ge(S("sTc"), c)
                    t.matmul(ps_sc[0][0:16, 0:512], relb[:, :],
                             sel[(c // 8) % 2][:, (c % 8) * 512:(c % 8 + 1) * 512],
                             start=True, stop=True).then_inc(S("peTc"), 1)
                qMisc.wait(t, 1)
                t.wait_ge(S("sTc"), 64)
                for half in range(2):
                    t.matmul(ps_sc[1][:, 0:512], Pc["ones_f32"][0:1, :],
                             bo_row[0:1, half * 512:(half + 1) * 512],
                             start=True, stop=True).then_inc(S("peBo"), 1)
                    t.wait_ge(S("sBo"), half + 1)
                t.wait_ge(S("vOne"), 16)
                def emit_bc(hh2):
                    s2 = hh2 // 16
                    t.wait_ge(S("sL2"), hh2 + 1)
                    if s2 >= 1:
                        t.wait_ge(S("vW"), 32 * s2)
                    for c in range(4):
                        t.wait_ge(S("sBc"), 4 * hh2 + c)
                        t.matmul(wo_ps[0:64, 0:512],
                                 Pc["ones_f32"][0:1, 0:64],
                                 linv[0:1, c * 512:(c + 1) * 512],
                                 start=True, stop=True).then_inc(S("peBc"), 1)

                for s in range(NSEQ):
                    for h in range(HEADS):
                        hh = s * 16 + h
                        qK.wait(t, 2 * hh)
                        qK.wait(t, 2 * hh + 1)
                        for jt in range(16):
                            j0 = jt * 128
                            for (ac, lo, hi) in _b_chunks(jt):
                                idx = NSC * hh + IDX[(jt, ac)]
                                if idx >= 2:
                                    t.wait_ge(S("sE"), idx - 1)
                                t.matmul(ps_sc[idx % 2][:, 0:hi - lo],
                                         kh[hh % 2][:, j0:j0 + 128],
                                         qh[hh % 2][:, lo:hi],
                                         start=True, stop=True
                                         ).then_inc(S("peS"), 1)
                            if jt == 0 and hh >= 1:
                                t.wait_ge(S("vCp"), hh)
                            for (ac, lo2, hi2) in _b_chunks(jt):
                                eidx = NSC * hh + IDX[(jt, ac)]
                                t.wait_ge(S("sE"), eidx + 1)
                                if (jt * 128) % 512 and ac == jt // 4:
                                    t.wait_ge(S("vM"), NMEM * hh
                                              + sum(1 for j in range(jt + 1)
                                                    if (j * 128) % 512))
                                t.wait_ge(S("vC"), 16 * hh + jt + 1)
                                t.matmul(av_ps[:, ac * 512:(ac + 1) * 512],
                                         v65[:, jt * 65:jt * 65 + 65],
                                         A[jt % 2][:, ac * 512:(ac + 1) * 512],
                                         start=(jt == 0),
                                         stop=(jt == min(15, 4 * ac + 3))
                                         ).then_inc(S("peA"), 1)
                        if h >= 1:
                            emit_bc(hh - 1)
                    emit_bc(s * 16 + 15)
                    t.wait_ge(S("vAo"), 64 * (s + 1))
                    qAo.wait(t, 8 * (s + 1) - 1)
                    qAo.wait(t, 8 * (s + 1) - 2)
                    for it in range(16):
                        wt = s * 16 + it
                        if wt >= 1:
                            t.wait_ge(S("vW"), 2 * wt)
                        for half in range(2):
                            for ec in range(NE):
                                ins = t.matmul(
                                    wo_ps[:, half * 512:(half + 1) * 512],
                                    ao[:, ec * T + it * 128: ec * T + (it + 1) * 128],
                                    wo_t[:, ec * D + half * 512: ec * D + (half + 1) * 512],
                                    start=(ec == 0), stop=(ec == 7))
                            ins.then_inc(S("peW"), 1)

            @blk.scalar
            def _(s_):
                for c in range(64):
                    s_.wait_ge(S("peTc"), c + 1)
                    if c >= 2:
                        qTc.wait(s_, c - 2)
                    s_.copy(tcsb[c % 2][:, :], ps_sc[0][0:16, 0:512]
                            ).then_inc(S("sTc"), 1)
                    s_.wait_ge(S("sTc"), c + 1)
                    qTc.inc(s_.dma_start(Dr["tcorr_d"][:, c * 512:(c + 1) * 512],
                                         tcsb[c % 2][:, :]))
                for h in range(16):
                    qTc.wait(s_, 63)
                    qTc.wait(s_, 62)
                    qTr.inc(s_.dma_start(
                        tcorr[:, h * 256:(h + 1) * 256],
                        Dr["tcorr_d"][h].rearrange("(p i) -> p i", p=128)))
                for half in range(2):
                    s_.wait_ge(S("peBo"), half + 1)
                    s_.copy(bo_bc[:, half * 512:(half + 1) * 512],
                            ps_sc[1][:, 0:512]).then_inc(S("sBo"), 1)
                def post(hh2):
                    s_.wait_ge(S("vCp"), hh2 + 1)
                    s_.activation(lrow[0:1, :], avc[64:65, :],
                                  AF.Ln, scale=2.0 ** -44).then_inc(S("sL"), 1)
                    s_.wait_ge(S("sL"), 2 * hh2 + 1)
                    s_.activation(avc[64:65, :], lrow[0:1, :], AF.Relu,
                                  bias=Pc["c44"][0:1, :]).then_inc(S("sL"), 1)
                    s_.wait_ge(S("sL"), 2 * hh2 + 2)
                    s_.activation(linv[0:1, :], avc[64:65, :], AF.Exp,
                                  scale=-1.0,
                                  bias=Pc["cLE"][0:1, :]).then_inc(S("sL2"), 1)
                    for c in range(4):
                        s_.wait_ge(S("peBc"), 4 * hh2 + c + 1)
                        if 4 * hh2 + c >= 2:
                            s_.wait_ge(S("vAo"), 4 * hh2 + c - 1)
                        s_.copy(bcb[c % 2][:, :], wo_ps[0:64, 0:512]
                                ).then_inc(S("sBc"), 1)

                for s in range(NSEQ):
                    for h in range(HEADS):
                        hh = s * 16 + h
                        nbi = 0
                        for jt in range(16):
                            segs = _b_bias_segs(jt)
                            for (ac, lo, hi) in _b_chunks(jt):
                                idx = NSC * hh + IDX[(jt, ac)]
                                s_.wait_ge(S("peS"), idx + 1)
                                has_bias = any(a == ac for (a, _, _) in segs)
                                if has_bias:
                                    nbi += sum(1 for (a, _, _) in segs if a == ac)
                                    s_.wait_ge(S("vB"), NBI * hh + nbi)
                                if jt >= 2 and ac == jt // 4:
                                    s_.wait_ge(S("peA"),
                                               NSC * hh + CUMAV[jt - 1])
                                s_.activation(A[jt % 2][:, lo:hi],
                                              ps_sc[idx % 2][:, 0:hi - lo],
                                              AF.Exp).then_inc(S("sE"), 1)
                        if h >= 1:
                            post(hh - 1)
                    post(s * 16 + 15)

            @blk.vector
            def _(v):
                def vmults(hh2):
                    s2, h2 = hh2 // 16, hh2 % 16
                    oh = hh2 // 2
                    for c in range(4):
                        v.wait_ge(S("sBc"), 4 * hh2 + c + 1)
                        if hh2 >= 1 and c == 0:
                            v.wait_ge(S("peW"), 32 * s2)
                        if h2 % 2 and c == 0 and oh >= 2:
                            qAo.wait(v, oh - 2)
                        dst = (ao[0:64, (h2 // 2) * T + c * 512:
                                   (h2 // 2) * T + (c + 1) * 512]
                               if h2 % 2 == 0 else
                               ao_tmp[oh % 2][:, c * 512:(c + 1) * 512])
                        v.tensor_tensor(
                            dst, avc[0:64, c * 512:(c + 1) * 512],
                            bcb[c % 2][:, :], ALU.mult).then_inc(S("vAo"), 1)

                for jt in range(16):
                    v.memset(v65[:, jt * 65 + 64: jt * 65 + 65],
                             1.0).then_inc(S("vOne"), 1)
                qMisc.wait(v, 0)
                v.tensor_scalar(relb[:, :], relf[:, :], SCALE, None,
                                ALU.mult).then_inc(S("vRel"), 1)
                qTr.wait(v, 14)
                qTr.wait(v, 15)
                for h in range(16):
                    if B_NOBIAS:
                        v.tensor_copy(tcorr[:, h * 256:(h + 1) * 256],
                                      Pc["maskband"][:, :]).then_inc(S("vTc"), 1)
                    else:
                        v.tensor_tensor(tcorr[:, h * 256:(h + 1) * 256],
                                        tcorr[:, h * 256:(h + 1) * 256],
                                        Pc["maskband"][:, :], ALU.add
                                        ).then_inc(S("vTc"), 1)
                for s in range(NSEQ):
                    for h in range(HEADS):
                        hh = s * 16 + h
                        if h == 0:
                            qVS.wait(v, s)
                        if hh >= 1:
                            v.wait_ge(S("peA"), NSC * hh)
                        for jt in range(16):
                            v.tensor_copy(v65[:, jt * 65:jt * 65 + 64],
                                          v_sb[:, jt * D + h * 64: jt * D + h * 64 + 64]
                                          ).then_inc(S("vC"), 1)
                        if hh == 0:
                            v.wait_ge(S("vTc"), 16)
                        nbi = 0
                        nmem = 0
                        for jt in range(16):
                            j0 = jt * 128
                            if jt >= 2:
                                v.wait_ge(S("peA"), NSC * hh + CUMAV[jt - 1])
                            if j0 % 512:
                                nmem += 1
                                c0 = j0 // 512
                                v.memset(A[jt % 2][:, c0 * 512:j0], 0.0
                                         ).then_inc(S("vM"), 1)
                            for (ac, blo, bhi) in _b_bias_segs(jt):
                                idx = NSC * hh + IDX[(jt, ac)]
                                clo = max(j0, ac * 512)
                                v.wait_ge(S("peS"), idx + 1)
                                nbi += 1
                                v.tensor_tensor(
                                    ps_sc[idx % 2][:, blo - clo:bhi - clo],
                                    ps_sc[idx % 2][:, blo - clo:bhi - clo],
                                    tcorr[:, h * 256 + blo - j0: h * 256 + bhi - j0],
                                    ALU.add).then_inc(S("vB"), 1)
                        if h >= 1:
                            vmults(hh - 1)
                        v.wait_ge(S("peA"), NSC * (hh + 1))
                        v.wait_ge(S("vAo"), 4 * hh)
                        v.tensor_copy(avc[:, :], av_ps[0:65, :]
                                      ).then_inc(S("vCp"), 1)
                    vmults(s * 16 + 15)
                    for it in range(16):
                        wt = s * 16 + it
                        if wt >= 2:
                            qR.wait(v, wt - 2)
                        for half in range(2):
                            v.wait_ge(S("peW"), 2 * wt + half + 1)
                            v.scalar_tensor_tensor(
                                x1t[wt % 2][:, half * 512:(half + 1) * 512],
                                wo_ps[:, half * 512:(half + 1) * 512],
                                1.0, bo_bc[:, half * 512:(half + 1) * 512],
                                ALU.mult, ALU.add).then_inc(S("vW"), 1)
                    if s == 0 and B_DBG:
                        v.tensor_copy(dbg[:, 0:2048], ao[:, 0:2048]
                                      ).then_inc(S("vDbg"), 1)
        S.release()



def emit_phase_c(nc, es, Dr, Pc):
    with ExitStack() as wes:
        rin = [wes.enter_context(nc.sbuf_tensor(f"c_rin{i}", [128, D], F32))
               for i in range(2)]
        xsq = wes.enter_context(nc.sbuf_tensor("c_xsq", [128, D], F32))
        xi = wes.enter_context(nc.sbuf_tensor("c_xi", [128, D], F32))
        t1 = wes.enter_context(nc.sbuf_tensor("c_t1", [128, D], F32))
        t2 = wes.enter_context(nc.sbuf_tensor("c_t2", [128, D], F32))
        xnb = wes.enter_context(nc.sbuf_tensor("c_xnb", [128, D], BF16))
        cols = wes.enter_context(nc.sbuf_tensor("c_cols", [128, 16], F32))
        mx = wes.enter_context(nc.sbuf_tensor("c_mx", [128, NT], F32))
        xfm = wes.enter_context(nc.sbuf_tensor("c_xfm", [128, NE * 512], BF16))
        gb = wes.enter_context(nc.sbuf_tensor("c_gb", [128, D], F32))
        bb = wes.enter_context(nc.sbuf_tensor("c_bb", [128, D], F32))
        gsc = wes.enter_context(nc.sbuf_tensor("c_gsc", [1, 8], F32))
        lrow = wes.enter_context(nc.sbuf_tensor("c_lrow", [1, D], F32))
        wstg = [wes.enter_context(nc.sbuf_tensor(f"c_wstg{i}", [128, D], F32))
                for i in range(2)]
        wqt = [wes.enter_context(nc.sbuf_tensor(f"c_wqt{i}", [128, D], BF16))
               for i in range(2)]
        wcols = wes.enter_context(nc.sbuf_tensor("c_wcols", [128, 8], F32))
        ones128 = wes.enter_context(nc.sbuf_tensor("c_ones", [128, 1], F32))
        pst = [wes.enter_context(nc.psum_tensor(f"c_pst{i}", [128, 1024], BF16))
               for i in range(2)]
        psb = wes.enter_context(nc.psum_tensor("c_psb", [128, 512], F32))
        S = Sems(nc, "c")
        qC = DmaQ(S, "dC", 2)
        qW = DmaQ(S, "dW", 2)
        qN = DmaQ(S, "dN", 2)
        qQn = DmaQ(S, "dQn", 2)
        qRow = DmaQ(S, "dRow", 2)
        qG = DmaQ(S, "dG", 2)

        with nc.Block() as blk:
            @blk.sync
            def _(sync):
                qRow.inc(sync.dma_start(lrow[0:1, :], Dr["lng"][:, :]))
                for g in range(NG):
                    for k in range(4):
                        tt = g * 4 + k
                        if tt >= 2:
                            sync.wait_ge(S("lnvXi"), tt - 1)
                        t0 = tt * 128
                        qC.inc(sync.dma_start(rin[tt % 2][:, :],
                                              Dr["r1"][t0:t0 + 128, :]))
                    sync.wait_ge(S("sC"), 32 * (g + 1))
                    qN.inc(sync.dma_start(
                        Dr["xn1g"][g].rearrange("e p t -> p e t"),
                        xfm[:, :].rearrange("p (e t) -> p e t", t=512)))
                qN.wait(sync, NG - 1)
                qN.wait(sync, NG - 2)

            @blk.gpsimd
            def _(g_):
                g_.memset(ones128[:, :], 1.0)
                for tt in range(NT):
                    g_.wait_ge(S("vT2"), tt + 1)
                    if tt >= 1:
                        g_.wait_ge(S("pA"), 8 * tt)
                        g_.wait_ge(S("vD"), tt)
                    g_.tensor_tensor(xnb[:, :], t2[:, :], bb[:, :],
                                     ALU.add).then_inc(S("gXn"), 1)
                for n in range(16):
                    dc = n % 8
                    if 2 <= n < 10:
                        g_.wait_ge(S("vWc"), n - 1)
                    elif n >= 10:
                        g_.wait_ge(S("vWq"), n - 9)
                    qW.inc(g_.dma_start(wstg[n % 2][:, :],
                                        Dr["wffn"][dc * 128:(dc + 1) * 128, :]))
                g_.wait_ge(S("vBeta"), 1)
                qG.inc(g_.dma_start(Dr["beta_d"][:, :], gsc[0:1, 0:1]))
                g_.wait_ge(S("vG"), 1)
                qG.inc(g_.dma_start(Dr["cc_in"][:, :], gsc[0:1, 2:3]))
                qG.wait(g_, 1)
                g_.collective_compute(
                    "AllReduce", ALU.max,
                    replica_groups=[list(range(NCORES))],
                    ins=[Dr["cc_in"].ap().opt()],
                    outs=[Dr["cc_out"].ap().opt()]).then_inc(S("cc"), 1)
                g_.wait_ge(S("cc"), 1)
                qG.inc(g_.dma_start(Dr["gmax_d"][:, :], Dr["cc_out"][:, :]))
                qG.wait(g_, 2)
                qG.wait(g_, 0)

            @blk.scalar
            def _(s_):
                for half in range(2):
                    s_.wait_ge(S("peGB"), half + 1)
                    s_.copy(gb[:, half * 512:(half + 1) * 512],
                            psb[:, 0:512]).then_inc(S("sGBe"), 1)
                qRow.inc(s_.dma_start(lrow[0:1, :], Dr["lnb"][:, :]))
                for half in range(2):
                    s_.wait_ge(S("peGB"), half + 3)
                    s_.copy(bb[:, half * 512:(half + 1) * 512],
                            psb[:, 0:512]).then_inc(S("sGBe"), 1)
                for tt in range(NT):
                    qC.wait(s_, tt)
                    if tt >= 1:
                        s_.wait_ge(S("lnvA"), 4 * tt)
                    s_.activation(xsq[:, :], rin[tt % 2][:, :],
                                  AF.Square).then_inc(S("lnsA"), 1)
                    s_.wait_ge(S("lnvB"), 2 * tt + 1)
                    s_.activation(cols[:, 6:7], cols[:, 5:6], AF.Sqrt, scale=-1.0,
                                  bias=Pc["c1e5"]).then_inc(S("lnsB"), 1)
                    s_.wait_ge(S("lnvXi"), tt + 1)
                    s_.activation(xsq[:, :], xi[:, :],
                                  AF.Square).then_inc(S("lnsA"), 1)
                    s_.wait_ge(S("lnvB"), 2 * tt + 2)
                    s_.activation(cols[:, 14:15], cols[:, 13:14], AF.Sqrt, scale=-1.0,
                                  bias=Pc["c1e3"]).then_inc(S("lnsB"), 1)
                    for ei in range(NE):
                        kk = tt * 8 + ei
                        s_.wait_ge(S("pA"), kk + 1)
                        if tt % 4 == 0 and ei == 0 and tt >= 4:
                            qN.wait(s_, tt // 4 - 1)
                        k = tt % 4
                        s_.copy(xfm[:, ei * 512 + k * 128: ei * 512 + (k + 1) * 128],
                                pst[kk % 2][:, 0:128]).then_inc(S("sC"), 1)
                s_.wait_ge(S("peWbM"), 1)
                s_.copy(cols[:, 12:13], psb[:, 4:5]).then_inc(S("sWb"), 1)
                for dc in range(8):
                    s_.wait_ge(S("vWq"), dc + 1)
                    qQn.inc(s_.dma_start(Dr["wqn_d"][dc * 128:(dc + 1) * 128, :],
                                         wqt[dc % 2][:, :]))
                qQn.wait(s_, 7)
                qQn.wait(s_, 6)

            @blk.vector
            def _(v):
                for tt in range(NT):
                    qC.wait(v, tt)
                    v.reduce_sum(cols[:, 0:1], rin[tt % 2][:, :],
                                 axis=AX).then_inc(S("lnvA"), 1)
                    v.wait_ge(S("lnsA"), 2 * tt + 1)
                    v.reduce_sum(cols[:, 1:2], xsq[:, :],
                                 axis=AX).then_inc(S("lnvA"), 1)
                    v.wait_ge(S("lnvA"), 4 * tt + 2)
                    v.tensor_scalar(cols[:, 2:3], cols[:, 0:1], 1.0 / D, None, ALU.mult)
                    v.tensor_scalar(cols[:, 3:4], cols[:, 1:2], 1.0 / D, None,
                                    ALU.mult).then_inc(S("lnvM"), 1)
                    v.wait_ge(S("lnvM"), 2 * tt + 1)
                    v.scalar_tensor_tensor(cols[:, 5:6], cols[:, 2:3], cols[:, 2:3],
                                           cols[:, 3:4], ALU.mult,
                                           ALU.subtract).then_inc(S("lnvB"), 1)
                    v.wait_ge(S("lnsB"), 2 * tt + 1)
                    v.reciprocal(cols[:, 7:8], cols[:, 6:7]).then_inc(S("lnvC"), 1)
                    v.wait_ge(S("lnvC"), 2 * tt + 1)
                    v.tensor_scalar(xi[:, :], rin[tt % 2][:, :], cols[:, 2:3],
                                    cols[:, 7:8], ALU.subtract,
                                    ALU.mult).then_inc(S("lnvXi"), 1)
                    v.wait_ge(S("lnvXi"), tt + 1)
                    v.reduce_sum(cols[:, 8:9], xi[:, :], axis=AX).then_inc(S("lnvA"), 1)
                    v.wait_ge(S("lnsA"), 2 * tt + 2)
                    v.reduce_sum(cols[:, 9:10], xsq[:, :],
                                 axis=AX).then_inc(S("lnvA"), 1)
                    v.wait_ge(S("lnvA"), 4 * tt + 4)
                    v.tensor_scalar(cols[:, 10:11], cols[:, 8:9], 1.0 / D, None, ALU.mult)
                    v.tensor_scalar(cols[:, 11:12], cols[:, 9:10], 1.0 / D, None,
                                    ALU.mult).then_inc(S("lnvM"), 1)
                    v.wait_ge(S("lnvM"), 2 * tt + 2)
                    v.scalar_tensor_tensor(cols[:, 13:14], cols[:, 10:11], cols[:, 10:11],
                                           cols[:, 11:12], ALU.mult,
                                           ALU.subtract).then_inc(S("lnvB"), 1)
                    v.wait_ge(S("lnsB"), 2 * tt + 2)
                    v.reciprocal(cols[:, 15:16], cols[:, 14:15]).then_inc(S("lnvC"), 1)
                    v.wait_ge(S("lnvC"), 2 * tt + 2)
                    if tt == 0:
                        v.wait_ge(S("sGBe"), 4)
                    if tt >= 1:
                        v.wait_ge(S("gXn"), tt)
                    v.tensor_scalar(t1[:, :], xi[:, :], cols[:, 10:11],
                                    cols[:, 15:16], ALU.subtract,
                                    ALU.mult).then_inc(S("vT1"), 1)
                    v.wait_ge(S("vT1"), tt + 1)
                    v.tensor_tensor(t2[:, :], t1[:, :], gb[:, :],
                                    ALU.mult).then_inc(S("vT2"), 1)
                    v.wait_ge(S("gXn"), tt + 1)
                    v.reduce_max(mx[:, tt:tt + 1], xnb[:, :], axis=AX,
                                 apply_absolute_value=True).then_inc(S("vD"), 1)
                for dc in range(8):
                    qW.wait(v, dc)
                    v.reduce_sum(wcols[:, dc:dc + 1], wstg[dc % 2][:, :], axis=AX,
                                 apply_absolute_value=True).then_inc(S("vWc"), 1)
                v.wait_ge(S("vWc"), 8)
                v.reduce_sum(cols[:, 0:1], wcols[:, 0:8], axis=AX).then_inc(S("vWs"), 1)
                v.wait_ge(S("vWs"), 1)
                v.wait_ge(S("peB"), 1)
                v.tensor_scalar(gsc[0:1, 0:1], psb[0:1, 0:1], 1.0 / (D * D), None,
                                ALU.mult).then_inc(S("vBeta"), 1)
                v.wait_ge(S("vBeta"), 1)
                v.tensor_scalar(gsc[0:1, 1:2], gsc[0:1, 0:1], 1e-5, None, ALU.add
                                ).then_inc(S("vBeta2"), 1)
                v.wait_ge(S("vBeta2"), 1)
                v.reciprocal(gsc[0:1, 4:5], gsc[0:1, 1:2]).then_inc(S("vBinv"), 1)
                v.wait_ge(S("vBinv"), 1)
                for dc in range(8):
                    n = 8 + dc
                    qW.wait(v, n)
                    v.wait_ge(S("sWb"), 1)
                    if dc >= 2:
                        qQn.wait(v, dc - 2)
                    v.tensor_scalar(t1[:, :], wstg[n % 2][:, :], cols[:, 12:13],
                                    12582912.0, ALU.mult,
                                    ALU.add).then_inc(S("vWq1"), 1)
                    v.wait_ge(S("vWq1"), dc + 1)
                    v.tensor_scalar(t2[:, :], t1[:, :], 12582912.0, 1.0,
                                    ALU.subtract, ALU.min).then_inc(S("vWq2"), 1)
                    v.wait_ge(S("vWq2"), dc + 1)
                    v.tensor_scalar(wqt[dc % 2][:, :], t2[:, :], -1.0, None,
                                    ALU.max).then_inc(S("vWq"), 1)
                v.wait_ge(S("vD"), NT)
                v.reduce_max(cols[:, 0:1], mx[:, :], axis=AX).then_inc(S("vGc"), 1)
                v.wait_ge(S("vGc"), 1)
                v.wait_ge(S("peG"), 1)
                v.reduce_max(gsc[0:1, 2:3], psb[0:1, 0:128], axis=AX
                             ).then_inc(S("vGr"), 1)
                v.wait_ge(S("vGr"), 1)
                v.tensor_copy(gsc[0:1, 3:4], gsc[0:1, 2:3]).then_inc(S("vG"), 1)

            @blk.tensor
            def _(t):
                qRow.wait(t, 0)
                for half in range(2):
                    t.matmul(psb[:, 0:512], Pc["ones_f32"][0:1, :],
                             lrow[0:1, half * 512:(half + 1) * 512],
                             start=True, stop=True).then_inc(S("peGB"), 1)
                    t.wait_ge(S("sGBe"), half + 1)
                qRow.wait(t, 1)
                for half in range(2):
                    t.matmul(psb[:, 0:512], Pc["ones_f32"][0:1, :],
                             lrow[0:1, half * 512:(half + 1) * 512],
                             start=True, stop=True).then_inc(S("peGB"), 1)
                    t.wait_ge(S("sGBe"), half + 3)
                for tt in range(NT):
                    t.wait_ge(S("gXn"), tt + 1)
                    for ei in range(NE):
                        kk = tt * 8 + ei
                        if kk >= 2:
                            t.wait_ge(S("sC"), kk - 1)
                        t.transpose(pst[kk % 2][:, 0:128],
                                    xnb[:, ei * 128:(ei + 1) * 128],
                                    Pc["ident_bf"][:, :]).then_inc(S("pA"), 1)
                t.wait_ge(S("vWs"), 1)
                t.matmul(psb[0:1, 0:1], cols[:, 0:1], ones128[:, :],
                         start=True, stop=True).then_inc(S("peB"), 1)
                t.wait_ge(S("vBinv"), 1)
                t.matmul(psb[:, 4:5], Pc["ones_f32"][0:1, :], gsc[0:1, 4:5],
                         start=True, stop=True).then_inc(S("peWbM"), 1)
                t.wait_ge(S("vGc"), 1)
                t.wait_ge(S("sWb"), 1)
                t.transpose(psb[0:1, 0:128], cols[:, 0:1],
                            Pc["ident_f32"][:, :]).then_inc(S("peG"), 1)
        S.release()



def emit_phase_d(nc, es, Dr, Pc):
    with ExitStack() as wes:
        xq = [wes.enter_context(nc.sbuf_tensor(f"d_xq{i}", [128, NE * 512], BF16))
              for i in range(2)]
        wq_sb = wes.enter_context(nc.sbuf_tensor("d_wq", [128, NE * D], BF16))
        rr = [wes.enter_context(nc.sbuf_tensor(f"d_rr{i}", [128, D], F32))
              for i in range(2)]
        x2t = [wes.enter_context(nc.sbuf_tensor(f"d_x2{i}", [128, D], F32))
               for i in range(2)]
        xsq = wes.enter_context(nc.sbuf_tensor("d_xsq", [128, D], F32))
        xnb = wes.enter_context(nc.sbuf_tensor("d_xnb", [128, D], BF16))
        cols = wes.enter_context(nc.sbuf_tensor("d_cols", [128, 8], F32))
        gsc = wes.enter_context(nc.sbuf_tensor("d_gsc", [1, 8], F32))
        scol = wes.enter_context(nc.sbuf_tensor("d_scol", [128, 2], F32))
        xfm = wes.enter_context(nc.sbuf_tensor("d_xfm", [128, NE * 512], BF16))
        ps = [wes.enter_context(nc.psum_tensor(f"d_ps{i}", [128, 512], F32))
              for i in range(2)]
        pst = [wes.enter_context(nc.psum_tensor(f"d_pst{i}", [128, 1024], BF16))
               for i in range(2)]
        psb = wes.enter_context(nc.psum_tensor("d_psb", [128, 512], F32))
        S = Sems(nc, "d")
        qX = DmaQ(S, "dX", 2)
        qR = DmaQ(S, "dR", 2)
        qO = DmaQ(S, "dO", 2)
        qN = DmaQ(S, "dN", 2)
        qMs = DmaQ(S, "dMs", 2)
        qMw = DmaQ(S, "dMw", 2)

        with nc.Block() as blk:
            @blk.sync
            def _(sync):
                qMs.inc(sync.dma_start(gsc[0:1, 0:1], Dr["gmax_d"][:, :]))
                qMs.inc(sync.dma_start(gsc[0:1, 1:2], Dr["beta_d"][:, :]))
                for g in range(NG):
                    if g >= 2:
                        sync.wait_ge(S("peM"), 8 * (g - 1))
                    qX.inc(sync.dma_start(
                        xq[g % 2][:, :].rearrange("p (e t) -> p e t", t=512),
                        Dr["xn1g"][g].rearrange("e p t -> p e t")))
                    for k in range(4):
                        tt = g * 4 + k
                        if tt >= 2:
                            sync.wait_ge(S("vE"), 2 * (tt - 1))
                        t0 = tt * 128
                        qR.inc(sync.dma_start(rr[tt % 2][:, :],
                                              Dr["r1"][t0:t0 + 128, :]))
                        sync.wait_ge(S("vE"), 2 * tt + 2)
                        qO.inc(sync.dma_start(Dr["r2"][t0:t0 + 128, :],
                                              x2t[tt % 2][:, :]))
                    sync.wait_ge(S("sC"), 32 * (g + 1))
                    qN.inc(sync.dma_start(
                        Dr["xn2g"][g].rearrange("e p t -> p e t"),
                        xfm[:, :].rearrange("p (e t) -> p e t", t=512)))
                qN.wait(sync, NG - 1)
                qN.wait(sync, NG - 2)
                qO.wait(sync, NT - 1)
                qO.wait(sync, NT - 2)

            @blk.gpsimd
            def _(g_):
                for dc in range(8):
                    qMw.inc(g_.dma_start(wq_sb[:, dc * D:(dc + 1) * D],
                                         Dr["wqn_d"][dc * 128:(dc + 1) * 128, :]))

            @blk.vector
            def _(v):
                qMs.wait(v, 0)
                qMs.wait(v, 1)
                v.reciprocal(gsc[0:1, 2:3], gsc[0:1, 0:1]).then_inc(S("vS"), 1)
                v.wait_ge(S("vS"), 1)
                v.tensor_scalar(gsc[0:1, 3:4], gsc[0:1, 2:3], 128.0, None, ALU.mult)
                v.tensor_tensor(gsc[0:1, 4:5], gsc[0:1, 0:1], gsc[0:1, 1:2],
                                ALU.mult).then_inc(S("vS1b"), 1)
                v.wait_ge(S("vS1b"), 1)
                v.tensor_scalar(gsc[0:1, 5:6], gsc[0:1, 4:5], 1.0 / 128.0, None,
                                ALU.mult).then_inc(S("vS2"), 1)
                for g in range(NG):
                    qX.wait(v, g)
                    if g == 0:
                        v.wait_ge(S("sSc"), 1)
                    v.tensor_scalar(xq[g % 2][:, :], xq[g % 2][:, :], scol[:, 0:1],
                                    127.99999, ALU.mult,
                                    ALU.min).then_inc(S("vQa"), 1)
                    v.wait_ge(S("vQa"), g + 1)
                    v.tensor_scalar(xq[g % 2][:, :], xq[g % 2][:, :], -127.99999, None,
                                    ALU.max).then_inc(S("vQ"), 1)
                    for k in range(4):
                        tt = g * 4 + k
                        if tt >= 2:
                            qO.wait(v, tt - 2)
                            v.wait_ge(S("lnsA"), tt - 1)
                        for half in range(2):
                            n = 2 * tt + half
                            v.wait_ge(S("peM"), n + 1)
                            qR.wait(v, tt)
                            v.scalar_tensor_tensor(
                                x2t[tt % 2][:, half * 512:(half + 1) * 512],
                                ps[n % 2][:, :], scol[:, 1:2],
                                rr[tt % 2][:, half * 512:(half + 1) * 512],
                                ALU.mult, ALU.add).then_inc(S("vE"), 1)
                        v.wait_ge(S("vE"), 2 * tt + 2)
                        v.reduce_sum(cols[:, 0:1], x2t[tt % 2][:, :],
                                     axis=AX).then_inc(S("lnvA"), 1)
                        v.wait_ge(S("lnsA"), tt + 1)
                        v.reduce_sum(cols[:, 1:2], xsq[:, :],
                                     axis=AX).then_inc(S("lnvA"), 1)
                        v.wait_ge(S("lnvA"), 2 * tt + 2)
                        v.tensor_scalar(cols[:, 2:3], cols[:, 0:1], 1.0 / D, None, ALU.mult)
                        v.tensor_scalar(cols[:, 3:4], cols[:, 1:2], 1.0 / D, None,
                                        ALU.mult).then_inc(S("lnvM"), 1)
                        v.wait_ge(S("lnvM"), tt + 1)
                        v.scalar_tensor_tensor(cols[:, 5:6], cols[:, 2:3], cols[:, 2:3],
                                               cols[:, 3:4], ALU.mult,
                                               ALU.subtract).then_inc(S("lnvB"), 1)
                        v.wait_ge(S("lnsB"), tt + 1)
                        v.reciprocal(cols[:, 7:8], cols[:, 6:7]).then_inc(S("lnvC"), 1)
                        v.wait_ge(S("lnvC"), tt + 1)
                        if tt >= 1:
                            v.wait_ge(S("pA"), 8 * tt)
                        v.tensor_scalar(xnb[:, :], x2t[tt % 2][:, :], cols[:, 2:3],
                                        cols[:, 7:8], ALU.subtract,
                                        ALU.mult).then_inc(S("vD"), 1)

            @blk.scalar
            def _(s_):
                s_.wait_ge(S("peSc"), 2)
                s_.copy(scol[:, 0:2], psb[:, 0:2]).then_inc(S("sSc"), 1)
                for tt in range(NT):
                    v_n = 2 * tt
                    s_.wait_ge(S("vE"), 2 * tt + 2)
                    if tt >= 1:
                        s_.wait_ge(S("lnvA"), 2 * tt)
                    s_.activation(xsq[:, :], x2t[tt % 2][:, :],
                                  AF.Square).then_inc(S("lnsA"), 1)
                    s_.wait_ge(S("lnvB"), tt + 1)
                    s_.activation(cols[:, 6:7], cols[:, 5:6], AF.Sqrt, scale=-1.0,
                                  bias=Pc["c1e5"]).then_inc(S("lnsB"), 1)
                    for ei in range(NE):
                        kk = tt * 8 + ei
                        s_.wait_ge(S("pA"), kk + 1)
                        if tt % 4 == 0 and ei == 0 and tt >= 4:
                            qN.wait(s_, tt // 4 - 1)
                        k = tt % 4
                        s_.copy(xfm[:, ei * 512 + k * 128: ei * 512 + (k + 1) * 128],
                                pst[kk % 2][:, 0:128]).then_inc(S("sC"), 1)

            @blk.tensor
            def _(t):
                t.wait_ge(S("vS2"), 1)
                t.matmul(psb[:, 0:1], Pc["ones_f32"][0:1, :], gsc[0:1, 3:4],
                         start=True, stop=True)
                t.matmul(psb[:, 1:2], Pc["ones_f32"][0:1, :], gsc[0:1, 5:6],
                         start=True, stop=True).then_inc(S("peSc"), 2)
                qMw.wait(t, 6)
                qMw.wait(t, 7)
                for g in range(NG):
                    t.wait_ge(S("vQ"), g + 1)
                    for k in range(4):
                        tt = g * 4 + k
                        for half in range(2):
                            n = 2 * tt + half
                            if n >= 2:
                                t.wait_ge(S("vE"), n - 1)
                            for dc in range(NE):
                                ins = t.matmul(
                                    ps[n % 2][:, :],
                                    xq[g % 2][:, dc * 512 + k * 128: dc * 512 + (k + 1) * 128],
                                    wq_sb[:, dc * D + half * 512: dc * D + (half + 1) * 512],
                                    start=(dc == 0), stop=(dc == 7))
                            ins.then_inc(S("peM"), 1)
                        t.wait_ge(S("vD"), tt + 1)
                        for ei in range(NE):
                            kk = tt * 8 + ei
                            if kk >= 2:
                                t.wait_ge(S("sC"), kk - 1)
                            t.transpose(pst[kk % 2][:, 0:128],
                                        xnb[:, ei * 128:(ei + 1) * 128],
                                        Pc["ident_bf"][:, :]).then_inc(S("pA"), 1)
        S.release()



def emit_phase_e(nc, es, Dr, Pc):
    with ExitStack() as wes:
        wts = emit_load_wT(nc, wes, [Dr["wq1"], Dr["wk1"], Dr["wv1"]], "ew",
                           Pc["ident_bf"])
        bcols = emit_load_bias_cols(nc, wes, [Dr["bq1"], Dr["bk1"], Dr["bv1"]],
                                    "eb", Pc["ident_f32"])
        xg = [wes.enter_context(nc.sbuf_tensor(f"e_xg{i}", [128, NE * 512], BF16))
              for i in range(2)]
        qt = [wes.enter_context(nc.sbuf_tensor(f"e_qt{i}", [128, 512], BF16))
              for i in range(2)]
        vfm = wes.enter_context(nc.sbuf_tensor("e_vfm", [128, NE * 512], BF16))
        vtm = [wes.enter_context(nc.sbuf_tensor(f"e_vtm{i}", [128, D], BF16))
               for i in range(2)]
        pstv = [wes.enter_context(nc.psum_tensor(f"e_pstv{i}", [128, 1024], BF16))
                for i in range(2)]
        psq = [wes.enter_context(nc.psum_tensor(f"e_psq{i}", [128, 512], F32))
               for i in range(2)]
        S = Sems(nc, "e")
        qX = DmaQ(S, "dX", 2)
        qQ = DmaQ(S, "dQ", 2)
        qV = DmaQ(S, "dV", 2)

        with nc.Block() as blk:
            @blk.sync
            def _(sync):
                for g in range(NG):
                    sq = g // 4
                    tg0 = (g % 4) * 512
                    if g >= 2:
                        sync.wait_ge(S("pB"), 24 * (g - 1))
                    qX.inc(sync.dma_start(
                        xg[g % 2][:, :].rearrange("p (e t) -> p e t", t=512),
                        Dr["xn2g"][g].rearrange("e p t -> p e t")))
                    for wi in range(2):
                        for fi in range(NE):
                            sync.wait_ge(S("sEv"), g * 24 + wi * 8 + fi + 1)
                            m = g * 16 + wi * 8 + fi + 1
                            dst = Dr["qf1"] if wi == 0 else Dr["kf1"]
                            qQ.inc(sync.dma_start(
                                dst[sq, fi * 128:(fi + 1) * 128, tg0:tg0 + 512],
                                qt[m % 2][:, :]))
                    for k2 in range(4):
                        sync.wait_ge(S("sV2"), g * 32 + k2 * 8 + 8)
                        n = g * 4 + k2 + 1
                        qV.inc(sync.dma_start(
                            Dr["vt1"][sq, tg0 + k2 * 128: tg0 + (k2 + 1) * 128, :],
                            vtm[n % 2][:, :]))
                qQ.wait(sync, NG * 16 - 1)
                qQ.wait(sync, NG * 16 - 2)
                qV.wait(sync, NG * 4 - 1)
                qV.wait(sync, NG * 4 - 2)

            @blk.scalar
            def _(s_):
                for g in range(NG):
                    for wi in range(3):
                        for fi in range(NE):
                            n = g * 24 + wi * 8 + fi + 1
                            s_.wait_ge(S("pB"), n)
                            if wi < 2:
                                m = g * 16 + wi * 8 + fi + 1
                                if m > 2:
                                    qQ.wait(s_, m - 3)
                                s_.activation(qt[m % 2][:, :], psq[n % 2][:, :],
                                              AF.Identity,
                                              bias=bcols[wi][:, fi:fi + 1]
                                              ).then_inc(S("sEv"), 1)
                            else:
                                if g >= 1 and fi == 0:
                                    s_.wait_ge(S("pV"), 32 * g)
                                s_.activation(vfm[:, fi * 512:(fi + 1) * 512],
                                              psq[n % 2][:, :], AF.Identity,
                                              bias=bcols[2][:, fi:fi + 1]
                                              ).then_inc(S("sEv"), 1)
                    for k2 in range(4):
                        for fi in range(NE):
                            kk = g * 32 + k2 * 8 + fi
                            s_.wait_ge(S("pV"), kk + 1)
                            n = g * 4 + k2 + 1
                            if n > 2 and fi == 0:
                                qV.wait(s_, n - 3)
                            s_.copy(vtm[n % 2][:, fi * 128:(fi + 1) * 128],
                                    pstv[kk % 2][:, 0:128]).then_inc(S("sV2"), 1)

            @blk.tensor
            def _(t):
                for g in range(NG):
                    qX.wait(t, g)
                    for wi in range(3):
                        for fi in range(NE):
                            n = g * 24 + wi * 8 + fi + 1
                            for ei in range(NE):
                                if ei == 0 and n > 2:
                                    t.wait_ge(S("sEv"), n - 2)
                                ins = t.matmul(
                                    psq[n % 2][:, :],
                                    wts[wi][:, ei * D + fi * 128: ei * D + fi * 128 + 128],
                                    xg[g % 2][:, ei * 512:(ei + 1) * 512],
                                    start=(ei == 0), stop=(ei == 7))
                                if ei == 7:
                                    ins.then_inc(S("pB"), 1)
                    t.wait_ge(S("sEv"), g * 24 + 24)
                    for k2 in range(4):
                        for fi in range(NE):
                            kk = g * 32 + k2 * 8 + fi
                            if kk >= 2:
                                t.wait_ge(S("sV2"), kk - 1)
                            t.transpose(pstv[kk % 2][:, 0:128],
                                        vfm[:, fi * 512 + k2 * 128: fi * 512 + (k2 + 1) * 128],
                                        Pc["ident_bf"][:, :]).then_inc(S("pV"), 1)
        S.release()




def emit_phase_f(nc, es, Dr, Pc):
    NAV = 32
    with ExitStack() as wes:
        k2 = wes.enter_context(nc.sbuf_tensor("f_k2", [128, NE * T], BF16))
        q2 = wes.enter_context(nc.sbuf_tensor("f_q2", [128, NE * T], BF16))
        v2 = wes.enter_context(nc.sbuf_tensor("f_v2", [128, 16 * D], BF16))
        Aall = wes.enter_context(nc.sbuf_tensor("f_A", [128, 16 * T], BF16))
        lcol = wes.enter_context(nc.sbuf_tensor("f_lcol", [128, 2], F32))
        rr = [wes.enter_context(nc.sbuf_tensor(f"f_rr{i}", [128, D], F32))
              for i in range(2)]
        x3t = [wes.enter_context(nc.sbuf_tensor(f"f_x3{i}", [128, D], F32))
               for i in range(2)]
        xsq = wes.enter_context(nc.sbuf_tensor("f_xsq", [128, D], F32))
        xnb = wes.enter_context(nc.sbuf_tensor("f_xnb", [128, D], BF16))
        cols = wes.enter_context(nc.sbuf_tensor("f_cols", [128, 8], F32))
        xfm = wes.enter_context(nc.sbuf_tensor("f_xfm", [128, NE * 512], BF16))
        ps_sc = [wes.enter_context(nc.psum_tensor(f"f_psc{i}", [128, 512], F32))
                 for i in range(2)]
        ps_av = [wes.enter_context(nc.psum_tensor(f"f_av{i}", [128, D], F32))
                 for i in range(2)]
        pst = [wes.enter_context(nc.psum_tensor(f"f_pst{i}", [128, 1024], BF16))
               for i in range(2)]
        S = Sems(nc, "f")
        qL = DmaQ(S, "dL", 4)
        qR = DmaQ(S, "dR", 2)
        qO = DmaQ(S, "dO", 2)
        qN = DmaQ(S, "dN", 2)

        def chunks(jt):
            j0 = jt * 128
            out = []
            for ac in range(j0 // 512, 4):
                lo = max(j0, ac * 512)
                out.append((lo, (ac + 1) * 512))
            return out

        NCH = [len(chunks(jt)) for jt in range(16)]
        CUM = [0] * 17
        for jt in range(16):
            CUM[jt + 1] = CUM[jt] + NCH[jt]
        NSC = CUM[16]

        with nc.Block() as blk:
            @blk.sync
            def _(sync):
                for s in range(NSEQ):
                    if s >= 1:
                        sync.wait_ge(S("peA"), NAV * s)
                    qL.inc(sync.dma_start(
                        k2[:, :].rearrange("p (e t) -> p e t", t=T),
                        Dr["kf1"][s].rearrange("(e p) t -> p e t", p=128)))
                    qL.inc(sync.dma_start(
                        q2[:, :].rearrange("p (e t) -> p e t", t=T),
                        Dr["qf1"][s].rearrange("(e p) t -> p e t", p=128)))
                    qL.inc(sync.dma_start(
                        v2[:, :].rearrange("p (jt f) -> p jt f", f=D),
                        Dr["vt1"][s].rearrange("(jt p) f -> p jt f", p=128)))
                    for it in range(16):
                        wt = s * 16 + it
                        if wt >= 2:
                            sync.wait_ge(S("vE"), 2 * (wt - 1))
                        t0 = wt * 128
                        qR.inc(sync.dma_start(rr[wt % 2][:, :],
                                              Dr["r2"][t0:t0 + 128, :]))
                        sync.wait_ge(S("vE"), 2 * wt + 2)
                        qO.inc(sync.dma_start(Dr["r3"][t0:t0 + 128, :],
                                              x3t[wt % 2][:, :]))
                        if it % 4 == 3:
                            g = wt // 4
                            sync.wait_ge(S("sC"), 32 * (g + 1))
                            qN.inc(sync.dma_start(
                                Dr["xn3g"][g].rearrange("e p t -> p e t"),
                                xfm[:, :].rearrange("p (e t) -> p e t", t=512)))
                qO.wait(sync, NT - 1)
                qO.wait(sync, NT - 2)
                qN.wait(sync, NG - 1)
                qN.wait(sync, NG - 2)

            @blk.tensor
            def _(t):
                for s in range(NSEQ):
                    qL.wait(t, 3 * s)
                    qL.wait(t, 3 * s + 1)
                    for jt in range(16):
                        j0 = jt * 128
                        for ci, (lo, hi) in enumerate(chunks(jt)):
                            idx = NSC * s + CUM[jt] + ci
                            if idx >= 2:
                                t.wait_ge(S("sE"), idx - 1)
                            for ei in range(NE):
                                ins = t.matmul(
                                    ps_sc[idx % 2][:, 0:hi - lo],
                                    k2[:, ei * T + j0: ei * T + j0 + 128],
                                    q2[:, ei * T + lo: ei * T + hi],
                                    start=(ei == 0), stop=(ei == 7))
                            ins.then_inc(S("peS"), 1)
                    qL.wait(t, 3 * s + 2)
                    for it in range(16):
                        wt = s * 16 + it
                        t.wait_ge(S("vN"), 16 * s + it + 1)
                        if wt >= 2:
                            t.wait_ge(S("vE"), 2 * (wt - 1))
                        for half in range(2):
                            for jt in range(it + 1):
                                ins = t.matmul(
                                    ps_av[it % 2][:, half * 512:(half + 1) * 512],
                                    Aall[:, jt * T + it * 128: jt * T + (it + 1) * 128],
                                    v2[:, jt * D + half * 512: jt * D + (half + 1) * 512],
                                    start=(jt == 0), stop=(jt == it))
                            ins.then_inc(S("peA"), 1)
                        t.wait_ge(S("vD"), wt + 1)
                        for ei in range(NE):
                            kk = wt * 8 + ei
                            if kk >= 2:
                                t.wait_ge(S("sC"), kk - 1)
                            t.transpose(pst[kk % 2][:, 0:128],
                                        xnb[:, ei * 128:(ei + 1) * 128],
                                        Pc["ident_bf"][:, :]).then_inc(S("pA"), 1)

            @blk.scalar
            def _(s_):
                for s in range(NSEQ):
                    for jt in range(16):
                        j0 = jt * 128
                        for ci, (lo, hi) in enumerate(chunks(jt)):
                            idx = NSC * s + CUM[jt] + ci
                            s_.wait_ge(S("peS"), idx + 1)
                            if ci == 0:
                                s_.wait_ge(S("vMk"), 16 * s + jt + 1)
                            s_.activation(Aall[:, jt * T + lo: jt * T + hi],
                                          ps_sc[idx % 2][:, 0:hi - lo],
                                          AF.Exp, scale=1.0 / 32.0
                                          ).then_inc(S("sE"), 1)
                    for it in range(16):
                        wt = s * 16 + it
                        s_.wait_ge(S("vE"), 2 * wt + 2)
                        if wt >= 1:
                            s_.wait_ge(S("lnvA"), 2 * wt)
                        s_.activation(xsq[:, :], x3t[wt % 2][:, :],
                                      AF.Square).then_inc(S("lnsA"), 1)
                        s_.wait_ge(S("lnvB"), wt + 1)
                        s_.activation(cols[:, 6:7], cols[:, 5:6], AF.Sqrt, scale=-1.0,
                                      bias=Pc["c1e5"]).then_inc(S("lnsB"), 1)
                        for ei in range(NE):
                            kk = wt * 8 + ei
                            s_.wait_ge(S("pA"), kk + 1)
                            if wt % 4 == 0 and ei == 0 and wt >= 4:
                                qN.wait(s_, wt // 4 - 1)
                            k = wt % 4
                            s_.copy(xfm[:, ei * 512 + k * 128: ei * 512 + (k + 1) * 128],
                                    pst[kk % 2][:, 0:128]).then_inc(S("sC"), 1)

            @blk.vector
            def _(v):
                for s in range(NSEQ):
                    for jt in range(16):
                        j0 = jt * 128
                        idx0 = NSC * s + CUM[jt]
                        v.wait_ge(S("peS"), idx0 + 1)
                        v.tensor_tensor(ps_sc[idx0 % 2][:, 0:128],
                                        ps_sc[idx0 % 2][:, 0:128],
                                        Pc["maskband"][:, 0:128],
                                        ALU.add).then_inc(S("vMk"), 1)
                        v.wait_ge(S("sE"), NSC * s + CUM[jt + 1])
                        v.reduce_sum(lcol[:, 0:1], Aall[:, jt * T + j0: (jt + 1) * T],
                                     axis=AX).then_inc(S("vL"), 1)
                        v.wait_ge(S("vL"), 16 * s + jt + 1)
                        v.reciprocal(lcol[:, 1:2], lcol[:, 0:1]).then_inc(S("vLr"), 1)
                        v.wait_ge(S("vLr"), 16 * s + jt + 1)
                        v.tensor_scalar(Aall[:, jt * T + j0: (jt + 1) * T],
                                        Aall[:, jt * T + j0: (jt + 1) * T],
                                        lcol[:, 1:2], None,
                                        ALU.mult).then_inc(S("vN"), 1)
                    for it in range(16):
                        wt = s * 16 + it
                        if wt >= 2:
                            qO.wait(v, wt - 2)
                            v.wait_ge(S("lnsA"), wt - 1)
                        for half in range(2):
                            n = 2 * wt + half
                            v.wait_ge(S("peA"), 2 * wt + half + 1)
                            qR.wait(v, wt)
                            v.tensor_tensor(
                                x3t[wt % 2][:, half * 512:(half + 1) * 512],
                                ps_av[it % 2][:, half * 512:(half + 1) * 512],
                                rr[wt % 2][:, half * 512:(half + 1) * 512],
                                ALU.add).then_inc(S("vE"), 1)
                        v.wait_ge(S("vE"), 2 * wt + 2)
                        v.reduce_sum(cols[:, 0:1], x3t[wt % 2][:, :],
                                     axis=AX).then_inc(S("lnvA"), 1)
                        v.wait_ge(S("lnsA"), wt + 1)
                        v.reduce_sum(cols[:, 1:2], xsq[:, :],
                                     axis=AX).then_inc(S("lnvA"), 1)
                        v.wait_ge(S("lnvA"), 2 * wt + 2)
                        v.tensor_scalar(cols[:, 2:3], cols[:, 0:1], 1.0 / D, None, ALU.mult)
                        v.tensor_scalar(cols[:, 3:4], cols[:, 1:2], 1.0 / D, None,
                                        ALU.mult).then_inc(S("lnvM"), 1)
                        v.wait_ge(S("lnvM"), wt + 1)
                        v.scalar_tensor_tensor(cols[:, 5:6], cols[:, 2:3], cols[:, 2:3],
                                               cols[:, 3:4], ALU.mult,
                                               ALU.subtract).then_inc(S("lnvB"), 1)
                        v.wait_ge(S("lnsB"), wt + 1)
                        v.reciprocal(cols[:, 7:8], cols[:, 6:7]).then_inc(S("lnvC"), 1)
                        v.wait_ge(S("lnvC"), wt + 1)
                        if wt >= 1:
                            v.wait_ge(S("pA"), 8 * wt)
                        v.tensor_scalar(xnb[:, :], x3t[wt % 2][:, :], cols[:, 2:3],
                                        cols[:, 7:8], ALU.subtract,
                                        ALU.mult).then_inc(S("vD"), 1)

            @blk.gpsimd
            def _(g_):
                pass

        S.release()




def emit_phase_g(nc, es, Dr, Pc):
    with ExitStack() as wes:
        gate_sb = wes.enter_context(nc.sbuf_tensor("g_gate", [128, NE * 8], BF16))
        xng = [wes.enter_context(nc.sbuf_tensor(f"g_xng{i}", [128, NE * 512], BF16))
               for i in range(2)]
        gwork = wes.enter_context(nc.sbuf_tensor("g_gw", [128, 48], F32))
        gcols = wes.enter_context(nc.sbuf_tensor("g_gc", [128, 16], F32))
        comb_all = wes.enter_context(nc.sbuf_tensor("g_comb", [128, NT * 8], F32))
        w1sb = [wes.enter_context(nc.sbuf_tensor(f"g_w1{i}", [128, NE * 2048], BF16))
                for i in range(2)]
        w2sb = [wes.enter_context(nc.sbuf_tensor(f"g_w2{i}", [128, 16 * D], BF16))
                for i in range(2)]
        hsb = wes.enter_context(nc.sbuf_tensor("g_h", [128, 16 * 512], BF16))
        ot = [wes.enter_context(nc.sbuf_tensor(f"g_ot{i}", [128, 512], F32))
              for i in range(2)]
        pg = wes.enter_context(nc.psum_tensor("g_pg", [128, 512], F32))
        psh = [wes.enter_context(nc.psum_tensor(f"g_psh{i}", [128, 512], F32))
               for i in range(2)]
        pso = [wes.enter_context(nc.psum_tensor(f"g_pso{i}", [128, 512], F32))
               for i in range(2)]
        S = Sems(nc, "g")
        qXg = DmaQ(S, "dXg", 2)
        qXe = DmaQ(S, "dXe", 2)
        qW1 = DmaQ(S, "dW1", 2)
        qW2 = DmaQ(S, "dW2", 2)
        qI = DmaQ(S, "dI", 2)
        qY = DmaQ(S, "dY", 2)
        qG = DmaQ(S, "dG", 2)

        with nc.Block() as blk:
            @blk.sync
            def _(sync):
                for g in range(NG):
                    if g >= 2:
                        sync.wait_ge(S("peG"), 4 * (g - 1))
                    qXg.inc(sync.dma_start(
                        xng[g % 2][:, :].rearrange("p (e t) -> p e t", t=512),
                        Dr["xn3g"][g].rearrange("e p t -> p e t")))
                for e in range(8):
                    for g in range(NG):
                        eg = e * NG + g
                        if eg < 2:
                            sync.wait_ge(S("peG"), NT)
                        else:
                            sync.wait_ge(S("peH"), 16 * (eg - 1))
                        qXe.inc(sync.dma_start(
                            xng[eg % 2][:, :].rearrange("p (e t) -> p e t", t=512),
                            Dr["xn3g"][g].rearrange("e p t -> p e t")))

            @blk.gpsimd
            def _(g_):
                qG.inc(g_.dma_start(
                    gate_sb[:, :].rearrange("p (e u) -> p e u", u=8),
                    Dr["gate_w"].rearrange("(e p) u -> p e u", p=128)))
                for i in range(8):
                    qI.inc(g_.dma_start(Dr["y"][i * 1024:(i + 1) * 1024, :],
                                        Dr["r3"][i * 1024:(i + 1) * 1024, :]))
                for e in range(8):
                    if e >= 2:
                        g_.wait_ge(S("peH"), 256 * (e - 1))
                    for dc in range(NE):
                        qW1.inc(g_.dma_start(
                            w1sb[e % 2][:, dc * 2048:(dc + 1) * 2048],
                            Dr["w1"][e, dc * 128:(dc + 1) * 128, :]))
                    if e >= 2:
                        g_.wait_ge(S("peW2"), 128 * (e - 1))
                    for fc in range(16):
                        qW2.inc(g_.dma_start(
                            w2sb[e % 2][:, fc * D:(fc + 1) * D],
                            Dr["w2"][e, fc * 128:(fc + 1) * 128, :]))
                    if e == 0:
                        qI.wait(g_, 7)
                        qI.wait(g_, 6)
                    for g in range(NG):
                        eg = e * NG + g
                        for tt in range(4):
                            for half in range(2):
                                n = eg * 8 + tt * 2 + half
                                g_.wait_ge(S("vO"), n + 1)
                                t0 = g * 512 + tt * 128
                                qY.inc(g_.dma_start(
                                    Dr["y"][t0:t0 + 128, half * 512:(half + 1) * 512],
                                    ot[n % 2][:, :], accum_op=ALU.add))
                qY.wait(g_, 8 * 128 - 1)
                qY.wait(g_, 8 * 128 - 2)

            @blk.tensor
            def _(t):
                qG.wait(t, 0)
                for g in range(NG):
                    qXg.wait(t, g)
                    for k in range(4):
                        tile = g * 4 + k
                        if tile >= 1:
                            t.wait_ge(S("sGc"), tile)
                        for dc in range(NE):
                            ins = t.matmul(
                                pg[:, 0:8],
                                xng[g % 2][:, dc * 512 + k * 128: dc * 512 + (k + 1) * 128],
                                gate_sb[:, dc * 8:(dc + 1) * 8],
                                start=(dc == 0), stop=(dc == 7))
                        ins.then_inc(S("peG"), 1)
                for e in range(8):
                    qW1.wait(t, e * 8 + 6)
                    qW1.wait(t, e * 8 + 7)
                    qW2.wait(t, e * 16 + 14)
                    qW2.wait(t, e * 16 + 15)
                    for g in range(NG):
                        eg = e * NG + g
                        qXe.wait(t, eg)
                        for fc in range(16):
                            m = eg * 16 + fc
                            if m >= 2:
                                t.wait_ge(S("sH"), m - 1)
                            for dc in range(NE):
                                ins = t.matmul(
                                    psh[m % 2][:, :],
                                    w1sb[e % 2][:, dc * 2048 + fc * 128: dc * 2048 + fc * 128 + 128],
                                    xng[eg % 2][:, dc * 512:(dc + 1) * 512],
                                    start=(dc == 0), stop=(dc == 7))
                            ins.then_inc(S("peH"), 1)
                        t.wait_ge(S("sH"), 16 * (eg + 1))
                        for tt in range(4):
                            for half in range(2):
                                n = eg * 8 + tt * 2 + half
                                if n >= 2:
                                    t.wait_ge(S("vO"), n - 1)
                                for fc in range(16):
                                    ins = t.matmul(
                                        pso[n % 2][:, :],
                                        hsb[:, fc * 512 + tt * 128: fc * 512 + (tt + 1) * 128],
                                        w2sb[e % 2][:, fc * D + half * 512: fc * D + (half + 1) * 512],
                                        start=(fc == 0), stop=(fc == 15))
                                ins.then_inc(S("peW2"), 1)

            @blk.scalar
            def _(s_):
                for tile in range(NT):
                    s_.wait_ge(S("peG"), tile + 1)
                    if tile >= 1:
                        s_.wait_ge(S("vExp"), tile)
                        s_.wait_ge(S("sExp"), tile)
                    s_.copy(gwork[:, 0:8], pg[:, 0:8]).then_inc(S("sGc"), 1)
                    s_.wait_ge(S("vExp"), tile + 1)
                    s_.activation(gwork[:, 32:40], gwork[:, 0:8], AF.Exp,
                                  bias=gcols[:, 7:8]).then_inc(S("sExp"), 1)
                for e in range(8):
                    for g in range(NG):
                        eg = e * NG + g
                        for fc in range(16):
                            m = eg * 16 + fc
                            s_.wait_ge(S("peH"), m + 1)
                            if fc == 0 and eg >= 1:
                                s_.wait_ge(S("peW2"), 8 * eg)
                            s_.activation(hsb[:, fc * 512:(fc + 1) * 512],
                                          psh[m % 2][:, :],
                                          AF.Silu).then_inc(S("sH"), 1)

            @blk.vector
            def _(v):
                for tile in range(NT):
                    v.wait_ge(S("sGc"), tile + 1)
                    v.reduce_max(gcols[:, 0:1], gwork[:, 0:8],
                                 axis=AX).then_inc(S("v1"), 1)
                    v.wait_ge(S("v1"), tile + 1)
                    v.tensor_scalar(gwork[:, 8:16], gwork[:, 0:8], gcols[:, 0:1],
                                    None, ALU.is_ge).then_inc(S("v2"), 1)
                    v.wait_ge(S("v2"), tile + 1)
                    v.reduce_sum(gcols[:, 1:2], gwork[:, 8:16],
                                 axis=AX).then_inc(S("v3"), 1)
                    v.scalar_tensor_tensor(gwork[:, 16:24], gwork[:, 8:16], -1e30,
                                           gwork[:, 0:8], ALU.mult,
                                           ALU.add).then_inc(S("v4"), 1)
                    v.wait_ge(S("v4"), tile + 1)
                    v.reduce_max(gcols[:, 2:3], gwork[:, 16:24],
                                 axis=AX).then_inc(S("v5"), 1)
                    v.wait_ge(S("v3"), tile + 1)
                    v.tensor_scalar(gcols[:, 3:4], gcols[:, 1:2], -1.0, 1.0,
                                    ALU.add, ALU.min).then_inc(S("v6"), 1)
                    v.wait_ge(S("v5"), tile + 1)
                    v.wait_ge(S("v6"), tile + 1)
                    v.tensor_tensor(gcols[:, 4:5], gcols[:, 0:1], gcols[:, 2:3],
                                    ALU.subtract).then_inc(S("v7"), 1)
                    v.wait_ge(S("v7"), tile + 1)
                    v.tensor_tensor(gcols[:, 5:6], gcols[:, 4:5], gcols[:, 3:4],
                                    ALU.mult).then_inc(S("v8"), 1)
                    v.wait_ge(S("v8"), tile + 1)
                    v.tensor_tensor(gcols[:, 6:7], gcols[:, 2:3], gcols[:, 5:6],
                                    ALU.add).then_inc(S("v9"), 1)
                    v.wait_ge(S("v9"), tile + 1)
                    v.tensor_scalar(gwork[:, 24:32], gwork[:, 0:8], gcols[:, 6:7],
                                    None, ALU.is_ge)
                    v.tensor_scalar(gcols[:, 7:8], gcols[:, 0:1], -1.0, None,
                                    ALU.mult).then_inc(S("vExp"), 1)
                    v.wait_ge(S("sExp"), tile + 1)
                    v.tensor_tensor(gwork[:, 40:48], gwork[:, 32:40], gwork[:, 24:32],
                                    ALU.mult).then_inc(S("vA"), 1)
                    v.wait_ge(S("vA"), tile + 1)
                    v.reduce_sum(gcols[:, 8:9], gwork[:, 40:48],
                                 axis=AX).then_inc(S("vB"), 1)
                    v.wait_ge(S("vB"), tile + 1)
                    v.reciprocal(gcols[:, 9:10], gcols[:, 8:9]).then_inc(S("vC"), 1)
                    v.wait_ge(S("vC"), tile + 1)
                    v.tensor_scalar(comb_all[:, tile * 8:(tile + 1) * 8],
                                    gwork[:, 40:48], gcols[:, 9:10],
                                    None, ALU.mult).then_inc(S("vCb"), 1)
                v.wait_ge(S("vCb"), NT)
                for e in range(8):
                    for g in range(NG):
                        eg = e * NG + g
                        for tt in range(4):
                            tile = g * 4 + tt
                            for half in range(2):
                                n = eg * 8 + tt * 2 + half
                                v.wait_ge(S("peW2"), n + 1)
                                if n >= 2:
                                    qY.wait(v, n - 2)
                                v.tensor_scalar(ot[n % 2][:, :],
                                                pso[n % 2][:, :],
                                                comb_all[:, tile * 8 + e: tile * 8 + e + 1],
                                                None, ALU.mult).then_inc(S("vO"), 1)
        S.release()


PHASE_IO = {
    "a": (["x", "wq0", "wk0", "wv0", "bq0", "bk0", "bv0"],
          ["qf0", "kf0", "vt0"]),
    "b": (["x", "qf0", "kf0", "vt0", "wo0", "bo0", "rel"],
          ["r1", "tcorr_d", "dbg"]),
    "c": (["r1", "wffn", "lng", "lnb"],
          ["xn1g", "wqn_d", "beta_d", "gmax_d", "cc_in", "cc_out"]),
    "d": (["r1", "xn1g", "wqn_d", "beta_d", "gmax_d"],
          ["r2", "xn2g"]),
    "e": (["xn2g", "wq1", "wk1", "wv1", "bq1", "bk1", "bv1"],
          ["qf1", "kf1", "vt1"]),
    "f": (["qf1", "kf1", "vt1", "r2"],
          ["r3", "xn3g"]),
    "g": (["xn3g", "r3", "gate_w", "w1", "w2"],
          ["y"]),
}

DRAM_SPECS = {
    "x": ([TOK, D], F32),
    "wq0": ([D, D], F32), "wk0": ([D, D], F32), "wv0": ([D, D], F32),
    "wo0": ([D, D], F32),
    "bq0": ([NE, 128], F32), "bk0": ([NE, 128], F32), "bv0": ([NE, 128], F32),
    "bo0": ([1, D], F32),
    "rel": ([32, 16], F32),
    "qf0": ([NSEQ, D, T], BF16), "kf0": ([NSEQ, D, T], BF16),
    "vt0": ([NSEQ, T, D], BF16),
    "r1": ([TOK, D], F32),
    "r2": ([TOK, D], F32),
    "tcorr_d": ([16, 128 * 256], F32),
    "wffn": ([D, D], F32),
    "lng": ([1, D], F32), "lnb": ([1, D], F32),
    "xn1g": ([NG, NE, 128, 512], BF16),
    "xn2g": ([NG, NE, 128, 512], BF16),
    "wqn_d": ([D, D], BF16),
    "wq1": ([D, D], F32), "wk1": ([D, D], F32), "wv1": ([D, D], F32),
    "bq1": ([NE, 128], F32), "bk1": ([NE, 128], F32), "bv1": ([NE, 128], F32),
    "qf1": ([NSEQ, D, T], BF16), "kf1": ([NSEQ, D, T], BF16),
    "vt1": ([NSEQ, T, D], BF16),
    "r3": ([TOK, D], F32),
    "xn3g": ([NG, NE, 128, 512], BF16),
    "gate_w": ([D, 8], F32),
    "w1": ([8, D, 2048], F32),
    "w2": ([8, 2048, D], F32),
    "y": ([TOK, D], F32),
    "beta_d": ([1, 1], F32),
    "gmax_d": ([1, 1], F32),
    "cc_in": ([1, 1], F32),
    "cc_out": ([1, 1], F32),
    "dbg": ([128, 2064], F32),
}

CONST_SPECS = {
    "ident_bf": ([128, 128], BF16),
    "ident_f32": ([128, 128], F32),
    "ones_f32": ([1, 128], F32),
    "selmat": ([32, 128 * 256], BF16, "dram_only"),
    "maskband": ([128, 256], F32),
}

CONST_COLS = {"c1e5": 1e-5, "c1e3": 1e-3, "c44": 44.0, "cLE": 13.5015240554}

PHASE_EMIT = {"a": emit_phase_a2, "b": emit_phase_b, "c": emit_phase_c, "d": emit_phase_d, "e": emit_phase_e, "f": emit_phase_f, "g": emit_phase_g}


def build_program(phases, external):
    nc = bass.Bass("TRN2", target_bir_lowering=False)
    consumes = set()
    produces = set()
    for p in phases:
        c, pr = PHASE_IO[p]
        consumes.update(x for x in c if x not in produces)
        produces.update(pr)
    Dr = {}
    ext_inputs = []
    for name in sorted(consumes | produces):
        shape, dt = DRAM_SPECS[name]
        if name in produces and name in external:
            kind = "ExternalOutput"
        elif name in consumes and name not in produces:
            kind = "ExternalInput"
            ext_inputs.append(name)
        else:
            kind = "Internal"
        Dr[name] = nc.dram_tensor(name, shape, dt, kind=kind)
        if kind == "ExternalInput":
            nc._ext_inputs = getattr(nc, "_ext_inputs", []) + [name]
    Pc = {}
    with ExitStack() as es:
        for cn, spec in CONST_SPECS.items():
            shape, dt = spec[0], spec[1]
            dr = nc.dram_tensor("c_" + cn, shape, dt, kind="ExternalInput")
            Pc[cn + "_dram"] = dr
            if len(spec) > 2 and spec[2] == "dram_only":
                continue
            sb = es.enter_context(nc.sbuf_tensor("sb_" + cn, shape, dt))
            Pc[cn] = sb
        ccols = es.enter_context(nc.sbuf_tensor("sb_ccols", [128, len(CONST_COLS)], F32))
        for i, cn in enumerate(CONST_COLS):
            Pc[cn] = ccols[:, i:i + 1]
        S0 = Sems(nc, "init")
        with nc.Block() as blk:
            @blk.sync
            def _(sync):
                n = 0
                for cn, spec in CONST_SPECS.items():
                    if len(spec) > 2 and spec[2] == "dram_only":
                        continue
                    sync.dma_start(Pc[cn][:, :], Pc[cn + "_dram"][:, :]
                                   ).then_inc(S0("dma"), 16)
                    n += 16
                sync.wait_ge(S0("dma"), n)
            @blk.gpsimd
            def _(g):
                for i, (cn, val) in enumerate(CONST_COLS.items()):
                    g.memset(ccols[:, i:i + 1], val)

        for p in phases:
            PHASE_EMIT[p](nc, es, Dr, Pc)
    return nc


FUSED = True
LAUNCH_GROUPS = ([["a", "b", "c", "d", "e", "f", "g"]] if FUSED else
                 [["a"], ["b"], ["c"], ["d"], ["e"], ["f"], ["g"]])

_INTERNAL_ONLY = {"tcorr_d", "cc_in", "cc_out", "dbg"}


def _base_maps(inputs):
    consts = host_consts()
    x = np.ascontiguousarray(np.asarray(inputs["L0_attn_x"], np.float32))
    maps = []
    for c in range(NCORES):
        m = {
            "x": np.ascontiguousarray(x[4 * c:4 * c + 4].reshape(TOK, D)),
            "wq0": np.asarray(inputs["L0_attn_Wq"], np.float32),
            "wk0": np.asarray(inputs["L0_attn_Wk"], np.float32),
            "wv0": np.asarray(inputs["L0_attn_Wv"], np.float32),
            "wo0": np.asarray(inputs["L0_attn_Wo"], np.float32),
            "bq0": np.asarray(inputs["L0_attn_bq"], np.float32).reshape(NE, 128),
            "bk0": np.asarray(inputs["L0_attn_bk"], np.float32).reshape(NE, 128),
            "bv0": np.asarray(inputs["L0_attn_bv"], np.float32).reshape(NE, 128),
            "bo0": np.asarray(inputs["L0_attn_bo"], np.float32).reshape(1, D),
            "rel": np.asarray(inputs["L0_attn_rel_emb"], np.float32),
            "wffn": np.asarray(inputs["L0_ffn_weight"], np.float32),
            "lng": np.asarray(inputs["L0_ffn_ln_gamma"], np.float32).reshape(1, D),
            "lnb": np.asarray(inputs["L0_ffn_ln_beta"], np.float32).reshape(1, D),
            "wq1": np.asarray(inputs["L1_attn_Wq"], np.float32),
            "wk1": np.asarray(inputs["L1_attn_Wk"], np.float32),
            "wv1": np.asarray(inputs["L1_attn_Wv"], np.float32),
            "bq1": np.asarray(inputs["L1_attn_bq"], np.float32).reshape(NE, 128),
            "bk1": np.asarray(inputs["L1_attn_bk"], np.float32).reshape(NE, 128),
            "bv1": np.asarray(inputs["L1_attn_bv"], np.float32).reshape(NE, 128),
            "gate_w": np.asarray(inputs["L1_ffn_gate_w"], np.float32),
            "w1": np.asarray(inputs["L1_ffn_w1"], np.float32),
            "w2": np.asarray(inputs["L1_ffn_w2"], np.float32),
        }
        for cn in CONST_SPECS:
            m["c_" + cn] = consts[cn]
        maps.append(m)
    return maps


def kernel(**inputs):
    state = _base_maps(inputs)
    n_groups = len(LAUNCH_GROUPS)
    later_consumes = []
    acc = {"y"}
    for gi in range(n_groups - 1, -1, -1):
        later_consumes.insert(0, set(acc))
        for p in LAUNCH_GROUPS[gi]:
            acc |= set(PHASE_IO[p][0])
    for gi, group in enumerate(LAUNCH_GROUPS):
        produced = set()
        for p in group:
            produced |= set(PHASE_IO[p][1])
        ext = (produced & later_consumes[gi]) - _INTERNAL_ONLY
        if gi == n_groups - 1:
            ext |= {"y"}
        nc = build_program(group, external=ext)
        ext_in = set(getattr(nc, "_ext_inputs", [])) | {"c_" + c for c in CONST_SPECS}
        in_maps = [{k: v for k, v in state[c].items() if k in ext_in}
                   for c in range(NCORES)]
        res = run_bass_kernel_spmd(nc, in_maps, core_ids=list(range(NCORES)))
        for c in range(NCORES):
            for k, v in res.results[c].items():
                state[c][k] = np.asarray(v)
    out = np.concatenate(
        [state[c]["y"].reshape(NSEQ, T, D) for c in range(NCORES)], axis=0)
    return out


# version 05 (final)
# speedup vs baseline: 1.0490x; 1.0013x over previous
import math
import numpy as np
import ml_dtypes
from contextlib import ExitStack

import concourse.bass as bass
import concourse.mybir as mybir
from concourse.bass_utils import run_bass_kernel_spmd

F32 = mybir.dt.float32
BF16 = mybir.dt.bfloat16
AF = mybir.ActivationFunctionType
ALU = mybir.AluOpType
AX = mybir.AxisListType.X
BF = ml_dtypes.bfloat16

NCORES = 8
NSEQ = 4
T = 2048
D = 1024
TOK = NSEQ * T
NE = D // 128
NG = TOK // 512
NT = TOK // 128
HEADS = 16
HD = 64
NUM_BUCKETS = 32
MAX_DISTANCE = 128
SCALE = (D // HEADS) ** (-0.5)
NEG = -1e30
B_DBG = False
B_NOBIAS = False


def _bucket(d):
    max_exact = NUM_BUCKETS // 2
    if d < max_exact:
        return d
    v = max_exact + int(
        math.log(d / max_exact) / math.log(MAX_DISTANCE / max_exact)
        * (NUM_BUCKETS - max_exact))
    return min(v, NUM_BUCKETS - 1)


def host_consts():
    c = {}
    c["ident_bf"] = np.eye(128, dtype=BF)
    c["ident_f32"] = np.eye(128, dtype=np.float32)
    c["ones_f32"] = np.ones((1, 128), dtype=np.float32)
    A = np.zeros((NUM_BUCKETS, 128 * 256), dtype=np.float32)
    for p in range(128):
        for ip in range(256):
            d = ip - p
            if d < 0:
                continue
            pos = p * 256 + ip
            A[_bucket(d), pos] += 1.0
            A[NUM_BUCKETS - 1, pos] -= 1.0
    c["selmat"] = A.astype(BF)
    mb = np.zeros((128, 256), dtype=np.float32)
    for p in range(128):
        mb[p, :p] = NEG
    c["maskband"] = mb
    return c


class Sems:
    def __init__(self, nc, prefix):
        self.nc = nc
        self.prefix = prefix
        self.h = {}

    def __call__(self, name):
        if name not in self.h:
            self.h[name] = self.nc.alloc_semaphore(self.prefix + "_" + name)
        return self.h[name]

    def release(self):
        if self.h:
            self.nc.clear_and_free_semaphores(list(self.h.values()))
            self.nc.all_engine_barrier()
            self.h = {}

class DmaQ:

    def __init__(self, S, name, nsem=2):
        self.sems = [S(f"{name}{i}") for i in range(nsem)]
        self.P = nsem
        self.n = 0

    def inc(self, instr):
        instr.then_inc(self.sems[self.n % self.P], 16)
        self.n += 1

    def wait(self, eng, k):
        eng.wait_ge(self.sems[k % self.P], 16 * (k // self.P + 1))



def emit_load_wT(nc, es, w_drams, tag, ident_bf):
    outs = [es.enter_context(nc.sbuf_tensor(f"{tag}_wt{i}", [128, NE * D], BF16))
            for i in range(len(w_drams))]
    les = ExitStack()
    stage = [les.enter_context(nc.sbuf_tensor(f"{tag}_wstg{i}", [128, D], BF16))
             for i in range(2)]
    ps = [les.enter_context(nc.psum_tensor(f"{tag}_wps{i}", [128, 1024], BF16))
          for i in range(2)]
    S = Sems(nc, tag + "_lw")
    NW = len(w_drams)
    qW = DmaQ(S, "dma", 2)
    with les, nc.Block() as blk:
        @blk.gpsimd
        def _(g):
            n = 0
            for wi in range(NW):
                for fi in range(NE):
                    n += 1
                    if n > 2:
                        g.wait_ge(S("pe"), 8 * (n - 2))
                    qW.inc(g.dma_start(stage[n % 2][:, :],
                                       w_drams[wi][fi * 128:(fi + 1) * 128, :]))

        @blk.tensor
        def _(t):
            n = 0
            for wi in range(NW):
                for fi in range(NE):
                    n += 1
                    qW.wait(t, n - 1)
                    for ei in range(NE):
                        k = (n - 1) * 8 + ei
                        if k >= 2:
                            t.wait_ge(S("se"), k - 1)
                        t.transpose(ps[k % 2][:, 0:128],
                                    stage[n % 2][:, ei * 128:(ei + 1) * 128],
                                    ident_bf[:, :]).then_inc(S("pe"), 1)

        @blk.scalar
        def _(s):
            n = 0
            for wi in range(NW):
                for fi in range(NE):
                    n += 1
                    for ei in range(NE):
                        k = (n - 1) * 8 + ei
                        s.wait_ge(S("pe"), k + 1)
                        s.copy(outs[wi][:, ei * D + fi * 128: ei * D + (fi + 1) * 128],
                               ps[k % 2][:, 0:128]).then_inc(S("se"), 1)
    S.release()
    return outs


def emit_load_bias_cols(nc, es, b_drams, tag, ident_f32):
    outs = [es.enter_context(nc.sbuf_tensor(f"{tag}_bc{i}", [128, NE], F32))
            for i in range(len(b_drams))]
    les = ExitStack()
    stage = [les.enter_context(nc.sbuf_tensor(f"{tag}_bstg{i}", [NE, 128], F32))
             for i in range(len(b_drams))]
    ps = les.enter_context(nc.psum_tensor(f"{tag}_bps", [128, 512], F32))
    S = Sems(nc, tag + "_lb")
    with les, nc.Block() as blk:
        @blk.sync
        def _(sync):
            for i, b in enumerate(b_drams):
                sync.dma_start(stage[i][:, :], b[:, :]).then_inc(S("dma"), 16)

        @blk.tensor
        def _(t):
            t.wait_ge(S("dma"), 16 * len(b_drams))
            for i in range(len(b_drams)):
                if i >= 1:
                    t.wait_ge(S("se"), i)
                t.transpose(ps[:, 0:NE], stage[i][:, :],
                            ident_f32[0:NE, 0:NE]).then_inc(S("pe"), 1)

        @blk.scalar
        def _(s):
            for i in range(len(b_drams)):
                s.wait_ge(S("pe"), i + 1)
                s.copy(outs[i][:, :], ps[:, 0:NE]).then_inc(S("se"), 1)
    S.release()
    return outs


def emit_phase_a2(nc, es, Dr, Pc):
    with ExitStack() as wes:
        wts = emit_load_wT(nc, wes, [Dr["wq0"], Dr["wk0"], Dr["wv0"]], "aw",
                           Pc["ident_bf"])
        bcols = emit_load_bias_cols(nc, wes, [Dr["bq0"], Dr["bk0"], Dr["bv0"]],
                                    "ab", Pc["ident_f32"])
        xin = [wes.enter_context(nc.sbuf_tensor(f"a_xin{i}", [128, D], F32))
               for i in range(2)]
        xsq = wes.enter_context(nc.sbuf_tensor("a_xsq", [128, D], F32))
        cols = wes.enter_context(nc.sbuf_tensor("a_cols", [128, 8], F32))
        xlnb = wes.enter_context(nc.sbuf_tensor("a_xlnb", [128, D], BF16))
        xfm = wes.enter_context(nc.sbuf_tensor("a_xfm", [128, NE * 512], BF16))
        qt = [wes.enter_context(nc.sbuf_tensor(f"a_qt{i}", [128, 512], BF16))
              for i in range(2)]
        vfm = wes.enter_context(nc.sbuf_tensor("a_vfm", [128, NE * 512], BF16))
        vtm = [wes.enter_context(nc.sbuf_tensor(f"a_vtm{i}", [128, D], BF16))
               for i in range(2)]
        pst = [wes.enter_context(nc.psum_tensor(f"a_pst{i}", [128, 1024], BF16))
               for i in range(2)]
        pstv = [wes.enter_context(nc.psum_tensor(f"a_pstv{i}", [128, 1024], BF16))
                for i in range(2)]
        psq = [wes.enter_context(nc.psum_tensor(f"a_psq{i}", [128, 512], F32))
               for i in range(2)]
        S = Sems(nc, "a")
        qA = DmaQ(S, "dA", 2)
        qQ = DmaQ(S, "dQ", 2)
        qV = DmaQ(S, "dV", 2)

        with nc.Block() as blk:
            @blk.sync
            def _(sync):
                for g in range(NG):
                    sq = g // 4
                    tg0 = (g % 4) * 512
                    for k in range(4):
                        tt = g * 4 + k
                        if tt >= 2:
                            sync.wait_ge(S("vD"), tt - 1)
                        t0 = tt * 128
                        qA.inc(sync.dma_start(xin[tt % 2][:, :],
                                              Dr["x"][t0:t0 + 128, :]))
                    for wi in range(2):
                        for fi in range(NE):
                            sync.wait_ge(S("sEv"), g * 24 + wi * 8 + fi + 1)
                            m = g * 16 + wi * 8 + fi + 1
                            dst = Dr["qf0"] if wi == 0 else Dr["kf0"]
                            qQ.inc(sync.dma_start(
                                dst[sq, fi * 128:(fi + 1) * 128, tg0:tg0 + 512],
                                qt[m % 2][:, :]))
                    for k2 in range(4):
                        sync.wait_ge(S("sV2"), g * 32 + k2 * 8 + 8)
                        n = g * 4 + k2 + 1
                        qV.inc(sync.dma_start(
                            Dr["vt0"][sq, tg0 + k2 * 128: tg0 + (k2 + 1) * 128, :],
                            vtm[n % 2][:, :]))
                qQ.wait(sync, NG * 16 - 1)
                qQ.wait(sync, NG * 16 - 2)
                qV.wait(sync, NG * 4 - 1)
                qV.wait(sync, NG * 4 - 2)

            @blk.scalar
            def _(s):
                for g in range(NG):
                    for k in range(4):
                        tt = g * 4 + k
                        qA.wait(s, tt)
                        if tt >= 1:
                            s.wait_ge(S("lnvA"), 2 * tt)
                        s.activation(xsq[:, :], xin[tt % 2][:, :],
                                     AF.Square).then_inc(S("lnsA"), 1)
                        s.wait_ge(S("lnvB"), tt + 1)
                        s.activation(cols[:, 6:7], cols[:, 5:6], AF.Sqrt, scale=-1.0,
                                     bias=Pc["c1e5"]).then_inc(S("lnsB"), 1)
                        for ei in range(NE):
                            kk = tt * 8 + ei
                            s.wait_ge(S("pA"), kk + 1)
                            if g >= 1 and k == 0 and ei == 0:
                                s.wait_ge(S("pB"), 24 * g)
                            s.copy(xfm[:, ei * 512 + k * 128: ei * 512 + (k + 1) * 128],
                                   pst[kk % 2][:, 0:128]).then_inc(S("sC"), 1)
                    for wi in range(3):
                        for fi in range(NE):
                            n = g * 24 + wi * 8 + fi + 1
                            s.wait_ge(S("pB"), n)
                            if wi < 2:
                                m = g * 16 + wi * 8 + fi + 1
                                if m > 2:
                                    qQ.wait(s, m - 3)
                                s.activation(qt[m % 2][:, :], psq[n % 2][:, :],
                                             AF.Identity, bias=bcols[wi][:, fi:fi + 1]
                                             ).then_inc(S("sEv"), 1)
                            else:
                                if g >= 1 and fi == 0:
                                    s.wait_ge(S("pV"), 32 * g)
                                s.activation(vfm[:, fi * 512:(fi + 1) * 512],
                                             psq[n % 2][:, :], AF.Identity,
                                             bias=bcols[2][:, fi:fi + 1]
                                             ).then_inc(S("sEv"), 1)
                    for k2 in range(4):
                        for fi in range(NE):
                            kk = g * 32 + k2 * 8 + fi
                            s.wait_ge(S("pV"), kk + 1)
                            n = g * 4 + k2 + 1
                            if n > 2 and fi == 0:
                                qV.wait(s, n - 3)
                            s.copy(vtm[n % 2][:, fi * 128:(fi + 1) * 128],
                                   pstv[kk % 2][:, 0:128]).then_inc(S("sV2"), 1)

            @blk.vector
            def _(v):
                for g in range(NG):
                    for k in range(4):
                        tt = g * 4 + k
                        qA.wait(v, tt)
                        v.reduce_sum(cols[:, 0:1], xin[tt % 2][:, :],
                                     axis=AX).then_inc(S("lnvA"), 1)
                        v.wait_ge(S("lnsA"), tt + 1)
                        v.reduce_sum(cols[:, 1:2], xsq[:, :],
                                     axis=AX).then_inc(S("lnvA"), 1)
                        v.wait_ge(S("lnvA"), 2 * (tt + 1))
                        v.tensor_scalar(cols[:, 2:3], cols[:, 0:1], 1.0 / D, None, ALU.mult)
                        v.tensor_scalar(cols[:, 3:4], cols[:, 1:2], 1.0 / D, None,
                                        ALU.mult).then_inc(S("lnvM"), 1)
                        v.wait_ge(S("lnvM"), tt + 1)
                        v.scalar_tensor_tensor(cols[:, 5:6], cols[:, 2:3], cols[:, 2:3],
                                               cols[:, 3:4], ALU.mult,
                                               ALU.subtract).then_inc(S("lnvB"), 1)
                        v.wait_ge(S("lnsB"), tt + 1)
                        v.reciprocal(cols[:, 7:8], cols[:, 6:7]).then_inc(S("lnvC"), 1)
                        v.wait_ge(S("lnvC"), tt + 1)
                        if tt >= 1:
                            v.wait_ge(S("pA"), 8 * tt)
                        v.tensor_scalar(xlnb[:, :], xin[tt % 2][:, :], cols[:, 2:3],
                                        cols[:, 7:8], ALU.subtract,
                                        ALU.mult).then_inc(S("vD"), 1)

            @blk.tensor
            def _(t):
                for g in range(NG):
                    for k in range(4):
                        tt = g * 4 + k
                        t.wait_ge(S("vD"), tt + 1)
                        for ei in range(NE):
                            kk = tt * 8 + ei
                            if kk >= 2:
                                t.wait_ge(S("sC"), kk - 1)
                            t.transpose(pst[kk % 2][:, 0:128],
                                        xlnb[:, ei * 128:(ei + 1) * 128],
                                        Pc["ident_bf"][:, :]).then_inc(S("pA"), 1)
                    t.wait_ge(S("sC"), 32 * (g + 1))
                    for wi in range(3):
                        for fi in range(NE):
                            n = g * 24 + wi * 8 + fi + 1
                            for ei in range(NE):
                                if ei == 0 and n > 2:
                                    t.wait_ge(S("sEv"), n - 2)
                                ins = t.matmul(
                                    psq[n % 2][:, :],
                                    wts[wi][:, ei * D + fi * 128: ei * D + fi * 128 + 128],
                                    xfm[:, ei * 512:(ei + 1) * 512],
                                    start=(ei == 0), stop=(ei == 7))
                                if ei == 7:
                                    ins.then_inc(S("pB"), 1)
                    t.wait_ge(S("sEv"), g * 24 + 24)
                    for k2 in range(4):
                        for fi in range(NE):
                            kk = g * 32 + k2 * 8 + fi
                            if kk >= 2:
                                t.wait_ge(S("sV2"), kk - 1)
                            t.transpose(pstv[kk % 2][:, 0:128],
                                        vfm[:, fi * 512 + k2 * 128: fi * 512 + (k2 + 1) * 128],
                                        Pc["ident_bf"][:, :]).then_inc(S("pV"), 1)
        S.release()



def _b_chunks(jt):
    j0 = jt * 128
    out = []
    for ac in range(j0 // 512, 4):
        lo = max(j0, ac * 512)
        hi = (ac + 1) * 512
        out.append((ac, lo, hi))
    return out


def _b_bias_segs(jt):
    j0 = jt * 128
    segs = []
    end = min(j0 + 256, 2048)
    c0 = j0 // 512
    s1hi = min(end, (c0 + 1) * 512)
    segs.append((c0, j0, s1hi))
    if end > s1hi and c0 + 1 <= 3:
        segs.append((c0 + 1, s1hi, end))
    return segs


_B_NSC = sum(len(_b_chunks(jt)) for jt in range(16))
_B_NBI = sum(len(_b_bias_segs(jt)) for jt in range(16))
_B_NMEM = sum(1 for jt in range(16) if (jt * 128) % 512)


def _b_sc_idx():
    m = {}
    i = 0
    for jt in range(16):
        for (ac, lo, hi) in _b_chunks(jt):
            m[(jt, ac)] = i
            i += 1
    return m


_B_IDX = _b_sc_idx()
_B_CUMAV = [0] * 17
for _jt in range(16):
    _B_CUMAV[_jt + 1] = _B_CUMAV[_jt] + len(_b_chunks(_jt))


def emit_phase_b(nc, es, Dr, Pc):
    with ExitStack() as wes:
        wo_t = emit_load_wT(nc, wes, [Dr["wo0"]], "bw", Pc["ident_bf"])[0]
        kh = [wes.enter_context(nc.sbuf_tensor(f"b_kh{i}", [64, T], BF16))
              for i in range(2)]
        qh = [wes.enter_context(nc.sbuf_tensor(f"b_qh{i}", [64, T], BF16))
              for i in range(2)]
        v_sb = wes.enter_context(nc.sbuf_tensor("b_vsb", [128, 16 * D], BF16))
        v65 = wes.enter_context(nc.sbuf_tensor("b_v65", [128, 16 * 65], BF16))
        A = [wes.enter_context(nc.sbuf_tensor(f"b_A{i}", [128, T], BF16))
             for i in range(2)]
        ao = wes.enter_context(nc.sbuf_tensor("b_ao", [128, 8 * T], BF16))
        ao_tmp = [wes.enter_context(nc.sbuf_tensor(f"b_aot{i}", [64, T], BF16))
                  for i in range(2)]
        tcorr = wes.enter_context(nc.sbuf_tensor("b_tcorr", [128, 16 * 256], F32))
        relf = wes.enter_context(nc.sbuf_tensor("b_relf", [32, 16], F32))
        relb = wes.enter_context(nc.sbuf_tensor("b_relb", [32, 16], BF16))
        sel = [wes.enter_context(nc.sbuf_tensor(f"b_sel{i}", [32, 4096], BF16))
               for i in range(2)]
        tcsb = [wes.enter_context(nc.sbuf_tensor(f"b_tcsb{i}", [16, 512], F32))
                for i in range(2)]
        avc = wes.enter_context(nc.sbuf_tensor("b_avc", [65, T], F32))
        lrow = wes.enter_context(nc.sbuf_tensor("b_lrow", [1, T], F32))
        linv = wes.enter_context(nc.sbuf_tensor("b_linv", [1, T], F32))
        bcb = [wes.enter_context(nc.sbuf_tensor(f"b_bcb{i}", [64, 512], F32))
               for i in range(2)]
        bo_row = wes.enter_context(nc.sbuf_tensor("b_bor", [1, D], F32))
        bo_bc = wes.enter_context(nc.sbuf_tensor("b_bobc", [128, D], F32))
        xr = [wes.enter_context(nc.sbuf_tensor(f"b_xr{i}", [128, D], F32))
              for i in range(2)]
        x1t = [wes.enter_context(nc.sbuf_tensor(f"b_x1t{i}", [128, D], F32))
               for i in range(2)]
        ps_sc = [wes.enter_context(nc.psum_tensor(f"b_psc{i}", [128, 512], F32))
                 for i in range(2)]
        av_ps = wes.enter_context(nc.psum_tensor("b_av", [65, T], F32))
        wo_ps = wes.enter_context(nc.psum_tensor("b_wo", [128, D], F32))

        dbg = (wes.enter_context(nc.sbuf_tensor("b_dbg", [128, 2064], F32))
               if B_DBG else None)
        S = Sems(nc, "b")
        qK = DmaQ(S, "dK", 4)
        qVS = DmaQ(S, "dVS", 2)
        qSel = DmaQ(S, "dSel", 2)
        qTc = DmaQ(S, "dTc", 2)
        qTr = DmaQ(S, "dTr", 2)
        qX = DmaQ(S, "dX", 2)
        qR = DmaQ(S, "dR", 2)
        qAo = DmaQ(S, "dAo", 2)
        qMisc = DmaQ(S, "dM", 2)

        NSC, NBI, NMEM = _B_NSC, _B_NBI, _B_NMEM
        IDX, CUMAV = _B_IDX, _B_CUMAV

        with nc.Block() as blk:
            @blk.sync
            def _(sync):
                qMisc.inc(sync.dma_start(relf[:, :], Dr["rel"][:, :]))
                qMisc.inc(sync.dma_start(bo_row[:, :], Dr["bo0"][:, :]))
                for s in range(NSEQ):
                    for h in range(HEADS):
                        hh = s * 16 + h
                        if hh >= 2:
                            sync.wait_ge(S("peS"), NSC * (hh - 1))
                        qK.inc(sync.dma_start(kh[hh % 2][:, :],
                                              Dr["kf0"][s, h * 64:h * 64 + 64, :]))
                        qK.inc(sync.dma_start(qh[hh % 2][:, :],
                                              Dr["qf0"][s, h * 64:h * 64 + 64, :]))
                    for it in range(16):
                        wt = s * 16 + it
                        if wt >= 2:
                            sync.wait_ge(S("gR"), 2 * (wt - 1))
                        t0 = (s * 16 + it) * 128
                        qX.inc(sync.dma_start(xr[wt % 2][:, :],
                                              Dr["x"][t0:t0 + 128, :]))
                        sync.wait_ge(S("gR"), 2 * wt + 2)
                        qR.inc(sync.dma_start(Dr["r1"][t0:t0 + 128, :],
                                              x1t[wt % 2][:, :]))
                if B_DBG:
                    sync.wait_ge(S("vDbg"), 1)
                    sync.dma_start(Dr["dbg"][:, :], dbg[:, :]).then_inc(S("dDbg"), 16)
                    sync.wait_ge(S("dDbg"), 16)
                qR.wait(sync, NSEQ * 16 - 1)
                qR.wait(sync, NSEQ * 16 - 2)

            @blk.gpsimd
            def _(g):
                for c in range(8):
                    if c >= 2:
                        g.wait_ge(S("peTc"), 8 * (c - 1))
                    qSel.inc(g.dma_start(
                        sel[c % 2][:, :],
                        Pc["selmat_dram"][:, c * 4096:(c + 1) * 4096]))
                for s in range(NSEQ):
                    g.wait_ge(S("peA"), NSC * 16 * s)
                    qVS.inc(g.dma_start(
                        v_sb[:, :].rearrange("p (jt f) -> p jt f", f=D),
                        Dr["vt0"][s].rearrange("(jt p) f -> p jt f", p=128)))
                    for h in range(1, HEADS, 2):
                        hh = s * 16 + h
                        oh = hh // 2
                        g.wait_ge(S("vAo"), 4 * (hh + 1))
                        qAo.inc(g.dma_start(
                            ao[64:128, (h // 2) * T:(h // 2 + 1) * T],
                            ao_tmp[oh % 2][:, :]))
                    for it in range(16):
                        wt = s * 16 + it
                        qX.wait(g, wt)
                        for half in range(2):
                            g.wait_ge(S("vW"), 2 * wt + half + 1)
                            g.tensor_tensor(
                                x1t[wt % 2][:, half * 512:(half + 1) * 512],
                                x1t[wt % 2][:, half * 512:(half + 1) * 512],
                                xr[wt % 2][:, half * 512:(half + 1) * 512],
                                ALU.add).then_inc(S("gR"), 1)

            @blk.tensor
            def _(t):
                qMisc.wait(t, 0)
                t.wait_ge(S("vRel"), 1)
                for c in range(64):
                    qSel.wait(t, c // 8)
                    if c >= 1:
                        t.wait_ge(S("sTc"), c)
                    t.matmul(ps_sc[0][0:16, 0:512], relb[:, :],
                             sel[(c // 8) % 2][:, (c % 8) * 512:(c % 8 + 1) * 512],
                             start=True, stop=True).then_inc(S("peTc"), 1)
                qMisc.wait(t, 1)
                t.wait_ge(S("sTc"), 64)
                for half in range(2):
                    t.matmul(ps_sc[1][:, 0:512], Pc["ones_f32"][0:1, :],
                             bo_row[0:1, half * 512:(half + 1) * 512],
                             start=True, stop=True).then_inc(S("peBo"), 1)
                    t.wait_ge(S("sBo"), half + 1)
                t.wait_ge(S("vOne"), 16)
                def emit_bc(hh2):
                    s2 = hh2 // 16
                    t.wait_ge(S("sL2"), hh2 + 1)
                    if s2 >= 1:
                        t.wait_ge(S("vW"), 32 * s2)
                    for c in range(4):
                        t.wait_ge(S("sBc"), 4 * hh2 + c)
                        t.matmul(wo_ps[0:64, 0:512],
                                 Pc["ones_f32"][0:1, 0:64],
                                 linv[0:1, c * 512:(c + 1) * 512],
                                 start=True, stop=True).then_inc(S("peBc"), 1)

                for s in range(NSEQ):
                    for h in range(HEADS):
                        hh = s * 16 + h
                        qK.wait(t, 2 * hh)
                        qK.wait(t, 2 * hh + 1)
                        for jt in range(16):
                            j0 = jt * 128
                            for (ac, lo, hi) in _b_chunks(jt):
                                idx = NSC * hh + IDX[(jt, ac)]
                                if idx >= 2:
                                    t.wait_ge(S("sE"), idx - 1)
                                t.matmul(ps_sc[idx % 2][:, 0:hi - lo],
                                         kh[hh % 2][:, j0:j0 + 128],
                                         qh[hh % 2][:, lo:hi],
                                         start=True, stop=True
                                         ).then_inc(S("peS"), 1)
                            if jt == 0 and hh >= 1:
                                t.wait_ge(S("vCp"), hh)
                            for (ac, lo2, hi2) in _b_chunks(jt):
                                eidx = NSC * hh + IDX[(jt, ac)]
                                t.wait_ge(S("sE"), eidx + 1)
                                if (jt * 128) % 512 and ac == jt // 4:
                                    t.wait_ge(S("vM"), NMEM * hh
                                              + sum(1 for j in range(jt + 1)
                                                    if (j * 128) % 512))
                                t.wait_ge(S("vC"), 16 * hh + jt + 1)
                                t.matmul(av_ps[:, ac * 512:(ac + 1) * 512],
                                         v65[:, jt * 65:jt * 65 + 65],
                                         A[jt % 2][:, ac * 512:(ac + 1) * 512],
                                         start=(jt == 0),
                                         stop=(jt == min(15, 4 * ac + 3))
                                         ).then_inc(S("peA"), 1)
                        if h >= 1:
                            emit_bc(hh - 1)
                    emit_bc(s * 16 + 15)
                    t.wait_ge(S("vAo"), 64 * (s + 1))
                    qAo.wait(t, 8 * (s + 1) - 1)
                    qAo.wait(t, 8 * (s + 1) - 2)
                    for it in range(16):
                        wt = s * 16 + it
                        if wt >= 1:
                            t.wait_ge(S("vW"), 2 * wt)
                        for half in range(2):
                            for ec in range(NE):
                                ins = t.matmul(
                                    wo_ps[:, half * 512:(half + 1) * 512],
                                    ao[:, ec * T + it * 128: ec * T + (it + 1) * 128],
                                    wo_t[:, ec * D + half * 512: ec * D + (half + 1) * 512],
                                    start=(ec == 0), stop=(ec == 7))
                            ins.then_inc(S("peW"), 1)

            @blk.scalar
            def _(s_):
                for c in range(64):
                    s_.wait_ge(S("peTc"), c + 1)
                    if c >= 2:
                        qTc.wait(s_, c - 2)
                    s_.copy(tcsb[c % 2][:, :], ps_sc[0][0:16, 0:512]
                            ).then_inc(S("sTc"), 1)
                    s_.wait_ge(S("sTc"), c + 1)
                    qTc.inc(s_.dma_start(Dr["tcorr_d"][:, c * 512:(c + 1) * 512],
                                         tcsb[c % 2][:, :]))
                for h in range(16):
                    qTc.wait(s_, 63)
                    qTc.wait(s_, 62)
                    qTr.inc(s_.dma_start(
                        tcorr[:, h * 256:(h + 1) * 256],
                        Dr["tcorr_d"][h].rearrange("(p i) -> p i", p=128)))
                for half in range(2):
                    s_.wait_ge(S("peBo"), half + 1)
                    s_.copy(bo_bc[:, half * 512:(half + 1) * 512],
                            ps_sc[1][:, 0:512]).then_inc(S("sBo"), 1)
                def post(hh2):
                    s_.wait_ge(S("vCp"), hh2 + 1)
                    s_.activation(lrow[0:1, :], avc[64:65, :],
                                  AF.Ln, scale=2.0 ** -44).then_inc(S("sL"), 1)
                    s_.wait_ge(S("sL"), 2 * hh2 + 1)
                    s_.activation(avc[64:65, :], lrow[0:1, :], AF.Relu,
                                  bias=Pc["c44"][0:1, :]).then_inc(S("sL"), 1)
                    s_.wait_ge(S("sL"), 2 * hh2 + 2)
                    s_.activation(linv[0:1, :], avc[64:65, :], AF.Exp,
                                  scale=-1.0,
                                  bias=Pc["cLE"][0:1, :]).then_inc(S("sL2"), 1)
                    for c in range(4):
                        s_.wait_ge(S("peBc"), 4 * hh2 + c + 1)
                        if 4 * hh2 + c >= 2:
                            s_.wait_ge(S("vAo"), 4 * hh2 + c - 1)
                        s_.copy(bcb[c % 2][:, :], wo_ps[0:64, 0:512]
                                ).then_inc(S("sBc"), 1)

                for s in range(NSEQ):
                    for h in range(HEADS):
                        hh = s * 16 + h
                        nbi = 0
                        for jt in range(16):
                            segs = _b_bias_segs(jt)
                            for (ac, lo, hi) in _b_chunks(jt):
                                idx = NSC * hh + IDX[(jt, ac)]
                                s_.wait_ge(S("peS"), idx + 1)
                                has_bias = any(a == ac for (a, _, _) in segs)
                                if has_bias:
                                    nbi += sum(1 for (a, _, _) in segs if a == ac)
                                    s_.wait_ge(S("vB"), NBI * hh + nbi)
                                if jt >= 2 and ac == jt // 4:
                                    s_.wait_ge(S("peA"),
                                               NSC * hh + CUMAV[jt - 1])
                                s_.activation(A[jt % 2][:, lo:hi],
                                              ps_sc[idx % 2][:, 0:hi - lo],
                                              AF.Exp).then_inc(S("sE"), 1)
                        if h >= 1:
                            post(hh - 1)
                    post(s * 16 + 15)

            @blk.vector
            def _(v):
                def vmults(hh2):
                    s2, h2 = hh2 // 16, hh2 % 16
                    oh = hh2 // 2
                    for c in range(4):
                        v.wait_ge(S("sBc"), 4 * hh2 + c + 1)
                        if hh2 >= 1 and c == 0:
                            v.wait_ge(S("peW"), 32 * s2)
                        if h2 % 2 and c == 0 and oh >= 2:
                            qAo.wait(v, oh - 2)
                        dst = (ao[0:64, (h2 // 2) * T + c * 512:
                                   (h2 // 2) * T + (c + 1) * 512]
                               if h2 % 2 == 0 else
                               ao_tmp[oh % 2][:, c * 512:(c + 1) * 512])
                        v.tensor_tensor(
                            dst, avc[0:64, c * 512:(c + 1) * 512],
                            bcb[c % 2][:, :], ALU.mult).then_inc(S("vAo"), 1)

                for jt in range(16):
                    v.memset(v65[:, jt * 65 + 64: jt * 65 + 65],
                             1.0).then_inc(S("vOne"), 1)
                qMisc.wait(v, 0)
                v.tensor_scalar(relb[:, :], relf[:, :], SCALE, None,
                                ALU.mult).then_inc(S("vRel"), 1)
                qTr.wait(v, 14)
                qTr.wait(v, 15)
                for h in range(16):
                    if B_NOBIAS:
                        v.tensor_copy(tcorr[:, h * 256:(h + 1) * 256],
                                      Pc["maskband"][:, :]).then_inc(S("vTc"), 1)
                    else:
                        v.tensor_tensor(tcorr[:, h * 256:(h + 1) * 256],
                                        tcorr[:, h * 256:(h + 1) * 256],
                                        Pc["maskband"][:, :], ALU.add
                                        ).then_inc(S("vTc"), 1)
                for s in range(NSEQ):
                    for h in range(HEADS):
                        hh = s * 16 + h
                        if h == 0:
                            qVS.wait(v, s)
                        if hh >= 1:
                            v.wait_ge(S("peA"), NSC * hh)
                        for jt in range(16):
                            v.tensor_copy(v65[:, jt * 65:jt * 65 + 64],
                                          v_sb[:, jt * D + h * 64: jt * D + h * 64 + 64]
                                          ).then_inc(S("vC"), 1)
                        if hh == 0:
                            v.wait_ge(S("vTc"), 16)
                        nbi = 0
                        nmem = 0
                        for jt in range(16):
                            j0 = jt * 128
                            if jt >= 2:
                                v.wait_ge(S("peA"), NSC * hh + CUMAV[jt - 1])
                            if j0 % 512:
                                nmem += 1
                                c0 = j0 // 512
                                v.memset(A[jt % 2][:, c0 * 512:j0], 0.0
                                         ).then_inc(S("vM"), 1)
                            for (ac, blo, bhi) in _b_bias_segs(jt):
                                idx = NSC * hh + IDX[(jt, ac)]
                                clo = max(j0, ac * 512)
                                v.wait_ge(S("peS"), idx + 1)
                                nbi += 1
                                v.tensor_tensor(
                                    ps_sc[idx % 2][:, blo - clo:bhi - clo],
                                    ps_sc[idx % 2][:, blo - clo:bhi - clo],
                                    tcorr[:, h * 256 + blo - j0: h * 256 + bhi - j0],
                                    ALU.add).then_inc(S("vB"), 1)
                        if h >= 1:
                            vmults(hh - 1)
                        v.wait_ge(S("peA"), NSC * (hh + 1))
                        v.wait_ge(S("vAo"), 4 * hh)
                        v.tensor_copy(avc[:, :], av_ps[0:65, :]
                                      ).then_inc(S("vCp"), 1)
                    vmults(s * 16 + 15)
                    for it in range(16):
                        wt = s * 16 + it
                        if wt >= 2:
                            qR.wait(v, wt - 2)
                        for half in range(2):
                            v.wait_ge(S("peW"), 2 * wt + half + 1)
                            v.scalar_tensor_tensor(
                                x1t[wt % 2][:, half * 512:(half + 1) * 512],
                                wo_ps[:, half * 512:(half + 1) * 512],
                                1.0, bo_bc[:, half * 512:(half + 1) * 512],
                                ALU.mult, ALU.add).then_inc(S("vW"), 1)
                    if s == 0 and B_DBG:
                        v.tensor_copy(dbg[:, 0:2048], ao[:, 0:2048]
                                      ).then_inc(S("vDbg"), 1)
        S.release()



def emit_phase_c(nc, es, Dr, Pc):
    with ExitStack() as wes:
        rin = [wes.enter_context(nc.sbuf_tensor(f"c_rin{i}", [128, D], F32))
               for i in range(2)]
        xsq = wes.enter_context(nc.sbuf_tensor("c_xsq", [128, D], F32))
        xi = wes.enter_context(nc.sbuf_tensor("c_xi", [128, D], F32))
        t1 = wes.enter_context(nc.sbuf_tensor("c_t1", [128, D], F32))
        t2 = wes.enter_context(nc.sbuf_tensor("c_t2", [128, D], F32))
        xnb = wes.enter_context(nc.sbuf_tensor("c_xnb", [128, D], BF16))
        cols = wes.enter_context(nc.sbuf_tensor("c_cols", [128, 16], F32))
        mx = wes.enter_context(nc.sbuf_tensor("c_mx", [128, NT], F32))
        xfm = wes.enter_context(nc.sbuf_tensor("c_xfm", [128, NE * 512], BF16))
        gb = wes.enter_context(nc.sbuf_tensor("c_gb", [128, D], F32))
        bb = wes.enter_context(nc.sbuf_tensor("c_bb", [128, D], F32))
        gsc = wes.enter_context(nc.sbuf_tensor("c_gsc", [1, 8], F32))
        lrow = wes.enter_context(nc.sbuf_tensor("c_lrow", [1, D], F32))
        wstg = [wes.enter_context(nc.sbuf_tensor(f"c_wstg{i}", [128, D], F32))
                for i in range(2)]
        wqt = [wes.enter_context(nc.sbuf_tensor(f"c_wqt{i}", [128, D], BF16))
               for i in range(2)]
        wcols = wes.enter_context(nc.sbuf_tensor("c_wcols", [128, 8], F32))
        ones128 = wes.enter_context(nc.sbuf_tensor("c_ones", [128, 1], F32))
        pst = [wes.enter_context(nc.psum_tensor(f"c_pst{i}", [128, 1024], BF16))
               for i in range(2)]
        psb = wes.enter_context(nc.psum_tensor("c_psb", [128, 512], F32))
        S = Sems(nc, "c")
        qC = DmaQ(S, "dC", 2)
        qW = DmaQ(S, "dW", 2)
        qN = DmaQ(S, "dN", 2)
        qQn = DmaQ(S, "dQn", 2)
        qRow = DmaQ(S, "dRow", 2)
        qG = DmaQ(S, "dG", 2)

        with nc.Block() as blk:
            @blk.sync
            def _(sync):
                qRow.inc(sync.dma_start(lrow[0:1, :], Dr["lng"][:, :]))
                for g in range(NG):
                    for k in range(4):
                        tt = g * 4 + k
                        if tt >= 2:
                            sync.wait_ge(S("lnvXi"), tt - 1)
                        t0 = tt * 128
                        qC.inc(sync.dma_start(rin[tt % 2][:, :],
                                              Dr["r1"][t0:t0 + 128, :]))
                    sync.wait_ge(S("sC"), 32 * (g + 1))
                    qN.inc(sync.dma_start(
                        Dr["xn1g"][g].rearrange("e p t -> p e t"),
                        xfm[:, :].rearrange("p (e t) -> p e t", t=512)))
                qN.wait(sync, NG - 1)
                qN.wait(sync, NG - 2)

            @blk.gpsimd
            def _(g_):
                g_.memset(ones128[:, :], 1.0)
                for tt in range(NT):
                    g_.wait_ge(S("vT2"), tt + 1)
                    if tt >= 1:
                        g_.wait_ge(S("pA"), 8 * tt)
                        g_.wait_ge(S("vD"), tt)
                    g_.tensor_tensor(xnb[:, :], t2[:, :], bb[:, :],
                                     ALU.add).then_inc(S("gXn"), 1)
                for n in range(16):
                    dc = n % 8
                    if 2 <= n < 10:
                        g_.wait_ge(S("vWc"), n - 1)
                    elif n >= 10:
                        g_.wait_ge(S("vWq"), n - 9)
                    qW.inc(g_.dma_start(wstg[n % 2][:, :],
                                        Dr["wffn"][dc * 128:(dc + 1) * 128, :]))
                g_.wait_ge(S("vBeta"), 1)
                qG.inc(g_.dma_start(Dr["beta_d"][:, :], gsc[0:1, 0:1]))
                g_.wait_ge(S("vG"), 1)
                qG.inc(g_.dma_start(Dr["cc_in"][:, :], gsc[0:1, 2:3]))
                qG.wait(g_, 1)
                g_.collective_compute(
                    "AllReduce", ALU.max,
                    replica_groups=[list(range(NCORES))],
                    ins=[Dr["cc_in"].ap().opt()],
                    outs=[Dr["cc_out"].ap().opt()]).then_inc(S("cc"), 1)
                g_.wait_ge(S("cc"), 1)
                qG.inc(g_.dma_start(Dr["gmax_d"][:, :], Dr["cc_out"][:, :]))
                qG.wait(g_, 2)
                qG.wait(g_, 0)

            @blk.scalar
            def _(s_):
                for half in range(2):
                    s_.wait_ge(S("peGB"), half + 1)
                    s_.copy(gb[:, half * 512:(half + 1) * 512],
                            psb[:, 0:512]).then_inc(S("sGBe"), 1)
                qRow.inc(s_.dma_start(lrow[0:1, :], Dr["lnb"][:, :]))
                for half in range(2):
                    s_.wait_ge(S("peGB"), half + 3)
                    s_.copy(bb[:, half * 512:(half + 1) * 512],
                            psb[:, 0:512]).then_inc(S("sGBe"), 1)
                for tt in range(NT):
                    qC.wait(s_, tt)
                    if tt >= 1:
                        s_.wait_ge(S("lnvA"), 4 * tt)
                    s_.activation(xsq[:, :], rin[tt % 2][:, :],
                                  AF.Square).then_inc(S("lnsA"), 1)
                    s_.wait_ge(S("lnvB"), 2 * tt + 1)
                    s_.activation(cols[:, 6:7], cols[:, 5:6], AF.Sqrt, scale=-1.0,
                                  bias=Pc["c1e5"]).then_inc(S("lnsB"), 1)
                    s_.wait_ge(S("lnvXi"), tt + 1)
                    s_.activation(xsq[:, :], xi[:, :],
                                  AF.Square).then_inc(S("lnsA"), 1)
                    s_.wait_ge(S("lnvB"), 2 * tt + 2)
                    s_.activation(cols[:, 14:15], cols[:, 13:14], AF.Sqrt, scale=-1.0,
                                  bias=Pc["c1e3"]).then_inc(S("lnsB"), 1)
                    for ei in range(NE):
                        kk = tt * 8 + ei
                        s_.wait_ge(S("pA"), kk + 1)
                        if tt % 4 == 0 and ei == 0 and tt >= 4:
                            qN.wait(s_, tt // 4 - 1)
                        k = tt % 4
                        s_.copy(xfm[:, ei * 512 + k * 128: ei * 512 + (k + 1) * 128],
                                pst[kk % 2][:, 0:128]).then_inc(S("sC"), 1)
                s_.wait_ge(S("peWbM"), 1)
                s_.copy(cols[:, 12:13], psb[:, 4:5]).then_inc(S("sWb"), 1)
                for dc in range(8):
                    s_.wait_ge(S("vWq"), dc + 1)
                    qQn.inc(s_.dma_start(Dr["wqn_d"][dc * 128:(dc + 1) * 128, :],
                                         wqt[dc % 2][:, :]))
                qQn.wait(s_, 7)
                qQn.wait(s_, 6)

            @blk.vector
            def _(v):
                for tt in range(NT):
                    qC.wait(v, tt)
                    v.reduce_sum(cols[:, 0:1], rin[tt % 2][:, :],
                                 axis=AX).then_inc(S("lnvA"), 1)
                    v.wait_ge(S("lnsA"), 2 * tt + 1)
                    v.reduce_sum(cols[:, 1:2], xsq[:, :],
                                 axis=AX).then_inc(S("lnvA"), 1)
                    v.wait_ge(S("lnvA"), 4 * tt + 2)
                    v.tensor_scalar(cols[:, 2:3], cols[:, 0:1], 1.0 / D, None, ALU.mult)
                    v.tensor_scalar(cols[:, 3:4], cols[:, 1:2], 1.0 / D, None,
                                    ALU.mult).then_inc(S("lnvM"), 1)
                    v.wait_ge(S("lnvM"), 2 * tt + 1)
                    v.scalar_tensor_tensor(cols[:, 5:6], cols[:, 2:3], cols[:, 2:3],
                                           cols[:, 3:4], ALU.mult,
                                           ALU.subtract).then_inc(S("lnvB"), 1)
                    v.wait_ge(S("lnsB"), 2 * tt + 1)
                    v.reciprocal(cols[:, 7:8], cols[:, 6:7]).then_inc(S("lnvC"), 1)
                    v.wait_ge(S("lnvC"), 2 * tt + 1)
                    v.tensor_scalar(xi[:, :], rin[tt % 2][:, :], cols[:, 2:3],
                                    cols[:, 7:8], ALU.subtract,
                                    ALU.mult).then_inc(S("lnvXi"), 1)
                    v.wait_ge(S("lnvXi"), tt + 1)
                    v.reduce_sum(cols[:, 8:9], xi[:, :], axis=AX).then_inc(S("lnvA"), 1)
                    v.wait_ge(S("lnsA"), 2 * tt + 2)
                    v.reduce_sum(cols[:, 9:10], xsq[:, :],
                                 axis=AX).then_inc(S("lnvA"), 1)
                    v.wait_ge(S("lnvA"), 4 * tt + 4)
                    v.tensor_scalar(cols[:, 10:11], cols[:, 8:9], 1.0 / D, None, ALU.mult)
                    v.tensor_scalar(cols[:, 11:12], cols[:, 9:10], 1.0 / D, None,
                                    ALU.mult).then_inc(S("lnvM"), 1)
                    v.wait_ge(S("lnvM"), 2 * tt + 2)
                    v.scalar_tensor_tensor(cols[:, 13:14], cols[:, 10:11], cols[:, 10:11],
                                           cols[:, 11:12], ALU.mult,
                                           ALU.subtract).then_inc(S("lnvB"), 1)
                    v.wait_ge(S("lnsB"), 2 * tt + 2)
                    v.reciprocal(cols[:, 15:16], cols[:, 14:15]).then_inc(S("lnvC"), 1)
                    v.wait_ge(S("lnvC"), 2 * tt + 2)
                    if tt == 0:
                        v.wait_ge(S("sGBe"), 4)
                    if tt >= 1:
                        v.wait_ge(S("gXn"), tt)
                    v.tensor_scalar(t1[:, :], xi[:, :], cols[:, 10:11],
                                    cols[:, 15:16], ALU.subtract,
                                    ALU.mult).then_inc(S("vT1"), 1)
                    v.wait_ge(S("vT1"), tt + 1)
                    v.tensor_tensor(t2[:, :], t1[:, :], gb[:, :],
                                    ALU.mult).then_inc(S("vT2"), 1)
                    v.wait_ge(S("gXn"), tt + 1)
                    v.reduce_max(mx[:, tt:tt + 1], xnb[:, :], axis=AX,
                                 apply_absolute_value=True).then_inc(S("vD"), 1)
                for dc in range(8):
                    qW.wait(v, dc)
                    v.reduce_sum(wcols[:, dc:dc + 1], wstg[dc % 2][:, :], axis=AX,
                                 apply_absolute_value=True).then_inc(S("vWc"), 1)
                v.wait_ge(S("vWc"), 8)
                v.reduce_sum(cols[:, 0:1], wcols[:, 0:8], axis=AX).then_inc(S("vWs"), 1)
                v.wait_ge(S("vWs"), 1)
                v.wait_ge(S("peB"), 1)
                v.tensor_scalar(gsc[0:1, 0:1], psb[0:1, 0:1], 1.0 / (D * D), None,
                                ALU.mult).then_inc(S("vBeta"), 1)
                v.wait_ge(S("vBeta"), 1)
                v.tensor_scalar(gsc[0:1, 1:2], gsc[0:1, 0:1], 1e-5, None, ALU.add
                                ).then_inc(S("vBeta2"), 1)
                v.wait_ge(S("vBeta2"), 1)
                v.reciprocal(gsc[0:1, 4:5], gsc[0:1, 1:2]).then_inc(S("vBinv"), 1)
                v.wait_ge(S("vBinv"), 1)
                for dc in range(8):
                    n = 8 + dc
                    qW.wait(v, n)
                    v.wait_ge(S("sWb"), 1)
                    if dc >= 2:
                        qQn.wait(v, dc - 2)
                    v.tensor_scalar(t1[:, :], wstg[n % 2][:, :], cols[:, 12:13],
                                    12582912.0, ALU.mult,
                                    ALU.add).then_inc(S("vWq1"), 1)
                    v.wait_ge(S("vWq1"), dc + 1)
                    v.tensor_scalar(t2[:, :], t1[:, :], 12582912.0, 1.0,
                                    ALU.subtract, ALU.min).then_inc(S("vWq2"), 1)
                    v.wait_ge(S("vWq2"), dc + 1)
                    v.tensor_scalar(wqt[dc % 2][:, :], t2[:, :], -1.0, None,
                                    ALU.max).then_inc(S("vWq"), 1)
                v.wait_ge(S("vD"), NT)
                v.reduce_max(cols[:, 0:1], mx[:, :], axis=AX).then_inc(S("vGc"), 1)
                v.wait_ge(S("vGc"), 1)
                v.wait_ge(S("peG"), 1)
                v.reduce_max(gsc[0:1, 2:3], psb[0:1, 0:128], axis=AX
                             ).then_inc(S("vGr"), 1)
                v.wait_ge(S("vGr"), 1)
                v.tensor_copy(gsc[0:1, 3:4], gsc[0:1, 2:3]).then_inc(S("vG"), 1)

            @blk.tensor
            def _(t):
                qRow.wait(t, 0)
                for half in range(2):
                    t.matmul(psb[:, 0:512], Pc["ones_f32"][0:1, :],
                             lrow[0:1, half * 512:(half + 1) * 512],
                             start=True, stop=True).then_inc(S("peGB"), 1)
                    t.wait_ge(S("sGBe"), half + 1)
                qRow.wait(t, 1)
                for half in range(2):
                    t.matmul(psb[:, 0:512], Pc["ones_f32"][0:1, :],
                             lrow[0:1, half * 512:(half + 1) * 512],
                             start=True, stop=True).then_inc(S("peGB"), 1)
                    t.wait_ge(S("sGBe"), half + 3)
                for tt in range(NT):
                    t.wait_ge(S("gXn"), tt + 1)
                    for ei in range(NE):
                        kk = tt * 8 + ei
                        if kk >= 2:
                            t.wait_ge(S("sC"), kk - 1)
                        t.transpose(pst[kk % 2][:, 0:128],
                                    xnb[:, ei * 128:(ei + 1) * 128],
                                    Pc["ident_bf"][:, :]).then_inc(S("pA"), 1)
                t.wait_ge(S("vWs"), 1)
                t.matmul(psb[0:1, 0:1], cols[:, 0:1], ones128[:, :],
                         start=True, stop=True).then_inc(S("peB"), 1)
                t.wait_ge(S("vBinv"), 1)
                t.matmul(psb[:, 4:5], Pc["ones_f32"][0:1, :], gsc[0:1, 4:5],
                         start=True, stop=True).then_inc(S("peWbM"), 1)
                t.wait_ge(S("vGc"), 1)
                t.wait_ge(S("sWb"), 1)
                t.transpose(psb[0:1, 0:128], cols[:, 0:1],
                            Pc["ident_f32"][:, :]).then_inc(S("peG"), 1)
        S.release()



def emit_phase_d(nc, es, Dr, Pc):
    with ExitStack() as wes:
        xq = [wes.enter_context(nc.sbuf_tensor(f"d_xq{i}", [128, NE * 512], BF16))
              for i in range(2)]
        wq_sb = wes.enter_context(nc.sbuf_tensor("d_wq", [128, NE * D], BF16))
        rr = [wes.enter_context(nc.sbuf_tensor(f"d_rr{i}", [128, D], F32))
              for i in range(2)]
        x2t = [wes.enter_context(nc.sbuf_tensor(f"d_x2{i}", [128, D], F32))
               for i in range(2)]
        xsq = wes.enter_context(nc.sbuf_tensor("d_xsq", [128, D], F32))
        xnb = wes.enter_context(nc.sbuf_tensor("d_xnb", [128, D], BF16))
        cols = wes.enter_context(nc.sbuf_tensor("d_cols", [128, 8], F32))
        gsc = wes.enter_context(nc.sbuf_tensor("d_gsc", [1, 8], F32))
        scol = wes.enter_context(nc.sbuf_tensor("d_scol", [128, 2], F32))
        xfm = wes.enter_context(nc.sbuf_tensor("d_xfm", [128, NE * 512], BF16))
        ps = [wes.enter_context(nc.psum_tensor(f"d_ps{i}", [128, 512], F32))
              for i in range(2)]
        pst = [wes.enter_context(nc.psum_tensor(f"d_pst{i}", [128, 1024], BF16))
               for i in range(2)]
        psb = wes.enter_context(nc.psum_tensor("d_psb", [128, 512], F32))
        S = Sems(nc, "d")
        qX = DmaQ(S, "dX", 2)
        qR = DmaQ(S, "dR", 2)
        qO = DmaQ(S, "dO", 2)
        qN = DmaQ(S, "dN", 2)
        qMs = DmaQ(S, "dMs", 2)
        qMw = DmaQ(S, "dMw", 2)

        with nc.Block() as blk:
            @blk.sync
            def _(sync):
                qMs.inc(sync.dma_start(gsc[0:1, 0:1], Dr["gmax_d"][:, :]))
                qMs.inc(sync.dma_start(gsc[0:1, 1:2], Dr["beta_d"][:, :]))
                for g in range(NG):
                    if g >= 2:
                        sync.wait_ge(S("peM"), 8 * (g - 1))
                    qX.inc(sync.dma_start(
                        xq[g % 2][:, :].rearrange("p (e t) -> p e t", t=512),
                        Dr["xn1g"][g].rearrange("e p t -> p e t")))
                    for k in range(4):
                        tt = g * 4 + k
                        if tt >= 2:
                            sync.wait_ge(S("vE"), 2 * (tt - 1))
                        t0 = tt * 128
                        qR.inc(sync.dma_start(rr[tt % 2][:, :],
                                              Dr["r1"][t0:t0 + 128, :]))
                        sync.wait_ge(S("vE"), 2 * tt + 2)
                        qO.inc(sync.dma_start(Dr["r2"][t0:t0 + 128, :],
                                              x2t[tt % 2][:, :]))
                    sync.wait_ge(S("sC"), 32 * (g + 1))
                    qN.inc(sync.dma_start(
                        Dr["xn2g"][g].rearrange("e p t -> p e t"),
                        xfm[:, :].rearrange("p (e t) -> p e t", t=512)))
                qN.wait(sync, NG - 1)
                qN.wait(sync, NG - 2)
                qO.wait(sync, NT - 1)
                qO.wait(sync, NT - 2)

            @blk.gpsimd
            def _(g_):
                for dc in range(8):
                    qMw.inc(g_.dma_start(wq_sb[:, dc * D:(dc + 1) * D],
                                         Dr["wqn_d"][dc * 128:(dc + 1) * 128, :]))

            @blk.vector
            def _(v):
                qMs.wait(v, 0)
                qMs.wait(v, 1)
                v.reciprocal(gsc[0:1, 2:3], gsc[0:1, 0:1]).then_inc(S("vS"), 1)
                v.wait_ge(S("vS"), 1)
                v.tensor_scalar(gsc[0:1, 3:4], gsc[0:1, 2:3], 128.0, None, ALU.mult)
                v.tensor_tensor(gsc[0:1, 4:5], gsc[0:1, 0:1], gsc[0:1, 1:2],
                                ALU.mult).then_inc(S("vS1b"), 1)
                v.wait_ge(S("vS1b"), 1)
                v.tensor_scalar(gsc[0:1, 5:6], gsc[0:1, 4:5], 1.0 / 128.0, None,
                                ALU.mult).then_inc(S("vS2"), 1)
                for g in range(NG):
                    qX.wait(v, g)
                    if g == 0:
                        v.wait_ge(S("sSc"), 1)
                    v.tensor_scalar(xq[g % 2][:, :], xq[g % 2][:, :], scol[:, 0:1],
                                    127.99999, ALU.mult,
                                    ALU.min).then_inc(S("vQa"), 1)
                    v.wait_ge(S("vQa"), g + 1)
                    v.tensor_scalar(xq[g % 2][:, :], xq[g % 2][:, :], -127.99999, None,
                                    ALU.max).then_inc(S("vQ"), 1)
                    for k in range(4):
                        tt = g * 4 + k
                        if tt >= 2:
                            qO.wait(v, tt - 2)
                            v.wait_ge(S("lnsA"), tt - 1)
                        for half in range(2):
                            n = 2 * tt + half
                            v.wait_ge(S("peM"), n + 1)
                            qR.wait(v, tt)
                            v.scalar_tensor_tensor(
                                x2t[tt % 2][:, half * 512:(half + 1) * 512],
                                ps[n % 2][:, :], scol[:, 1:2],
                                rr[tt % 2][:, half * 512:(half + 1) * 512],
                                ALU.mult, ALU.add).then_inc(S("vE"), 1)
                        v.wait_ge(S("vE"), 2 * tt + 2)
                        v.reduce_sum(cols[:, 0:1], x2t[tt % 2][:, :],
                                     axis=AX).then_inc(S("lnvA"), 1)
                        v.wait_ge(S("lnsA"), tt + 1)
                        v.reduce_sum(cols[:, 1:2], xsq[:, :],
                                     axis=AX).then_inc(S("lnvA"), 1)
                        v.wait_ge(S("lnvA"), 2 * tt + 2)
                        v.tensor_scalar(cols[:, 2:3], cols[:, 0:1], 1.0 / D, None, ALU.mult)
                        v.tensor_scalar(cols[:, 3:4], cols[:, 1:2], 1.0 / D, None,
                                        ALU.mult).then_inc(S("lnvM"), 1)
                        v.wait_ge(S("lnvM"), tt + 1)
                        v.scalar_tensor_tensor(cols[:, 5:6], cols[:, 2:3], cols[:, 2:3],
                                               cols[:, 3:4], ALU.mult,
                                               ALU.subtract).then_inc(S("lnvB"), 1)
                        v.wait_ge(S("lnsB"), tt + 1)
                        v.reciprocal(cols[:, 7:8], cols[:, 6:7]).then_inc(S("lnvC"), 1)
                        v.wait_ge(S("lnvC"), tt + 1)
                        if tt >= 1:
                            v.wait_ge(S("pA"), 8 * tt)
                        v.tensor_scalar(xnb[:, :], x2t[tt % 2][:, :], cols[:, 2:3],
                                        cols[:, 7:8], ALU.subtract,
                                        ALU.mult).then_inc(S("vD"), 1)

            @blk.scalar
            def _(s_):
                s_.wait_ge(S("peSc"), 2)
                s_.copy(scol[:, 0:2], psb[:, 0:2]).then_inc(S("sSc"), 1)
                for tt in range(NT):
                    v_n = 2 * tt
                    s_.wait_ge(S("vE"), 2 * tt + 2)
                    if tt >= 1:
                        s_.wait_ge(S("lnvA"), 2 * tt)
                    s_.activation(xsq[:, :], x2t[tt % 2][:, :],
                                  AF.Square).then_inc(S("lnsA"), 1)
                    s_.wait_ge(S("lnvB"), tt + 1)
                    s_.activation(cols[:, 6:7], cols[:, 5:6], AF.Sqrt, scale=-1.0,
                                  bias=Pc["c1e5"]).then_inc(S("lnsB"), 1)
                    for ei in range(NE):
                        kk = tt * 8 + ei
                        s_.wait_ge(S("pA"), kk + 1)
                        if tt % 4 == 0 and ei == 0 and tt >= 4:
                            qN.wait(s_, tt // 4 - 1)
                        k = tt % 4
                        s_.copy(xfm[:, ei * 512 + k * 128: ei * 512 + (k + 1) * 128],
                                pst[kk % 2][:, 0:128]).then_inc(S("sC"), 1)

            @blk.tensor
            def _(t):
                t.wait_ge(S("vS2"), 1)
                t.matmul(psb[:, 0:1], Pc["ones_f32"][0:1, :], gsc[0:1, 3:4],
                         start=True, stop=True)
                t.matmul(psb[:, 1:2], Pc["ones_f32"][0:1, :], gsc[0:1, 5:6],
                         start=True, stop=True).then_inc(S("peSc"), 2)
                qMw.wait(t, 6)
                qMw.wait(t, 7)
                for g in range(NG):
                    t.wait_ge(S("vQ"), g + 1)
                    for k in range(4):
                        tt = g * 4 + k
                        for half in range(2):
                            n = 2 * tt + half
                            if n >= 2:
                                t.wait_ge(S("vE"), n - 1)
                            for dc in range(NE):
                                ins = t.matmul(
                                    ps[n % 2][:, :],
                                    xq[g % 2][:, dc * 512 + k * 128: dc * 512 + (k + 1) * 128],
                                    wq_sb[:, dc * D + half * 512: dc * D + (half + 1) * 512],
                                    start=(dc == 0), stop=(dc == 7))
                            ins.then_inc(S("peM"), 1)
                        t.wait_ge(S("vD"), tt + 1)
                        for ei in range(NE):
                            kk = tt * 8 + ei
                            if kk >= 2:
                                t.wait_ge(S("sC"), kk - 1)
                            t.transpose(pst[kk % 2][:, 0:128],
                                        xnb[:, ei * 128:(ei + 1) * 128],
                                        Pc["ident_bf"][:, :]).then_inc(S("pA"), 1)
        S.release()



def emit_phase_e(nc, es, Dr, Pc):
    with ExitStack() as wes:
        wts = emit_load_wT(nc, wes, [Dr["wq1"], Dr["wk1"], Dr["wv1"]], "ew",
                           Pc["ident_bf"])
        bcols = emit_load_bias_cols(nc, wes, [Dr["bq1"], Dr["bk1"], Dr["bv1"]],
                                    "eb", Pc["ident_f32"])
        xg = [wes.enter_context(nc.sbuf_tensor(f"e_xg{i}", [128, NE * 512], BF16))
              for i in range(2)]
        qt = [wes.enter_context(nc.sbuf_tensor(f"e_qt{i}", [128, 512], BF16))
              for i in range(2)]
        vfm = wes.enter_context(nc.sbuf_tensor("e_vfm", [128, NE * 512], BF16))
        vtm = [wes.enter_context(nc.sbuf_tensor(f"e_vtm{i}", [128, D], BF16))
               for i in range(2)]
        pstv = [wes.enter_context(nc.psum_tensor(f"e_pstv{i}", [128, 1024], BF16))
                for i in range(2)]
        psq = [wes.enter_context(nc.psum_tensor(f"e_psq{i}", [128, 512], F32))
               for i in range(2)]
        S = Sems(nc, "e")
        qX = DmaQ(S, "dX", 2)
        qQ = DmaQ(S, "dQ", 2)
        qV = DmaQ(S, "dV", 2)

        with nc.Block() as blk:
            @blk.sync
            def _(sync):
                for g in range(NG):
                    sq = g // 4
                    tg0 = (g % 4) * 512
                    if g >= 2:
                        sync.wait_ge(S("pB"), 24 * (g - 1))
                    qX.inc(sync.dma_start(
                        xg[g % 2][:, :].rearrange("p (e t) -> p e t", t=512),
                        Dr["xn2g"][g].rearrange("e p t -> p e t")))
                    for wi in range(2):
                        for fi in range(NE):
                            sync.wait_ge(S("sEv"), g * 24 + wi * 8 + fi + 1)
                            m = g * 16 + wi * 8 + fi + 1
                            dst = Dr["qf1"] if wi == 0 else Dr["kf1"]
                            qQ.inc(sync.dma_start(
                                dst[sq, fi * 128:(fi + 1) * 128, tg0:tg0 + 512],
                                qt[m % 2][:, :]))
                    for k2 in range(4):
                        sync.wait_ge(S("sV2"), g * 32 + k2 * 8 + 8)
                        n = g * 4 + k2 + 1
                        qV.inc(sync.dma_start(
                            Dr["vt1"][sq, tg0 + k2 * 128: tg0 + (k2 + 1) * 128, :],
                            vtm[n % 2][:, :]))
                qQ.wait(sync, NG * 16 - 1)
                qQ.wait(sync, NG * 16 - 2)
                qV.wait(sync, NG * 4 - 1)
                qV.wait(sync, NG * 4 - 2)

            @blk.scalar
            def _(s_):
                for g in range(NG):
                    for wi in range(3):
                        for fi in range(NE):
                            n = g * 24 + wi * 8 + fi + 1
                            s_.wait_ge(S("pB"), n)
                            if wi < 2:
                                m = g * 16 + wi * 8 + fi + 1
                                if m > 2:
                                    qQ.wait(s_, m - 3)
                                s_.activation(qt[m % 2][:, :], psq[n % 2][:, :],
                                              AF.Identity,
                                              bias=bcols[wi][:, fi:fi + 1]
                                              ).then_inc(S("sEv"), 1)
                            else:
                                if g >= 1 and fi == 0:
                                    s_.wait_ge(S("pV"), 32 * g)
                                s_.activation(vfm[:, fi * 512:(fi + 1) * 512],
                                              psq[n % 2][:, :], AF.Identity,
                                              bias=bcols[2][:, fi:fi + 1]
                                              ).then_inc(S("sEv"), 1)
                    for k2 in range(4):
                        for fi in range(NE):
                            kk = g * 32 + k2 * 8 + fi
                            s_.wait_ge(S("pV"), kk + 1)
                            n = g * 4 + k2 + 1
                            if n > 2 and fi == 0:
                                qV.wait(s_, n - 3)
                            s_.copy(vtm[n % 2][:, fi * 128:(fi + 1) * 128],
                                    pstv[kk % 2][:, 0:128]).then_inc(S("sV2"), 1)

            @blk.tensor
            def _(t):
                for g in range(NG):
                    qX.wait(t, g)
                    for wi in range(3):
                        for fi in range(NE):
                            n = g * 24 + wi * 8 + fi + 1
                            for ei in range(NE):
                                if ei == 0 and n > 2:
                                    t.wait_ge(S("sEv"), n - 2)
                                ins = t.matmul(
                                    psq[n % 2][:, :],
                                    wts[wi][:, ei * D + fi * 128: ei * D + fi * 128 + 128],
                                    xg[g % 2][:, ei * 512:(ei + 1) * 512],
                                    start=(ei == 0), stop=(ei == 7))
                                if ei == 7:
                                    ins.then_inc(S("pB"), 1)
                    t.wait_ge(S("sEv"), g * 24 + 24)
                    for k2 in range(4):
                        for fi in range(NE):
                            kk = g * 32 + k2 * 8 + fi
                            if kk >= 2:
                                t.wait_ge(S("sV2"), kk - 1)
                            t.transpose(pstv[kk % 2][:, 0:128],
                                        vfm[:, fi * 512 + k2 * 128: fi * 512 + (k2 + 1) * 128],
                                        Pc["ident_bf"][:, :]).then_inc(S("pV"), 1)
        S.release()




def emit_phase_f(nc, es, Dr, Pc):
    NAV = 32
    with ExitStack() as wes:
        k2 = wes.enter_context(nc.sbuf_tensor("f_k2", [128, NE * T], BF16))
        q2 = wes.enter_context(nc.sbuf_tensor("f_q2", [128, NE * T], BF16))
        v2 = wes.enter_context(nc.sbuf_tensor("f_v2", [128, 16 * D], BF16))
        Aall = wes.enter_context(nc.sbuf_tensor("f_A", [128, 16 * T], BF16))
        lcol = wes.enter_context(nc.sbuf_tensor("f_lcol", [128, 2], F32))
        rr = [wes.enter_context(nc.sbuf_tensor(f"f_rr{i}", [128, D], F32))
              for i in range(2)]
        x3t = [wes.enter_context(nc.sbuf_tensor(f"f_x3{i}", [128, D], F32))
               for i in range(2)]
        xsq = wes.enter_context(nc.sbuf_tensor("f_xsq", [128, D], F32))
        xnb = wes.enter_context(nc.sbuf_tensor("f_xnb", [128, D], BF16))
        cols = wes.enter_context(nc.sbuf_tensor("f_cols", [128, 8], F32))
        xfm = wes.enter_context(nc.sbuf_tensor("f_xfm", [128, NE * 512], BF16))
        ps_sc = [wes.enter_context(nc.psum_tensor(f"f_psc{i}", [128, 512], F32))
                 for i in range(2)]
        ps_av = [wes.enter_context(nc.psum_tensor(f"f_av{i}", [128, D], F32))
                 for i in range(2)]
        pst = [wes.enter_context(nc.psum_tensor(f"f_pst{i}", [128, 1024], BF16))
               for i in range(2)]
        S = Sems(nc, "f")
        qL = DmaQ(S, "dL", 4)
        qR = DmaQ(S, "dR", 2)
        qO = DmaQ(S, "dO", 2)
        qN = DmaQ(S, "dN", 2)

        def chunks(jt):
            j0 = jt * 128
            out = []
            for ac in range(j0 // 512, 4):
                lo = max(j0, ac * 512)
                out.append((lo, (ac + 1) * 512))
            return out

        NCH = [len(chunks(jt)) for jt in range(16)]
        CUM = [0] * 17
        for jt in range(16):
            CUM[jt + 1] = CUM[jt] + NCH[jt]
        NSC = CUM[16]

        with nc.Block() as blk:
            @blk.sync
            def _(sync):
                for s in range(NSEQ):
                    if s >= 1:
                        sync.wait_ge(S("peA"), NAV * s)
                    qL.inc(sync.dma_start(
                        k2[:, :].rearrange("p (e t) -> p e t", t=T),
                        Dr["kf1"][s].rearrange("(e p) t -> p e t", p=128)))
                    qL.inc(sync.dma_start(
                        q2[:, :].rearrange("p (e t) -> p e t", t=T),
                        Dr["qf1"][s].rearrange("(e p) t -> p e t", p=128)))
                    qL.inc(sync.dma_start(
                        v2[:, :].rearrange("p (jt f) -> p jt f", f=D),
                        Dr["vt1"][s].rearrange("(jt p) f -> p jt f", p=128)))
                    for it in range(16):
                        wt = s * 16 + it
                        if wt >= 2:
                            sync.wait_ge(S("vE"), 2 * (wt - 1))
                        t0 = wt * 128
                        qR.inc(sync.dma_start(rr[wt % 2][:, :],
                                              Dr["r2"][t0:t0 + 128, :]))
                        sync.wait_ge(S("vE"), 2 * wt + 2)
                        qO.inc(sync.dma_start(Dr["r3"][t0:t0 + 128, :],
                                              x3t[wt % 2][:, :]))
                        if it % 4 == 3:
                            g = wt // 4
                            sync.wait_ge(S("sC"), 32 * (g + 1))
                            qN.inc(sync.dma_start(
                                Dr["xn3g"][g].rearrange("e p t -> p e t"),
                                xfm[:, :].rearrange("p (e t) -> p e t", t=512)))
                qO.wait(sync, NT - 1)
                qO.wait(sync, NT - 2)
                qN.wait(sync, NG - 1)
                qN.wait(sync, NG - 2)

            @blk.tensor
            def _(t):
                for s in range(NSEQ):
                    qL.wait(t, 3 * s)
                    qL.wait(t, 3 * s + 1)
                    for jt in range(16):
                        j0 = jt * 128
                        for ci, (lo, hi) in enumerate(chunks(jt)):
                            idx = NSC * s + CUM[jt] + ci
                            if idx >= 2:
                                t.wait_ge(S("sE"), idx - 1)
                            for ei in range(NE):
                                ins = t.matmul(
                                    ps_sc[idx % 2][:, 0:hi - lo],
                                    k2[:, ei * T + j0: ei * T + j0 + 128],
                                    q2[:, ei * T + lo: ei * T + hi],
                                    start=(ei == 0), stop=(ei == 7))
                            ins.then_inc(S("peS"), 1)
                    qL.wait(t, 3 * s + 2)
                    for it in range(16):
                        wt = s * 16 + it
                        t.wait_ge(S("vN"), 16 * s + it + 1)
                        if wt >= 2:
                            t.wait_ge(S("vE"), 2 * (wt - 1))
                        for half in range(2):
                            for jt in range(it + 1):
                                ins = t.matmul(
                                    ps_av[it % 2][:, half * 512:(half + 1) * 512],
                                    Aall[:, jt * T + it * 128: jt * T + (it + 1) * 128],
                                    v2[:, jt * D + half * 512: jt * D + (half + 1) * 512],
                                    start=(jt == 0), stop=(jt == it))
                            ins.then_inc(S("peA"), 1)
                        t.wait_ge(S("vD"), wt + 1)
                        for ei in range(NE):
                            kk = wt * 8 + ei
                            if kk >= 2:
                                t.wait_ge(S("sC"), kk - 1)
                            t.transpose(pst[kk % 2][:, 0:128],
                                        xnb[:, ei * 128:(ei + 1) * 128],
                                        Pc["ident_bf"][:, :]).then_inc(S("pA"), 1)

            @blk.scalar
            def _(s_):
                for s in range(NSEQ):
                    for jt in range(16):
                        j0 = jt * 128
                        for ci, (lo, hi) in enumerate(chunks(jt)):
                            idx = NSC * s + CUM[jt] + ci
                            s_.wait_ge(S("peS"), idx + 1)
                            if ci == 0:
                                s_.wait_ge(S("vMk"), 16 * s + jt + 1)
                            s_.activation(Aall[:, jt * T + lo: jt * T + hi],
                                          ps_sc[idx % 2][:, 0:hi - lo],
                                          AF.Exp, scale=1.0 / 32.0
                                          ).then_inc(S("sE"), 1)
                    for it in range(16):
                        wt = s * 16 + it
                        s_.wait_ge(S("vE"), 2 * wt + 2)
                        if wt >= 1:
                            s_.wait_ge(S("lnvA"), 2 * wt)
                        s_.activation(xsq[:, :], x3t[wt % 2][:, :],
                                      AF.Square).then_inc(S("lnsA"), 1)
                        s_.wait_ge(S("lnvB"), wt + 1)
                        s_.activation(cols[:, 6:7], cols[:, 5:6], AF.Sqrt, scale=-1.0,
                                      bias=Pc["c1e5"]).then_inc(S("lnsB"), 1)
                        for ei in range(NE):
                            kk = wt * 8 + ei
                            s_.wait_ge(S("pA"), kk + 1)
                            if wt % 4 == 0 and ei == 0 and wt >= 4:
                                qN.wait(s_, wt // 4 - 1)
                            k = wt % 4
                            s_.copy(xfm[:, ei * 512 + k * 128: ei * 512 + (k + 1) * 128],
                                    pst[kk % 2][:, 0:128]).then_inc(S("sC"), 1)

            @blk.vector
            def _(v):
                for s in range(NSEQ):
                    for jt in range(16):
                        j0 = jt * 128
                        idx0 = NSC * s + CUM[jt]
                        v.wait_ge(S("peS"), idx0 + 1)
                        v.tensor_tensor(ps_sc[idx0 % 2][:, 0:128],
                                        ps_sc[idx0 % 2][:, 0:128],
                                        Pc["maskband"][:, 0:128],
                                        ALU.add).then_inc(S("vMk"), 1)
                        v.wait_ge(S("sE"), NSC * s + CUM[jt + 1])
                        v.reduce_sum(lcol[:, 0:1], Aall[:, jt * T + j0: (jt + 1) * T],
                                     axis=AX).then_inc(S("vL"), 1)
                        v.wait_ge(S("vL"), 16 * s + jt + 1)
                        v.reciprocal(lcol[:, 1:2], lcol[:, 0:1]).then_inc(S("vLr"), 1)
                        v.wait_ge(S("vLr"), 16 * s + jt + 1)
                        v.tensor_scalar(Aall[:, jt * T + j0: (jt + 1) * T],
                                        Aall[:, jt * T + j0: (jt + 1) * T],
                                        lcol[:, 1:2], None,
                                        ALU.mult).then_inc(S("vN"), 1)
                    for it in range(16):
                        wt = s * 16 + it
                        if wt >= 2:
                            qO.wait(v, wt - 2)
                            v.wait_ge(S("lnsA"), wt - 1)
                        for half in range(2):
                            n = 2 * wt + half
                            v.wait_ge(S("peA"), 2 * wt + half + 1)
                            qR.wait(v, wt)
                            v.tensor_tensor(
                                x3t[wt % 2][:, half * 512:(half + 1) * 512],
                                ps_av[it % 2][:, half * 512:(half + 1) * 512],
                                rr[wt % 2][:, half * 512:(half + 1) * 512],
                                ALU.add).then_inc(S("vE"), 1)
                        v.wait_ge(S("vE"), 2 * wt + 2)
                        v.reduce_sum(cols[:, 0:1], x3t[wt % 2][:, :],
                                     axis=AX).then_inc(S("lnvA"), 1)
                        v.wait_ge(S("lnsA"), wt + 1)
                        v.reduce_sum(cols[:, 1:2], xsq[:, :],
                                     axis=AX).then_inc(S("lnvA"), 1)
                        v.wait_ge(S("lnvA"), 2 * wt + 2)
                        v.tensor_scalar(cols[:, 2:3], cols[:, 0:1], 1.0 / D, None, ALU.mult)
                        v.tensor_scalar(cols[:, 3:4], cols[:, 1:2], 1.0 / D, None,
                                        ALU.mult).then_inc(S("lnvM"), 1)
                        v.wait_ge(S("lnvM"), wt + 1)
                        v.scalar_tensor_tensor(cols[:, 5:6], cols[:, 2:3], cols[:, 2:3],
                                               cols[:, 3:4], ALU.mult,
                                               ALU.subtract).then_inc(S("lnvB"), 1)
                        v.wait_ge(S("lnsB"), wt + 1)
                        v.reciprocal(cols[:, 7:8], cols[:, 6:7]).then_inc(S("lnvC"), 1)
                        v.wait_ge(S("lnvC"), wt + 1)
                        if wt >= 1:
                            v.wait_ge(S("pA"), 8 * wt)
                        v.tensor_scalar(xnb[:, :], x3t[wt % 2][:, :], cols[:, 2:3],
                                        cols[:, 7:8], ALU.subtract,
                                        ALU.mult).then_inc(S("vD"), 1)

            @blk.gpsimd
            def _(g_):
                pass

        S.release()




def emit_phase_g(nc, es, Dr, Pc):
    with ExitStack() as wes:
        gate_sb = wes.enter_context(nc.sbuf_tensor("g_gate", [128, NE * 8], BF16))
        xng = [wes.enter_context(nc.sbuf_tensor(f"g_xng{i}", [128, NE * 512], BF16))
               for i in range(2)]
        gwork = wes.enter_context(nc.sbuf_tensor("g_gw", [128, 48], F32))
        gcols = wes.enter_context(nc.sbuf_tensor("g_gc", [128, 16], F32))
        comb_all = wes.enter_context(nc.sbuf_tensor("g_comb", [128, NT * 8], F32))
        w1sb = [wes.enter_context(nc.sbuf_tensor(f"g_w1{i}", [128, NE * 2048], BF16))
                for i in range(2)]
        w2sb = [wes.enter_context(nc.sbuf_tensor(f"g_w2{i}", [128, 16 * D], BF16))
                for i in range(2)]
        hsb = wes.enter_context(nc.sbuf_tensor("g_h", [128, 16 * 512], BF16))
        ot = [wes.enter_context(nc.sbuf_tensor(f"g_ot{i}", [128, 512], F32))
              for i in range(2)]
        pg = wes.enter_context(nc.psum_tensor("g_pg", [128, 512], F32))
        psh = [wes.enter_context(nc.psum_tensor(f"g_psh{i}", [128, 512], F32))
               for i in range(2)]
        pso = [wes.enter_context(nc.psum_tensor(f"g_pso{i}", [128, 512], F32))
               for i in range(2)]
        S = Sems(nc, "g")
        qXg = DmaQ(S, "dXg", 2)
        qXe = DmaQ(S, "dXe", 2)
        qW1 = DmaQ(S, "dW1", 2)
        qW2 = DmaQ(S, "dW2", 2)
        qI = DmaQ(S, "dI", 2)
        qY = DmaQ(S, "dY", 2)
        qG = DmaQ(S, "dG", 2)

        with nc.Block() as blk:
            @blk.sync
            def _(sync):
                for g in range(NG):
                    if g >= 2:
                        sync.wait_ge(S("peG"), 4 * (g - 1))
                    qXg.inc(sync.dma_start(
                        xng[g % 2][:, :].rearrange("p (e t) -> p e t", t=512),
                        Dr["xn3g"][g].rearrange("e p t -> p e t")))
                for e in range(8):
                    for g in range(NG):
                        eg = e * NG + g
                        if eg < 2:
                            sync.wait_ge(S("peG"), NT)
                        else:
                            sync.wait_ge(S("peH"), 16 * (eg - 1))
                        qXe.inc(sync.dma_start(
                            xng[eg % 2][:, :].rearrange("p (e t) -> p e t", t=512),
                            Dr["xn3g"][g].rearrange("e p t -> p e t")))

            @blk.gpsimd
            def _(g_):
                qG.inc(g_.dma_start(
                    gate_sb[:, :].rearrange("p (e u) -> p e u", u=8),
                    Dr["gate_w"].rearrange("(e p) u -> p e u", p=128)))
                for i in range(8):
                    qI.inc(g_.dma_start(Dr["y"][i * 1024:(i + 1) * 1024, :],
                                        Dr["r3"][i * 1024:(i + 1) * 1024, :]))
                for e in range(8):
                    if e >= 2:
                        g_.wait_ge(S("peH"), 256 * (e - 1))
                    for dc in range(NE):
                        qW1.inc(g_.dma_start(
                            w1sb[e % 2][:, dc * 2048:(dc + 1) * 2048],
                            Dr["w1"][e, dc * 128:(dc + 1) * 128, :]))
                    if e >= 2:
                        g_.wait_ge(S("peW2"), 128 * (e - 1))
                    for fc in range(16):
                        qW2.inc(g_.dma_start(
                            w2sb[e % 2][:, fc * D:(fc + 1) * D],
                            Dr["w2"][e, fc * 128:(fc + 1) * 128, :]))
                    if e == 0:
                        qI.wait(g_, 7)
                        qI.wait(g_, 6)
                    for g in range(NG):
                        eg = e * NG + g
                        for tt in range(4):
                            for half in range(2):
                                n = eg * 8 + tt * 2 + half
                                g_.wait_ge(S("vO"), n + 1)
                                t0 = g * 512 + tt * 128
                                qY.inc(g_.dma_start(
                                    Dr["y"][t0:t0 + 128, half * 512:(half + 1) * 512],
                                    ot[n % 2][:, :], accum_op=ALU.add))
                qY.wait(g_, 8 * 128 - 1)
                qY.wait(g_, 8 * 128 - 2)

            @blk.tensor
            def _(t):
                qG.wait(t, 0)
                for g in range(NG):
                    qXg.wait(t, g)
                    for k in range(4):
                        tile = g * 4 + k
                        if tile >= 1:
                            t.wait_ge(S("sGc"), tile)
                        for dc in range(NE):
                            ins = t.matmul(
                                pg[:, 0:8],
                                xng[g % 2][:, dc * 512 + k * 128: dc * 512 + (k + 1) * 128],
                                gate_sb[:, dc * 8:(dc + 1) * 8],
                                start=(dc == 0), stop=(dc == 7))
                        ins.then_inc(S("peG"), 1)
                for e in range(8):
                    qW1.wait(t, e * 8 + 6)
                    qW1.wait(t, e * 8 + 7)
                    qW2.wait(t, e * 16 + 14)
                    qW2.wait(t, e * 16 + 15)
                    for g in range(NG):
                        eg = e * NG + g
                        qXe.wait(t, eg)
                        for fc in range(16):
                            m = eg * 16 + fc
                            if m >= 2:
                                t.wait_ge(S("sH"), m - 1)
                            for dc in range(NE):
                                ins = t.matmul(
                                    psh[m % 2][:, :],
                                    w1sb[e % 2][:, dc * 2048 + fc * 128: dc * 2048 + fc * 128 + 128],
                                    xng[eg % 2][:, dc * 512:(dc + 1) * 512],
                                    start=(dc == 0), stop=(dc == 7))
                            ins.then_inc(S("peH"), 1)
                        for tt in range(4):
                            for half in range(2):
                                n = eg * 8 + tt * 2 + half
                                if n >= 2:
                                    t.wait_ge(S("vO"), n - 1)
                                for fc in range(16):
                                    if tt == 0 and half == 0:
                                        t.wait_ge(S("sH"), 16 * eg + fc + 1)
                                    ins = t.matmul(
                                        pso[n % 2][:, :],
                                        hsb[:, fc * 512 + tt * 128: fc * 512 + (tt + 1) * 128],
                                        w2sb[e % 2][:, fc * D + half * 512: fc * D + (half + 1) * 512],
                                        start=(fc == 0), stop=(fc == 15))
                                ins.then_inc(S("peW2"), 1)

            @blk.scalar
            def _(s_):
                for tile in range(NT):
                    s_.wait_ge(S("peG"), tile + 1)
                    if tile >= 1:
                        s_.wait_ge(S("vExp"), tile)
                        s_.wait_ge(S("sExp"), tile)
                    s_.copy(gwork[:, 0:8], pg[:, 0:8]).then_inc(S("sGc"), 1)
                    s_.wait_ge(S("vExp"), tile + 1)
                    s_.activation(gwork[:, 32:40], gwork[:, 0:8], AF.Exp,
                                  bias=gcols[:, 7:8]).then_inc(S("sExp"), 1)
                for e in range(8):
                    for g in range(NG):
                        eg = e * NG + g
                        for fc in range(16):
                            m = eg * 16 + fc
                            s_.wait_ge(S("peH"), m + 1)
                            if fc == 0 and eg >= 1:
                                s_.wait_ge(S("peW2"), 8 * eg)
                            s_.activation(hsb[:, fc * 512:(fc + 1) * 512],
                                          psh[m % 2][:, :],
                                          AF.Silu).then_inc(S("sH"), 1)

            @blk.vector
            def _(v):
                for tile in range(NT):
                    v.wait_ge(S("sGc"), tile + 1)
                    v.reduce_max(gcols[:, 0:1], gwork[:, 0:8],
                                 axis=AX).then_inc(S("v1"), 1)
                    v.wait_ge(S("v1"), tile + 1)
                    v.tensor_scalar(gwork[:, 8:16], gwork[:, 0:8], gcols[:, 0:1],
                                    None, ALU.is_ge).then_inc(S("v2"), 1)
                    v.wait_ge(S("v2"), tile + 1)
                    v.reduce_sum(gcols[:, 1:2], gwork[:, 8:16],
                                 axis=AX).then_inc(S("v3"), 1)
                    v.scalar_tensor_tensor(gwork[:, 16:24], gwork[:, 8:16], -1e30,
                                           gwork[:, 0:8], ALU.mult,
                                           ALU.add).then_inc(S("v4"), 1)
                    v.wait_ge(S("v4"), tile + 1)
                    v.reduce_max(gcols[:, 2:3], gwork[:, 16:24],
                                 axis=AX).then_inc(S("v5"), 1)
                    v.wait_ge(S("v3"), tile + 1)
                    v.tensor_scalar(gcols[:, 3:4], gcols[:, 1:2], -1.0, 1.0,
                                    ALU.add, ALU.min).then_inc(S("v6"), 1)
                    v.wait_ge(S("v5"), tile + 1)
                    v.wait_ge(S("v6"), tile + 1)
                    v.tensor_tensor(gcols[:, 4:5], gcols[:, 0:1], gcols[:, 2:3],
                                    ALU.subtract).then_inc(S("v7"), 1)
                    v.wait_ge(S("v7"), tile + 1)
                    v.tensor_tensor(gcols[:, 5:6], gcols[:, 4:5], gcols[:, 3:4],
                                    ALU.mult).then_inc(S("v8"), 1)
                    v.wait_ge(S("v8"), tile + 1)
                    v.tensor_tensor(gcols[:, 6:7], gcols[:, 2:3], gcols[:, 5:6],
                                    ALU.add).then_inc(S("v9"), 1)
                    v.wait_ge(S("v9"), tile + 1)
                    v.tensor_scalar(gwork[:, 24:32], gwork[:, 0:8], gcols[:, 6:7],
                                    None, ALU.is_ge)
                    v.tensor_scalar(gcols[:, 7:8], gcols[:, 0:1], -1.0, None,
                                    ALU.mult).then_inc(S("vExp"), 1)
                    v.wait_ge(S("sExp"), tile + 1)
                    v.tensor_tensor(gwork[:, 40:48], gwork[:, 32:40], gwork[:, 24:32],
                                    ALU.mult).then_inc(S("vA"), 1)
                    v.wait_ge(S("vA"), tile + 1)
                    v.reduce_sum(gcols[:, 8:9], gwork[:, 40:48],
                                 axis=AX).then_inc(S("vB"), 1)
                    v.wait_ge(S("vB"), tile + 1)
                    v.reciprocal(gcols[:, 9:10], gcols[:, 8:9]).then_inc(S("vC"), 1)
                    v.wait_ge(S("vC"), tile + 1)
                    v.tensor_scalar(comb_all[:, tile * 8:(tile + 1) * 8],
                                    gwork[:, 40:48], gcols[:, 9:10],
                                    None, ALU.mult).then_inc(S("vCb"), 1)
                v.wait_ge(S("vCb"), NT)
                for e in range(8):
                    for g in range(NG):
                        eg = e * NG + g
                        for tt in range(4):
                            tile = g * 4 + tt
                            for half in range(2):
                                n = eg * 8 + tt * 2 + half
                                v.wait_ge(S("peW2"), n + 1)
                                if n >= 2:
                                    qY.wait(v, n - 2)
                                v.tensor_scalar(ot[n % 2][:, :],
                                                pso[n % 2][:, :],
                                                comb_all[:, tile * 8 + e: tile * 8 + e + 1],
                                                None, ALU.mult).then_inc(S("vO"), 1)
        S.release()


PHASE_IO = {
    "a": (["x", "wq0", "wk0", "wv0", "bq0", "bk0", "bv0"],
          ["qf0", "kf0", "vt0"]),
    "b": (["x", "qf0", "kf0", "vt0", "wo0", "bo0", "rel"],
          ["r1", "tcorr_d", "dbg"]),
    "c": (["r1", "wffn", "lng", "lnb"],
          ["xn1g", "wqn_d", "beta_d", "gmax_d", "cc_in", "cc_out"]),
    "d": (["r1", "xn1g", "wqn_d", "beta_d", "gmax_d"],
          ["r2", "xn2g"]),
    "e": (["xn2g", "wq1", "wk1", "wv1", "bq1", "bk1", "bv1"],
          ["qf1", "kf1", "vt1"]),
    "f": (["qf1", "kf1", "vt1", "r2"],
          ["r3", "xn3g"]),
    "g": (["xn3g", "r3", "gate_w", "w1", "w2"],
          ["y"]),
}

DRAM_SPECS = {
    "x": ([TOK, D], F32),
    "wq0": ([D, D], F32), "wk0": ([D, D], F32), "wv0": ([D, D], F32),
    "wo0": ([D, D], F32),
    "bq0": ([NE, 128], F32), "bk0": ([NE, 128], F32), "bv0": ([NE, 128], F32),
    "bo0": ([1, D], F32),
    "rel": ([32, 16], F32),
    "qf0": ([NSEQ, D, T], BF16), "kf0": ([NSEQ, D, T], BF16),
    "vt0": ([NSEQ, T, D], BF16),
    "r1": ([TOK, D], F32),
    "r2": ([TOK, D], F32),
    "tcorr_d": ([16, 128 * 256], F32),
    "wffn": ([D, D], F32),
    "lng": ([1, D], F32), "lnb": ([1, D], F32),
    "xn1g": ([NG, NE, 128, 512], BF16),
    "xn2g": ([NG, NE, 128, 512], BF16),
    "wqn_d": ([D, D], BF16),
    "wq1": ([D, D], F32), "wk1": ([D, D], F32), "wv1": ([D, D], F32),
    "bq1": ([NE, 128], F32), "bk1": ([NE, 128], F32), "bv1": ([NE, 128], F32),
    "qf1": ([NSEQ, D, T], BF16), "kf1": ([NSEQ, D, T], BF16),
    "vt1": ([NSEQ, T, D], BF16),
    "r3": ([TOK, D], F32),
    "xn3g": ([NG, NE, 128, 512], BF16),
    "gate_w": ([D, 8], F32),
    "w1": ([8, D, 2048], F32),
    "w2": ([8, 2048, D], F32),
    "y": ([TOK, D], F32),
    "beta_d": ([1, 1], F32),
    "gmax_d": ([1, 1], F32),
    "cc_in": ([1, 1], F32),
    "cc_out": ([1, 1], F32),
    "dbg": ([128, 2064], F32),
}

CONST_SPECS = {
    "ident_bf": ([128, 128], BF16),
    "ident_f32": ([128, 128], F32),
    "ones_f32": ([1, 128], F32),
    "selmat": ([32, 128 * 256], BF16, "dram_only"),
    "maskband": ([128, 256], F32),
}

CONST_COLS = {"c1e5": 1e-5, "c1e3": 1e-3, "c44": 44.0, "cLE": 13.5015240554}

PHASE_EMIT = {"a": emit_phase_a2, "b": emit_phase_b, "c": emit_phase_c, "d": emit_phase_d, "e": emit_phase_e, "f": emit_phase_f, "g": emit_phase_g}


def build_program(phases, external):
    nc = bass.Bass("TRN2", target_bir_lowering=False)
    consumes = set()
    produces = set()
    for p in phases:
        c, pr = PHASE_IO[p]
        consumes.update(x for x in c if x not in produces)
        produces.update(pr)
    Dr = {}
    ext_inputs = []
    for name in sorted(consumes | produces):
        shape, dt = DRAM_SPECS[name]
        if name in produces and name in external:
            kind = "ExternalOutput"
        elif name in consumes and name not in produces:
            kind = "ExternalInput"
            ext_inputs.append(name)
        else:
            kind = "Internal"
        Dr[name] = nc.dram_tensor(name, shape, dt, kind=kind)
        if kind == "ExternalInput":
            nc._ext_inputs = getattr(nc, "_ext_inputs", []) + [name]
    Pc = {}
    with ExitStack() as es:
        for cn, spec in CONST_SPECS.items():
            shape, dt = spec[0], spec[1]
            dr = nc.dram_tensor("c_" + cn, shape, dt, kind="ExternalInput")
            Pc[cn + "_dram"] = dr
            if len(spec) > 2 and spec[2] == "dram_only":
                continue
            sb = es.enter_context(nc.sbuf_tensor("sb_" + cn, shape, dt))
            Pc[cn] = sb
        ccols = es.enter_context(nc.sbuf_tensor("sb_ccols", [128, len(CONST_COLS)], F32))
        for i, cn in enumerate(CONST_COLS):
            Pc[cn] = ccols[:, i:i + 1]
        S0 = Sems(nc, "init")
        with nc.Block() as blk:
            @blk.sync
            def _(sync):
                n = 0
                for cn, spec in CONST_SPECS.items():
                    if len(spec) > 2 and spec[2] == "dram_only":
                        continue
                    sync.dma_start(Pc[cn][:, :], Pc[cn + "_dram"][:, :]
                                   ).then_inc(S0("dma"), 16)
                    n += 16
                sync.wait_ge(S0("dma"), n)
            @blk.gpsimd
            def _(g):
                for i, (cn, val) in enumerate(CONST_COLS.items()):
                    g.memset(ccols[:, i:i + 1], val)

        for p in phases:
            PHASE_EMIT[p](nc, es, Dr, Pc)
    return nc


FUSED = True
LAUNCH_GROUPS = ([["a", "b", "c", "d", "e", "f", "g"]] if FUSED else
                 [["a"], ["b"], ["c"], ["d"], ["e"], ["f"], ["g"]])

_INTERNAL_ONLY = {"tcorr_d", "cc_in", "cc_out", "dbg"}


def _base_maps(inputs):
    consts = host_consts()
    x = np.ascontiguousarray(np.asarray(inputs["L0_attn_x"], np.float32))
    maps = []
    for c in range(NCORES):
        m = {
            "x": np.ascontiguousarray(x[4 * c:4 * c + 4].reshape(TOK, D)),
            "wq0": np.asarray(inputs["L0_attn_Wq"], np.float32),
            "wk0": np.asarray(inputs["L0_attn_Wk"], np.float32),
            "wv0": np.asarray(inputs["L0_attn_Wv"], np.float32),
            "wo0": np.asarray(inputs["L0_attn_Wo"], np.float32),
            "bq0": np.asarray(inputs["L0_attn_bq"], np.float32).reshape(NE, 128),
            "bk0": np.asarray(inputs["L0_attn_bk"], np.float32).reshape(NE, 128),
            "bv0": np.asarray(inputs["L0_attn_bv"], np.float32).reshape(NE, 128),
            "bo0": np.asarray(inputs["L0_attn_bo"], np.float32).reshape(1, D),
            "rel": np.asarray(inputs["L0_attn_rel_emb"], np.float32),
            "wffn": np.asarray(inputs["L0_ffn_weight"], np.float32),
            "lng": np.asarray(inputs["L0_ffn_ln_gamma"], np.float32).reshape(1, D),
            "lnb": np.asarray(inputs["L0_ffn_ln_beta"], np.float32).reshape(1, D),
            "wq1": np.asarray(inputs["L1_attn_Wq"], np.float32),
            "wk1": np.asarray(inputs["L1_attn_Wk"], np.float32),
            "wv1": np.asarray(inputs["L1_attn_Wv"], np.float32),
            "bq1": np.asarray(inputs["L1_attn_bq"], np.float32).reshape(NE, 128),
            "bk1": np.asarray(inputs["L1_attn_bk"], np.float32).reshape(NE, 128),
            "bv1": np.asarray(inputs["L1_attn_bv"], np.float32).reshape(NE, 128),
            "gate_w": np.asarray(inputs["L1_ffn_gate_w"], np.float32),
            "w1": np.asarray(inputs["L1_ffn_w1"], np.float32),
            "w2": np.asarray(inputs["L1_ffn_w2"], np.float32),
        }
        for cn in CONST_SPECS:
            m["c_" + cn] = consts[cn]
        maps.append(m)
    return maps


def kernel(**inputs):
    state = _base_maps(inputs)
    n_groups = len(LAUNCH_GROUPS)
    later_consumes = []
    acc = {"y"}
    for gi in range(n_groups - 1, -1, -1):
        later_consumes.insert(0, set(acc))
        for p in LAUNCH_GROUPS[gi]:
            acc |= set(PHASE_IO[p][0])
    for gi, group in enumerate(LAUNCH_GROUPS):
        produced = set()
        for p in group:
            produced |= set(PHASE_IO[p][1])
        ext = (produced & later_consumes[gi]) - _INTERNAL_ONLY
        if gi == n_groups - 1:
            ext |= {"y"}
        nc = build_program(group, external=ext)
        ext_in = set(getattr(nc, "_ext_inputs", [])) | {"c_" + c for c in CONST_SPECS}
        in_maps = [{k: v for k, v in state[c].items() if k in ext_in}
                   for c in range(NCORES)]
        res = run_bass_kernel_spmd(nc, in_maps, core_ids=list(range(NCORES)))
        for c in range(NCORES):
            for k, v in res.results[c].items():
                state[c][k] = np.asarray(v)
    out = np.concatenate(
        [state[c]["y"].reshape(NSEQ, T, D) for c in range(NCORES)], axis=0)
    return out
